# version 01
# baseline (speedup 1.0000x reference)
import contextlib
import numpy as np

import concourse.bass as bass
import concourse.mybir as mybir
from concourse.bass_utils import run_bass_kernel_spmd

F32 = mybir.dt.float32
BF16 = mybir.dt.bfloat16
AF = mybir.ActivationFunctionType
ALU = mybir.AluOpType
AX = mybir.AxisListType

S_LEN = 2048
E = 1024
NH = 8
TH = 256

INV_SQRT8 = 0.35355339059327373
SOFTMAX_SHIFT = -30.0
CV = [1.0 / 1024.0, 1e-5, -1e30, -1.0, INV_SQRT8, SOFTMAX_SHIFT]

ENG_NAMES = {"t": "tensor", "sc": "scalar", "v": "vector", "g": "gpsimd", "sy": "sync"}


class Rec:
    def __init__(self, nc, stack):
        self.nc = nc
        self.q = {e: [] for e in ENG_NAMES}
        self.sems = {}
        self.vals = {}
        for name in ("cg", "cs", "ct", "cc", "cv", "cgp",
                     "et", "ec", "ev2", "eg"):
            self.sems[name] = stack.enter_context(nc.semaphore(name))
            self.vals[name] = 0

    EVT = {"sc": "ec", "v": "ev2", "g": "eg"}

    def op(self, eng, fn, inc=None, n=1):
        if inc is None:
            if eng == "t":
                self.q[eng].append({"fn": fn, "inc": None})
            else:
                self.q[eng].append(fn)
            return None
        sem = self.sems[inc]
        self.vals[inc] += n
        v = self.vals[inc]
        if eng == "t":
            if fn is None:
                last = self.q[eng][-1]
                assert isinstance(last, dict) and last["inc"] is None, \
                    "no tensor instruction to attach inc to"
                last["inc"] = (sem, n)
            else:
                self.q[eng].append({"fn": fn, "inc": (sem, n)})
            return v
        if inc in ("cs", "cg"):
            def wrapped(h, fn=fn, sem=sem, n=n):
                fn(h).then_inc(sem, n)

            self.q[eng].append(wrapped)
        else:
            en = self.EVT[eng]
            esem = self.sems[en]
            self.vals[en] += 1
            ev = self.vals[en]

            def wrapped(h, fn=fn, sem=sem, n=n, esem=esem, ev=ev):
                fn(h).then_inc(esem, 1)
                h.wait_ge(esem, ev)
                h.nop().then_inc(sem, n)

            self.q[eng].append(wrapped)
        return v

    def wait(self, eng, cname, v):
        if not v or v <= 0:
            return
        sem = self.sems[cname]
        self.q[eng].append(lambda h, sem=sem, v=v: h.wait_ge(sem, v))

    def barrier(self):
        engs = ("t", "sc", "v", "g", "sy")
        for e in ("sc", "v", "g"):
            self.op(e, lambda h: h.nop(),
                    inc={"sc": "cc", "v": "cv", "g": "cgp"}[e])
        if self.q["t"] and isinstance(self.q["t"][-1], dict) \
                and self.q["t"][-1]["inc"] is None:
            self.op("t", None, inc="ct")
        snap = dict(self.vals)
        for e in engs:
            for cn in ("ct", "cc", "cv", "cgp", "cg", "cs"):
                if e == "t" and cn == "ct":
                    continue
                self.wait(e, cn, snap[cn])

    def replay(self, block):
        for eng, name in ENG_NAMES.items():
            ops = self.q[eng]

            def body(h, ops=ops):
                for f in ops:
                    if isinstance(f, dict):
                        instr = f["fn"](h)
                        if f["inc"] is not None:
                            instr.then_inc(*f["inc"])
                    else:
                        f(h)

            getattr(block, name)(body)


def host_constants():
    import ml_dtypes
    DH = 128
    d = {}
    inv_freq = 1.0 / 10000.0 ** (np.arange(0, DH, 2, dtype=np.float64) / DH)
    t = np.arange(S_LEN, dtype=np.float64)
    freqs = np.repeat(t[:, None] * inv_freq[None, :], 2, axis=1)
    COS = np.cos(freqs)
    SIN2 = np.sin(freqs)
    SIN2[:, 0::2] *= -1.0
    tau_to_sp = np.empty(S_LEN, np.int64)
    for c in range(8):
        for sl in range(TH):
            tau_to_sp[c * TH + sl] = 8 * sl + c
    d["cost"] = np.ascontiguousarray(COS[tau_to_sp, :].T).astype(ml_dtypes.bfloat16)
    d["sint"] = np.ascontiguousarray(SIN2[tau_to_sp, :].T).astype(ml_dtypes.bfloat16)
    d["ident"] = np.eye(128, dtype=ml_dtypes.bfloat16)
    d["identf"] = np.eye(128, dtype=np.float32)
    r = np.zeros((128, 128), np.float32)
    for i in range(128):
        r[i ^ 1, i] = 1.0
    d["rmat"] = r.astype(ml_dtypes.bfloat16)
    consts = np.zeros((128, 8), np.float32)
    for i, v in enumerate(CV):
        consts[:, i] = v
    d["consts"] = consts
    return d


def build(n_heads=NH, do_attn=True, do_ffn=True, taps=(), stage=9, p0_tr=True):
    nc = bass.Bass("TRN2", target_bir_lowering=False, detect_race_conditions=False)

    x_h = nc.dram_tensor("x", [S_LEN, E], F32, kind="ExternalInput").ap()
    w_h = {nm: nc.dram_tensor(nm, [E, E], F32, kind="ExternalInput").ap()
           for nm in ("w1", "w2", "w3", "wo")}
    rw_h = nc.dram_tensor("rw", [E, 16], F32, kind="ExternalInput").ap()
    wg_r_h = nc.dram_tensor("wg_r", [16, E, 256], F32, kind="ExternalInput").ap()
    wu_r_h = nc.dram_tensor("wu_r", [16, E, 256], F32, kind="ExternalInput").ap()
    wd_r_h = nc.dram_tensor("wd_r", [16, 256, E], F32, kind="ExternalInput").ap()
    wg_s_h = nc.dram_tensor("wg_s", [2, E, 512], F32, kind="ExternalInput").ap()
    wu_s_h = nc.dram_tensor("wu_s", [2, E, 512], F32, kind="ExternalInput").ap()
    wd_s_h = nc.dram_tensor("wd_s", [2, 512, E], F32, kind="ExternalInput").ap()
    cost_h = nc.dram_tensor("cost", [128, S_LEN], BF16, kind="ExternalInput").ap()
    sint_h = nc.dram_tensor("sint", [128, S_LEN], BF16, kind="ExternalInput").ap()
    ident_h = nc.dram_tensor("ident", [128, 128], BF16, kind="ExternalInput").ap()
    identf_h = nc.dram_tensor("identf", [128, 128], F32, kind="ExternalInput").ap()
    rmat_h = nc.dram_tensor("rmat", [128, 128], BF16, kind="ExternalInput").ap()
    consts_h = nc.dram_tensor("consts", [128, 8], F32, kind="ExternalInput").ap()
    out_h = nc.dram_tensor("out", [S_LEN, E], F32, kind="ExternalOutput").ap()
    combd_h = nc.dram_tensor("combd", [16, S_LEN], F32, kind="Internal").ap()

    tap_h = {}

    def mktap(name, shape, dtype):
        if name in taps:
            tap_h[name] = nc.dram_tensor("tap_" + name, shape, dtype,
                                         kind="ExternalOutput").ap()

    mktap("xnt", [128, 8 * S_LEN], BF16)
    mktap("qr0", [128, S_LEN], BF16)
    mktap("kr0", [128, S_LEN], BF16)
    mktap("vt0", [128, S_LEN], BF16)
    mktap("ot0", [128, S_LEN], BF16)
    mktap("acc1", [128, 16 * E], F32)
    mktap("comb", [128, 256], F32)
    mktap("accf", [128, 16 * E], F32)

    top = contextlib.ExitStack()
    with top:
        R = Rec(nc, top)
        sb = lambda name, shape, dtype: top.enter_context(nc.sbuf_tensor(name, shape, dtype))

        ACC = sb("ACC", [128, 16 * E], F32)
        XNT = sb("XNT", [128, 8 * S_LEN], BF16)
        CONSTS = sb("CONSTS", [128, 8], F32)
        IDENT = sb("IDENT", [128, 128], BF16)
        IDENTF = sb("IDENTF", [128, 128], F32)
        RMAT = sb("RMAT", [128, 128], BF16)
        ONES = sb("ONESB", [128, 1], BF16)
        for i, v in enumerate(CV):
            nc.const_aps.register(F32, v, CONSTS[:, i:i + 1])

        v_c = R.op("sy", lambda h: h.dma_start(CONSTS[:], consts_h[:]), inc="cs", n=16)
        v_c = R.op("sy", lambda h: h.dma_start(IDENT[:], ident_h[:]), inc="cs", n=16)
        v_c = R.op("sy", lambda h: h.dma_start(IDENTF[:], identf_h[:]), inc="cs", n=16)
        v_c = R.op("sy", lambda h: h.dma_start(RMAT[:], rmat_h[:]), inc="cs", n=16)
        v_ones = R.op("g", lambda h: h.memset(ONES[:], 1.0), inc="cgp")
        for e in ("t", "sc", "v", "g", "sy"):
            R.wait(e, "cs", v_c)
            R.wait(e, "cgp", v_ones)

        def phase_p0():
          with contextlib.ExitStack() as p0:
            psb0 = lambda n_, s_, d_: p0.enter_context(nc.psum_tensor(n_, s_, d_))
            sb0 = lambda n_, s_, d_: p0.enter_context(nc.sbuf_tensor(n_, s_, d_))
            XIN = [sb0(f"XIN{i}", [128, E], F32) for i in range(2)]
            XNTM = sb0("XNTM", [128, 16 * E], BF16)
            JUNK = sb0("JUNK", [128, E], BF16)
            MB = sb0("MB", [128, 8], F32)
            psT = psb0("psT0", [128, 1024], F32)

            v_ln = [0] * 16
            for tc in range(16):
                par = tc % 2
                if tc >= 2:
                    R.wait("sy", "cv", v_ln[tc - 2])
                v_x = R.op("sy", lambda h, tc=tc, par=par: h.dma_start(
                    XIN[par][:], x_h[tc * 128:(tc + 1) * 128, :]), inc="cs", n=16)
                R.wait("v", "cs", v_x)
                if tc >= 1:
                    R.wait("v", "cv", v_ln[tc - 1])
                R.op("v", lambda h, par=par: h.reduce_sum(MB[:, 6:7], XIN[par][:], AX.X),
                     inc="cv")
                vv = R.op("v", lambda h: h.tensor_scalar(
                    MB[:, 0:1], MB[:, 6:7], CV[0], None, ALU.mult), inc="cv")
                R.wait("sc", "cs", v_x)
                if tc >= 1:
                    R.wait("sc", "cv", v_ln[tc - 1])
                vs = R.op("sc", lambda h, par=par: h.activation(
                    JUNK[:], XIN[par][:], AF.Square, accum_out=MB[:, 1:2]), inc="cc")
                R.wait("v", "cc", vs)
                R.op("v", lambda h: h.tensor_scalar(
                    MB[:, 2:3], MB[:, 1:2], CV[0], None, ALU.mult), inc="cv")
                R.wait("v", "cv", vv)
                vv = R.op("v", lambda h: h.tensor_tensor(
                    MB[:, 6:7], MB[:, 0:1], MB[:, 0:1], ALU.mult), inc="cv")
                R.wait("v", "cv", vv)
                vv = R.op("v", lambda h: h.tensor_tensor(
                    MB[:, 2:3], MB[:, 2:3], MB[:, 6:7], ALU.subtract), inc="cv")
                R.wait("sc", "cv", vv)
                vs = R.op("sc", lambda h: h.activation(
                    MB[:, 3:4], MB[:, 2:3], AF.Sqrt, bias=CV[1]), inc="cc")
                R.wait("v", "cc", vs)
                vv = R.op("v", lambda h: h.reciprocal(MB[:, 4:5], MB[:, 3:4]), inc="cv")
                R.wait("v", "cv", vv)
                v_ln[tc] = R.op("v", lambda h, tc=tc, par=par: h.tensor_scalar(
                    XNTM[:, tc * E:(tc + 1) * E], XIN[par][:], MB[:, 0:1], MB[:, 4:5],
                    ALU.subtract, ALU.mult), inc="cv")

            R.wait("t", "cv", v_ln[15])
            cc_ev = 0
            for ec in range(int(p0_tr) if p0_tr is not True else 8):
                for hb in range(2):
                    R.wait("t", "cc", cc_ev)
                    for j in range(8):
                        tc = hb * 8 + j
                        R.op("t", lambda h, ec=ec, tc=tc, j=j: h.matmul(
                            psT[:, j * 128:(j + 1) * 128],
                            XNTM[:, tc * E + ec * 128: tc * E + ec * 128 + 128],
                            IDENT[:], start=True, stop=True))
                    vt = R.op("t", None, inc="ct")
                    R.wait("sc", "ct", vt)
                    cc_ev = R.op("sc", lambda h, ec=ec, hb=hb: h.activation(
                        XNT[:, ec * S_LEN + hb * 1024: ec * S_LEN + hb * 1024 + 1024],
                        psT[:, 0:1024], AF.Copy), inc="cc")
            if "xnt" in taps:
                R.wait("sy", "cc", cc_ev)
                R.op("sy", lambda h: h.dma_start(tap_h["xnt"], XNT[:]), inc="cs", n=16)
        if stage >= 1:
            phase_p0()
            if p0_tr is not True and int(p0_tr) < 8:
                vz = R.op("v", lambda h: h.memset(XNT[:], 0.0), inc="cv")
                R.wait("t", "cv", vz)
        else:
            vz = R.op("v", lambda h: h.memset(XNT[:], 0.0), inc="cv")
            R.wait("t", "cv", vz)
        R.barrier()

        pA = contextlib.ExitStack()
        sbA = lambda name, shape, dtype: pA.enter_context(nc.sbuf_tensor(name, shape, dtype))
        WOTP = sbA("WOTP", [128, 8 * E], BF16)
        W123T = {nm: sbA(nm + "T", [128, 8 * E], BF16) for nm in ("w1", "w2", "w3")}
        def phase_w():
          with contextlib.ExitStack() as pw:
            psbw = lambda n_, s_, d_: pw.enter_context(nc.psum_tensor(n_, s_, d_))
            sbw = lambda n_, s_, d_: pw.enter_context(nc.sbuf_tensor(n_, s_, d_))
            WNAT = [sbw(f"WNAT{i}", [128, 8 * E], BF16) for i in range(2)]
            psTW = psbw("psTW", [128, 1024], F32)
            names = ["w1", "w2", "w3", "wo"]
            v_tdone = [0] * 4
            cc_ev = 0
            for i, nm in enumerate(names):
                par = i % 2
                if i >= 2:
                    R.wait("g", "ct", v_tdone[i - 2])
                v_wl = R.op("g", lambda h, nm=nm, par=par: h.dma_start(
                    WNAT[par][:].rearrange("p (pc e) -> p pc e", pc=8),
                    w_h[nm].rearrange("(pc p) e -> p pc e", p=128)),
                    inc="cg", n=16)
                R.wait("t", "cg", v_wl)
                for blk in range(8):
                    R.wait("t", "cc", cc_ev)
                    for j in range(8):
                        if nm != "wo":
                            R.op("t", lambda h, blk=blk, j=j, par=par: h.matmul(
                                psTW[:, j * 128:(j + 1) * 128],
                                WNAT[par][:, j * E + blk * 128: j * E + blk * 128 + 128],
                                IDENT[:], start=True, stop=True))
                        else:
                            R.op("t", lambda h, blk=blk, j=j, par=par: h.matmul(
                                psTW[:, j * 128:(j + 1) * 128],
                                WNAT[par][:, j * E + blk: j * E + blk + 1017:8],
                                IDENT[:], start=True, stop=True))
                    vt = R.op("t", None, inc="ct")
                    R.wait("sc", "ct", vt)
                    if nm != "wo":
                        cc_ev = R.op("sc", lambda h, nm=nm, blk=blk: h.activation(
                            W123T[nm][:, blk * E:(blk + 1) * E], psTW[:], AF.Copy),
                            inc="cc")
                    else:
                        cc_ev = R.op("sc", lambda h, blk=blk: h.activation(
                            WOTP[:, blk * E:(blk + 1) * E], psTW[:], AF.Copy), inc="cc")
                v_tdone[i] = R.vals["ct"]
            R.wait("t", "cc", cc_ev)
        if stage >= 2:
            phase_w()
        R.barrier()

        if do_attn and stage >= 3:
            COST = sbA("COST", [128, S_LEN], BF16)
            SINT = sbA("SINT", [128, S_LEN], BF16)
            R.op("sy", lambda h: h.dma_start(COST[:], cost_h[:]), inc="cs", n=16)
            v_tab = R.op("sy", lambda h: h.dma_start(SINT[:], sint_h[:]), inc="cs", n=16)

            QHB = sbA("QHB", [128, S_LEN], BF16)
            KHB = sbA("KHB", [128, S_LEN], BF16)
            QR = sbA("QR", [128, S_LEN], BF16)
            KR = sbA("KR", [128, S_LEN], BF16)
            VT = sbA("VT", [128, S_LEN], BF16)
            T1 = sbA("T1", [128, 512], BF16)
            T2 = sbA("T2", [128, 512], BF16)
            PT = [sbA(f"PT{i}", [128, 1024], BF16) for i in range(2)]
            OSB = sbA("OSB", [128, S_LEN], BF16)
            OT = QHB
            ASB = OSB
            RSROW = sbA("RSROW", [1, S_LEN], F32)
            RECT = sbA("RECT", [128, 16], F32)

            v_rope_prev = 0
            v_accev_prev = 0
            v_exp_carry = [0, 0]
            v_pv_carry = [0, 0]
            v_oe = 0
            for head in range(n_heads):
                with contextlib.ExitStack() as p1a:
                    psba = lambda n_, s_, d_: p1a.enter_context(nc.psum_tensor(n_ + f"_{head}", s_, d_))
                    psQK = psba("psQK", [128, S_LEN], F32)
                    psV = psba("psV", [128, 1024], F32)
                    psSH = psba("psSH", [128, 1024], F32)

                    def proj_qk(wt, dst, head=head):
                        for pc in range(8):
                            for ec in range(8):
                                R.op("t", lambda h, pc=pc, ec=ec, wt=wt, head=head: h.matmul(
                                    psQK[:, pc * TH:(pc + 1) * TH],
                                    W123T[wt][:, ec * E + pc * 128: ec * E + pc * 128 + 128],
                                    XNT[:, ec * S_LEN + head * TH: ec * S_LEN + head * TH + TH],
                                    start=(ec == 0), stop=(ec == 7)))
                        vt = R.op("t", None, inc="ct")
                        R.wait("sc", "ct", vt)
                        return R.op("sc", lambda h, dst=dst: h.activation(
                            dst[:], psQK[:], AF.Copy), inc="cc")

                    R.wait("sc", "cv", v_rope_prev)
                    v_q = proj_qk("w1", QHB)
                    R.wait("t", "cc", v_q)
                    v_k = proj_qk("w2", KHB)

                    v_vev = 0
                    for blk in range(2):
                        R.wait("t", "cc", v_vev)
                        for ec in range(8):
                            for hf in range(2):
                                R.op("t", lambda h, ec=ec, hf=hf, blk=blk, head=head: h.matmul(
                                    psV[:, hf * 512:(hf + 1) * 512],
                                    XNT[:, ec * S_LEN + head * TH + blk * 128:
                                        ec * S_LEN + head * TH + blk * 128 + 128],
                                    W123T["w3"][:, ec * E + hf * 512: ec * E + hf * 512 + 512],
                                    start=(ec == 0), stop=(ec == 7)))
                        vt = R.op("t", None, inc="ct")
                        R.wait("sc", "ct", vt)
                        v_vev = R.op("sc", lambda h, blk=blk: h.activation(
                            bass.AP(VT, blk * 128, [[S_LEN, 128], [TH, 8], [1, 128]]),
                            bass.AP(psV, 0, [[1024, 128], [128, 8], [1, 128]]),
                            AF.Copy), inc="cc")

                    def rope(src, dst, v_src):
                        vq = 0
                        for j in range(4):
                            par = j % 2
                            R.wait("t", "cc", v_src)
                            R.wait("t", "cv", vq)
                            vt = R.op("t", lambda h, j=j, par=par, src=src: h.matmul(
                                psSH[:, par * 512:(par + 1) * 512], RMAT[:],
                                src[:, j * 512:(j + 1) * 512], start=True, stop=True),
                                inc="ct")
                            R.wait("v", "cs", v_tab)
                            R.wait("v", "cc", v_src)
                            R.op("v", lambda h, j=j, src=src: h.tensor_tensor(
                                T1[:], src[:, j * 512:(j + 1) * 512],
                                COST[:, j * 512:(j + 1) * 512], ALU.mult), inc="cv")
                            R.wait("v", "ct", vt)
                            vv = R.op("v", lambda h, j=j, par=par: h.tensor_tensor(
                                T2[:], psSH[:, par * 512:(par + 1) * 512],
                                SINT[:, j * 512:(j + 1) * 512], ALU.mult), inc="cv")
                            R.wait("v", "cv", vv)
                            vq = R.op("v", lambda h, j=j, dst=dst: h.tensor_tensor(
                                dst[:, j * 512:(j + 1) * 512], T1[:], T2[:], ALU.add),
                                inc="cv")
                        return vq

                    R.wait("v", "ct", v_smm_prev if head > 0 else 0)
                    v_qr = rope(QHB, QR, v_q)
                    v_kr = rope(KHB, KR, v_k)
                    v_rope_prev = v_kr

                if head == 0 and "qr0" in taps:
                    R.wait("sy", "cv", v_qr)
                    R.op("sy", lambda h: h.dma_start(tap_h["qr0"], QR[:]), inc="cs", n=16)
                    R.wait("sy", "cv", v_kr)
                    R.op("sy", lambda h: h.dma_start(tap_h["kr0"], KR[:]), inc="cs", n=16)
                if head == 0 and "vt0" in taps:
                    R.wait("sy", "cc", R.vals["cc"])
                    R.op("sy", lambda h: h.dma_start(tap_h["vt0"], VT[:]), inc="cs", n=16)

                with contextlib.ExitStack() as p1b:
                    psbb = lambda n_, s_, d_: p1b.enter_context(nc.psum_tensor(n_ + f"_{head}", s_, d_))
                    psS = [psbb(f"psS{i}", [128, 1024], F32) for i in range(2)]
                    psO = psbb("psO", [128, 1024], F32)
                    psR = psbb("psR", [1, 1024], F32)
                    R.wait("t", "cv", v_qr)
                    R.wait("t", "cv", v_kr)
                    v_exp = [0] * 32
                    v_pv = [0] * 32
                    for kk in range(32):
                        qh, kt = kk // 16, kk % 16
                        par = kk % 2
                        R.wait("t", "cc", v_exp[kk - 2] if kk >= 2 else v_exp_carry[kk])
                        for hf in range(2):
                            R.op("t", lambda h, kt=kt, par=par, qh=qh, hf=hf: h.matmul(
                                psS[par][:, hf * 512:(hf + 1) * 512],
                                KR[:, kt * 128:(kt + 1) * 128],
                                QR[:, qh * 1024 + hf * 512: qh * 1024 + hf * 512 + 512],
                                start=True, stop=True))
                        vt_s = R.op("t", None, inc="ct")
                        if kk == 0:
                            v_smm_prev = R.vals["ct"]
                        R.wait("sc", "ct", vt_s)
                        R.wait("sc", "ct", v_pv[kk - 2] if kk >= 2 else v_pv_carry[kk])
                        v_exp[kk] = R.op("sc", lambda h, par=par: h.activation(
                            PT[par][:], psS[par][:], AF.Exp,
                            bias=SOFTMAX_SHIFT, scale=INV_SQRT8), inc="cc")
                        R.wait("t", "cc", v_exp[kk])
                        if kt == 0:
                            R.wait("t", "cc", v_oe)
                        for hf in range(2):
                            R.op("t", lambda h, kt=kt, par=par, hf=hf: h.matmul(
                                psO[:, hf * 512:(hf + 1) * 512],
                                VT[:, kt * 128:(kt + 1) * 128],
                                PT[par][:, hf * 512:(hf + 1) * 512],
                                start=(kt == 0), stop=(kt == 15),
                                skip_group_check=True))
                        for hf in range(2):
                            R.op("t", lambda h, kt=kt, par=par, hf=hf: h.matmul(
                                psR[:, hf * 512:(hf + 1) * 512],
                                ONES[:],
                                PT[par][:, hf * 512:(hf + 1) * 512],
                                start=(kt == 0), stop=(kt == 15),
                                skip_group_check=True))
                        v_pv[kk] = R.op("t", None, inc="ct")
                        if kt == 15:
                            R.wait("sc", "ct", v_pv[kk])
                            R.op("sc", lambda h, qh=qh: h.activation(
                                RSROW[0:1, qh * 1024:(qh + 1) * 1024], psR[0:1, :],
                                AF.Copy), inc="cc")
                            v_oe = R.op("sc", lambda h, qh=qh: h.activation(
                                OSB[:, qh * 1024:(qh + 1) * 1024], psO[:],
                                AF.Copy), inc="cc")
                    v_exp_carry = [v_exp[30], v_exp[31]]
                    v_pv_carry = [v_pv[30], v_pv[31]]

                with contextlib.ExitStack() as p1c:
                    psbc = lambda n_, s_, d_: p1c.enter_context(nc.psum_tensor(n_ + f"_{head}", s_, d_))
                    psAm = psbc("psAm", [128, 1024], F32)
                    psTT = psbc("psTT", [128, 1024], F32)
                    R.wait("t", "cc", v_oe)
                    cc_ev = v_accev_prev
                    for j in range(16):
                        R.op("t", lambda h, j=j: h.matmul(
                            psAm[:, j:j + 1],
                            RSROW[0:1, j * 128:(j + 1) * 128], IDENTF[0:1, 0:1], start=True, stop=True))
                    vt_rs = R.op("t", None, inc="ct")
                    R.wait("v", "ct", vt_rs)
                    v_rec = R.op("v", lambda h: h.reciprocal(
                        RECT[:], psAm[:, 0:16]), inc="cv")
                    for r in range(2):
                        R.wait("t", "cc", cc_ev)
                        for j in range(8):
                            R.op("t", lambda h, r=r, j=j: h.matmul(
                                psTT[:, j * 128:(j + 1) * 128],
                                OSB[:, (r * 8 + j) * 128:(r * 8 + j) * 128 + 128],
                                IDENT[:], start=True, stop=True))
                        vt = R.op("t", None, inc="ct")
                        R.wait("sc", "ct", vt)
                        R.wait("sc", "cv", v_rec)
                        for j in range(8):
                            kt2 = r * 8 + j
                            cc_ev = R.op("sc", lambda h, kt2=kt2, j=j: h.activation(
                                OT[:, kt2 * 128:(kt2 + 1) * 128],
                                psTT[:, j * 128:(j + 1) * 128], AF.Copy,
                                scale=RECT[:, kt2:kt2 + 1]), inc="cc")
                    if head == 0 and "ot0" in taps:
                        R.wait("sy", "cc", cc_ev)
                        R.op("sy", lambda h: h.dma_start(tap_h["ot0"], OT[:]), inc="cs",
                             n=16)
                    for u in range(2):
                        R.wait("t", "cc", cc_ev)
                        R.wait("t", "cv", v_rec)
                        for i, kt in enumerate(range(u, 16, 2)):
                            for hf in range(2):
                                R.op("t", lambda h, kt=kt, i=i, hf=hf: h.matmul(
                                    psAm[:, hf * 512:(hf + 1) * 512],
                                    OT[:, kt * 128:(kt + 1) * 128],
                                    WOTP[:, (kt // 2) * E + hf * 512:
                                         (kt // 2) * E + hf * 512 + 512],
                                    start=(i == 0), stop=(i == 7)))
                        vt = R.op("t", None, inc="ct")
                        R.wait("sc", "ct", vt)
                        v_asb = R.op("sc", lambda h: h.activation(
                            ASB[:, 0:1024], psAm[:], AF.Copy), inc="cc")
                        R.wait("t", "cc", v_asb)
                        for cc_i in range(8):
                            R.op("t", lambda h, cc_i=cc_i: h.matmul(
                                psTT[:, cc_i * 128:(cc_i + 1) * 128],
                                ASB[:, cc_i * 128:(cc_i + 1) * 128], IDENT[:], start=True, stop=True))
                        vt = R.op("t", None, inc="ct")
                        R.wait("sc", "ct", vt)
                        cc_ev = R.op("sc", lambda h, u=u, head=head: h.activation(
                            bass.AP(ACC, (u * 8) * E + head * 128,
                                    [[16 * E, 128], [E, 8], [1, 128]]),
                            bass.AP(psTT, 0, [[1024, 128], [128, 8], [1, 128]]),
                            AF.Copy), inc="cc")
                    v_accev_prev = cc_ev
            if "acc1" in taps:
                R.wait("sy", "cc", R.vals["cc"])
                R.op("sy", lambda h: h.dma_start(tap_h["acc1"], ACC[:]), inc="cs", n=16)
        if not (do_attn and stage >= 3):
            vz = R.op("v", lambda h: h.memset(ACC[:], 0.0), inc="cv")
            R.wait("t", "cv", vz)
        R.barrier()

        pA.close()

        if do_ffn and stage >= 4:
            COMBF = sb("COMBF", [16, S_LEN], F32)
            with contextlib.ExitStack() as p2r:
                psbr = lambda n_, s_, d_: p2r.enter_context(nc.psum_tensor(n_, s_, d_))
                sbr = lambda n_, s_, d_: p2r.enter_context(nc.sbuf_tensor(n_, s_, d_))
                RWB = sbr("RWB", [128, 8 * 16], BF16)
                LGF = sbr("LGF", [16, S_LEN], F32)
                LGT = sbr("LGT", [128, 256], F32)
                COMBT = sbr("COMBT", [128, 256], F32)
                RS = sbr("RS", [128, 8], F32)
                L2 = sbr("L2", [128, 16], F32)
                MS1 = sbr("MS1", [128, 16], F32)
                MS2 = sbr("MS2", [128, 16], F32)
                psL = psbr("psL", [16, S_LEN], F32)
                psLT = psbr("psLT", [128, 256], F32)
                v_rw = R.op("g", lambda h: h.dma_start(
                    RWB[:].rearrange("e (ec x) -> e ec x", ec=8),
                    rw_h.rearrange("(ec e) x -> e ec x", e=128)), inc="cg", n=16)
                R.wait("t", "cg", v_rw)
                for ec in range(8):
                    for hf in range(4):
                        R.op("t", lambda h, ec=ec, hf=hf: h.matmul(
                            psL[:, hf * 512:(hf + 1) * 512],
                            RWB[:, ec * 16:(ec + 1) * 16],
                            XNT[:, ec * S_LEN + hf * 512: ec * S_LEN + hf * 512 + 512],
                            start=(ec == 0), stop=(ec == 7)))
                vt = R.op("t", None, inc="ct")
                R.wait("sc", "ct", vt)
                v_lg = R.op("sc", lambda h: h.activation(LGF[:], psL[:], AF.Copy),
                            inc="cc")
                R.wait("t", "cc", v_lg)
                for tc in range(16):
                    R.op("t", lambda h, tc=tc: h.matmul(
                        psLT[:, tc * 16:(tc + 1) * 16],
                        LGF[:, tc * 128:(tc + 1) * 128], IDENTF[0:16, 0:16], start=True, stop=True))
                vt = R.op("t", None, inc="ct")
                R.wait("sc", "ct", vt)
                v_lgt = R.op("sc", lambda h: h.activation(LGT[:], psLT[:], AF.Copy),
                             inc="cc")
                R.wait("v", "cc", v_lgt)
                for tc in range(16):
                    vv = R.op("v", lambda h, tc=tc: h.reduce_max(
                        RS[:, 0:1], LGT[:, tc * 16:(tc + 1) * 16], AX.X), inc="cv")
                    R.wait("v", "cv", vv)
                    vv = R.op("v", lambda h, tc=tc: h.tensor_scalar(
                        MS1[:], LGT[:, tc * 16:(tc + 1) * 16], RS[:, 0:1], None,
                        ALU.is_equal), inc="cv")
                    R.wait("v", "cv", vv)
                    vv = R.op("v", lambda h, tc=tc: h.scalar_tensor_tensor(
                        L2[:], MS1[:], CV[2], LGT[:, tc * 16:(tc + 1) * 16],
                        ALU.mult, ALU.add), inc="cv")
                    R.wait("v", "cv", vv)
                    vv = R.op("v", lambda h: h.reduce_max(RS[:, 1:2], L2[:], AX.X),
                              inc="cv")
                    R.wait("v", "cv", vv)
                    vv = R.op("v", lambda h: h.tensor_scalar(
                        MS2[:], L2[:], RS[:, 1:2], None, ALU.is_equal), inc="cv")
                    R.wait("v", "cv", vv)
                    vv = R.op("v", lambda h: h.tensor_tensor(
                        RS[:, 2:3], RS[:, 1:2], RS[:, 0:1], ALU.subtract), inc="cv")
                    R.wait("sc", "cv", vv)
                    vs = R.op("sc", lambda h: h.activation(
                        RS[:, 4:5], RS[:, 2:3], AF.Sigmoid), inc="cc")
                    R.wait("sc", "cc", vs)
                    vs = R.op("sc", lambda h: h.activation(
                        RS[:, 3:4], RS[:, 2:3], AF.Sigmoid, scale=CV[3]), inc="cc")
                    R.wait("v", "cc", vs)
                    vv = R.op("v", lambda h, tc=tc: h.tensor_scalar(
                        COMBT[:, tc * 16:(tc + 1) * 16], MS1[:], RS[:, 3:4], None,
                        ALU.mult), inc="cv")
                    R.wait("v", "cv", vv)
                    R.op("v", lambda h, tc=tc: h.scalar_tensor_tensor(
                        COMBT[:, tc * 16:(tc + 1) * 16], MS2[:], RS[:, 4:5],
                        COMBT[:, tc * 16:(tc + 1) * 16], ALU.mult, ALU.add), inc="cv")
                v_combt = R.vals["cv"]
                if "comb" in taps:
                    R.wait("sy", "cv", v_combt)
                    R.op("sy", lambda h: h.dma_start(tap_h["comb"], COMBT[:]),
                         inc="cs", n=16)
                R.wait("t", "cv", v_combt)
                for tc in range(16):
                    R.op("t", lambda h, tc=tc: h.matmul(
                        psL[:, tc * 128:(tc + 1) * 128],
                        COMBT[:, tc * 16:(tc + 1) * 16], IDENTF[:], start=True, stop=True))
                vt = R.op("t", None, inc="ct")
                R.wait("sc", "ct", vt)
                v_combf = R.op("sc", lambda h: h.activation(
                    COMBF[:], psL[:], AF.Copy), inc="cc")
                R.wait("g", "cc", v_combf)
                v_combd = R.op("g", lambda h: h.dma_start(combd_h, COMBF[:]),
                               inc="cg", n=16)
            R.barrier()

            with contextlib.ExitStack() as p2e:
                psbe = lambda n_, s_, d_: p2e.enter_context(nc.psum_tensor(n_, s_, d_))
                sbe = lambda n_, s_, d_: p2e.enter_context(nc.sbuf_tensor(n_, s_, d_))
                psG = psbe("psG", [128, S_LEN], F32)
                psD = [psbe(f"psD{i}", [128, 1024], F32) for i in range(2)]
                HARE = sbe("HARE", [128, 8 * S_LEN], BF16)
                SG = sbe("SG", [128, S_LEN], BF16)
                TFFN = sbe("TFFN", [128, S_LEN], BF16)
                WGA = sbe("WGA", [128, 8 * 512], BF16)
                WUA = sbe("WUA", [128, 8 * 512], BF16)
                WDA = sbe("WDA", [128, 8 * 1024], BF16)
                CB = [sbe(f"CB{i}", [128, S_LEN], F32) for i in range(2)]

                groups = [("r", [4 * g + i for i in range(4)]) for g in range(4)]
                groups.append(("s", [0, 1]))
                v_wg_used = {}
                v_wd_used = {}
                v_cb_used = {}
                v_prev_add = [0, 0]
                v_psg_free = 0
                for gi, (gkind, gl) in enumerate(groups):
                    shared = gkind == "s"
                    nmc = 4 if shared else 2
                    mid = 512 if shared else 256
                    ndslots = 2 if shared else 4
                    v_h = []
                    v_ld_all = 0
                    for li, e in enumerate(gl):
                        sl_g = 0 if shared else li % 2
                        sl_d = li % ndslots
                        wg_off = sl_g * 8 * 256 if not shared else 0
                        wd_off = sl_d * (4 * E if shared else 2 * E)
                        R.wait("g", "cv", v_wg_used.get(sl_g, 0))
                        src_g = (wg_s_h if shared else wg_r_h)[e].rearrange(
                            "(ec e2) m -> e2 ec m", e2=128)
                        v_lg_ = R.op("g", lambda h, src=src_g, off=wg_off, mid=mid:
                                     h.dma_start(WGA[:, off:off + 8 * mid].rearrange(
                                         "p (ec m) -> p ec m", ec=8), src),
                                     inc="cg", n=16)
                        src_u = (wu_s_h if shared else wu_r_h)[e].rearrange(
                            "(ec e2) m -> e2 ec m", e2=128)
                        v_lu_ = R.op("g", lambda h, src=src_u, off=wg_off, mid=mid:
                                     h.dma_start(WUA[:, off:off + 8 * mid].rearrange(
                                         "p (ec m) -> p ec m", ec=8), src),
                                     inc="cg", n=16)
                        R.wait("g", "cv", v_wd_used.get(sl_d, 0))
                        src_d = (wd_s_h if shared else wd_r_h)[e].rearrange(
                            "(mc m) x -> m mc x", m=128)
                        v_ld_all = R.op("g", lambda h, src=src_d, off=wd_off, nmc=nmc:
                                        h.dma_start(WDA[:, off:off + nmc * E].rearrange(
                                            "p (mc x) -> p mc x", mc=nmc), src),
                                        inc="cg", n=16)
                        v_cb_ = 0
                        if not shared:
                            R.wait("g", "cv", v_cb_used.get(sl_g, 0))
                            v_cb_ = R.op("g", lambda h, e=e, sl=sl_g: h.dma_start(
                                CB[sl][:], combd_h[e:e + 1, :].partition_broadcast(128)),
                                inc="cg", n=16)
                        for mc in range(nmc):
                            R.wait("t", "cg", v_lg_)
                            R.wait("t", "cv", v_psg_free)
                            for ec in range(8):
                                for hf in range(4):
                                    R.op("t", lambda h, ec=ec, hf=hf, mc=mc, off=wg_off,
                                         mid=mid: h.matmul(
                                        psG[:, hf * 512:(hf + 1) * 512],
                                        WGA[:, off + ec * mid + mc * 128:
                                            off + ec * mid + mc * 128 + 128],
                                        XNT[:, ec * S_LEN + hf * 512:
                                            ec * S_LEN + hf * 512 + 512],
                                        start=(ec == 0), stop=(ec == 7)))
                            vt = R.op("t", None, inc="ct")
                            R.wait("sc", "ct", vt)
                            R.wait("sc", "cv", v_psg_free)
                            v_sg = R.op("sc", lambda h: h.activation(
                                SG[:], psG[:], AF.Silu), inc="cc")
                            R.wait("t", "cc", v_sg)
                            R.wait("t", "cg", v_lu_)
                            for ec in range(8):
                                for hf in range(4):
                                    R.op("t", lambda h, ec=ec, hf=hf, mc=mc, off=wg_off,
                                         mid=mid: h.matmul(
                                        psG[:, hf * 512:(hf + 1) * 512],
                                        WUA[:, off + ec * mid + mc * 128:
                                            off + ec * mid + mc * 128 + 128],
                                        XNT[:, ec * S_LEN + hf * 512:
                                            ec * S_LEN + hf * 512 + 512],
                                        start=(ec == 0), stop=(ec == 7)))
                            vt = R.op("t", None, inc="ct")
                            R.wait("v", "ct", vt)
                            R.wait("v", "cc", v_sg)
                            hslot = li * nmc + mc
                            if not shared:
                                R.wait("v", "cg", v_cb_)
                                vv = R.op("v", lambda h: h.tensor_tensor(
                                    TFFN[:], SG[:], psG[:], ALU.mult), inc="cv")
                                R.wait("v", "cv", vv)
                                vv = R.op("v", lambda h, hslot=hslot, sl=sl_g:
                                          h.tensor_tensor(
                                    HARE[:, hslot * S_LEN:(hslot + 1) * S_LEN],
                                    TFFN[:], CB[sl][:], ALU.mult), inc="cv")
                            else:
                                vv = R.op("v", lambda h, hslot=hslot: h.tensor_tensor(
                                    HARE[:, hslot * S_LEN:(hslot + 1) * S_LEN],
                                    SG[:], psG[:], ALU.mult), inc="cv")
                            v_psg_free = vv
                            v_h.append(vv)
                        v_wg_used[sl_g] = v_psg_free
                        if not shared:
                            v_cb_used[sl_g] = v_psg_free
                    v_hall = max(v_h)
                    R.wait("t", "cv", v_hall)
                    R.wait("t", "cg", v_ld_all)
                    v_add = [0] * 16
                    for tc in range(16):
                        par = tc % 2
                        prev = v_add[tc - 2] if tc >= 2 else v_prev_add[par]
                        R.wait("t", "cv", prev)
                        first = True
                        nexp = len(gl)
                        for li in range(nexp):
                            sl_d = li % ndslots
                            wd_off2 = sl_d * (4 * E if shared else 2 * E)
                            for mc in range(nmc):
                                hslot = li * nmc + mc
                                last = (li == nexp - 1 and mc == nmc - 1)
                                for hf in range(2):
                                    R.op("t", lambda h, tc=tc, par=par, hslot=hslot,
                                         off=wd_off2, mc=mc, first=first, hf=hf, last=last:
                                         h.matmul(
                                        psD[par][:, hf * 512:(hf + 1) * 512],
                                        HARE[:, hslot * S_LEN + tc * 128:
                                             hslot * S_LEN + tc * 128 + 128],
                                        WDA[:, off + mc * E + hf * 512:
                                            off + mc * E + hf * 512 + 512],
                                        start=first, stop=last,
                                        skip_group_check=True))
                                first = False
                        vt = R.op("t", None, inc="ct")
                        R.wait("v", "ct", vt)
                        v_add[tc] = R.op("v", lambda h, tc=tc, par=par: h.tensor_tensor(
                            ACC[:, tc * E:(tc + 1) * E], psD[par][:],
                            ACC[:, tc * E:(tc + 1) * E], ALU.add), inc="cv")
                    v_prev_add = [v_add[14], v_add[15]]
                    for sl in range(ndslots):
                        v_wd_used[sl] = v_add[15]
        R.barrier()
        if "accf" in taps:
            R.op("sy", lambda h: h.dma_start(tap_h["accf"], ACC[:]), inc="cs", n=16)

        with contextlib.ExitStack() as p3:
            sb3 = lambda n_, s_, d_: p3.enter_context(nc.sbuf_tensor(n_, s_, d_))
            XI = [sb3(f"XI{i}", [128, E], F32) for i in range(2)]
            OU = [sb3(f"OU{i}", [128, E], F32) for i in range(2)]
            v_st = [0] * 16
            v_ad = [0] * 16
            for tc in range(16):
                par = tc % 2
                if tc >= 2:
                    R.wait("sy", "cv", v_ad[tc - 2])
                    R.wait("sy", "cs", v_st[tc - 2])
                v_xl = R.op("sy", lambda h, tc=tc, par=par: h.dma_start(
                    XI[par][:], x_h[tc * 128:(tc + 1) * 128, :]), inc="cs", n=16)
                R.wait("v", "cs", v_xl)
                v_ad[tc] = R.op("v", lambda h, tc=tc, par=par: h.tensor_tensor(
                    OU[par][:], ACC[:, tc * E:(tc + 1) * E], XI[par][:], ALU.add),
                    inc="cv")
                R.wait("sy", "cv", v_ad[tc])
                v_st[tc] = R.op("sy", lambda h, tc=tc, par=par: h.dma_start(
                    out_h[tc * 128:(tc + 1) * 128, :], OU[par][:]), inc="cs", n=16)
            R.wait("sy", "cs", R.vals["cs"])
            R.wait("g", "cs", R.vals["cs"])

        with nc.Block() as block:
            R.replay(block)
    return nc


def make_in_map(inputs, core_id):
    hc = host_constants()
    m = {
        "x": np.ascontiguousarray(np.asarray(inputs["L0_attn_emb"])[core_id]),
        "w1": np.asarray(inputs["L0_attn_W1"]),
        "w2": np.asarray(inputs["L0_attn_W2"]),
        "w3": np.asarray(inputs["L0_attn_W3"]),
        "wo": np.asarray(inputs["L0_attn_Wo"]),
        "rw": np.asarray(inputs["L0_ffn_router_w"]),
        "wg_r": np.asarray(inputs["L0_ffn_wg_r"]),
        "wu_r": np.asarray(inputs["L0_ffn_wu_r"]),
        "wd_r": np.asarray(inputs["L0_ffn_wd_r"]),
        "wg_s": np.asarray(inputs["L0_ffn_wg_s"]),
        "wu_s": np.asarray(inputs["L0_ffn_wu_s"]),
        "wd_s": np.asarray(inputs["L0_ffn_wd_s"]),
        "cost": hc["cost"], "sint": hc["sint"], "ident": hc["ident"],
        "identf": hc["identf"], "rmat": hc["rmat"], "consts": hc["consts"],
    }
    return m


def kernel(**inputs):
    nc = build()
    in_maps = [make_in_map(inputs, c) for c in range(8)]
    res = run_bass_kernel_spmd(nc, in_maps, core_ids=list(range(8)))
    import numpy as _np
    return _np.stack([_np.asarray(r["out"]) for r in res.results], axis=0)


# version 02
# speedup vs baseline: 1.3776x; 1.3776x over previous
import contextlib
import numpy as np

import concourse.bass as bass
import concourse.mybir as mybir
from concourse.bass_utils import run_bass_kernel_spmd

F32 = mybir.dt.float32
BF16 = mybir.dt.bfloat16
AF = mybir.ActivationFunctionType
ALU = mybir.AluOpType
AX = mybir.AxisListType

S_LEN = 2048
E = 1024
NH = 8
TH = 256

INV_SQRT8 = 0.35355339059327373
SOFTMAX_SHIFT = -30.0
CV = [1.0 / 1024.0, 1e-5, -1e30, -1.0, INV_SQRT8, SOFTMAX_SHIFT]

ENG_NAMES = {"t": "tensor", "sc": "scalar", "v": "vector", "g": "gpsimd", "sy": "sync"}


class Rec:
    def __init__(self, nc, stack):
        self.nc = nc
        self.q = {e: [] for e in ENG_NAMES}
        self.hwm = {}
        self.sems = {}
        self.vals = {}
        for name in ("cg", "cs", "ct", "cc", "cv", "cgp",
                     "et", "ec", "ev2", "eg"):
            self.sems[name] = stack.enter_context(nc.semaphore(name))
            self.vals[name] = 0

    EVT = {"sc": "ec", "v": "ev2", "g": "eg"}

    def op(self, eng, fn, inc=None, n=1):
        if inc is None:
            if eng == "t":
                self.q[eng].append({"fn": fn, "inc": None})
            else:
                self.q[eng].append(fn)
            return None
        sem = self.sems[inc]
        self.vals[inc] += n
        v = self.vals[inc]
        if eng == "t":
            if fn is None:
                last = self.q[eng][-1]
                assert isinstance(last, dict) and last["inc"] is None, \
                    "no tensor instruction to attach inc to"
                last["inc"] = (sem, n)
            else:
                self.q[eng].append({"fn": fn, "inc": (sem, n)})
            return v
        if inc in ("cs", "cg"):
            def wrapped(h, fn=fn, sem=sem, n=n):
                fn(h).then_inc(sem, n)

            self.q[eng].append(wrapped)
        else:
            en = self.EVT[eng]
            esem = self.sems[en]
            self.vals[en] += 1
            ev = self.vals[en]

            def wrapped(h, fn=fn, sem=sem, n=n, esem=esem, ev=ev):
                fn(h).then_inc(esem, 1)
                h.wait_ge(esem, ev)
                h.nop().then_inc(sem, n)

            self.q[eng].append(wrapped)
        return v

    def wait(self, eng, cname, v):
        if not v or v <= 0:
            return
        hw = self.hwm.setdefault((eng, cname), 0)
        if v <= hw:
            return
        self.hwm[(eng, cname)] = v
        sem = self.sems[cname]
        self.q[eng].append(lambda h, sem=sem, v=v: h.wait_ge(sem, v))

    def barrier(self):
        engs = ("t", "sc", "v", "g", "sy")
        for e in ("sc", "v", "g"):
            self.op(e, lambda h: h.nop(),
                    inc={"sc": "cc", "v": "cv", "g": "cgp"}[e])
        if self.q["t"] and isinstance(self.q["t"][-1], dict) \
                and self.q["t"][-1]["inc"] is None:
            self.op("t", None, inc="ct")
        snap = dict(self.vals)
        for e in engs:
            for cn in ("ct", "cc", "cv", "cgp", "cg", "cs"):
                if e == "t" and cn == "ct":
                    continue
                self.wait(e, cn, snap[cn])

    def replay(self, block):
        for eng, name in ENG_NAMES.items():
            ops = self.q[eng]

            def body(h, ops=ops):
                for f in ops:
                    if isinstance(f, dict):
                        instr = f["fn"](h)
                        if f["inc"] is not None:
                            instr.then_inc(*f["inc"])
                    else:
                        f(h)

            getattr(block, name)(body)


def host_constants():
    import ml_dtypes
    DH = 128
    d = {}
    inv_freq = 1.0 / 10000.0 ** (np.arange(0, DH, 2, dtype=np.float64) / DH)
    t = np.arange(S_LEN, dtype=np.float64)
    freqs = np.repeat(t[:, None] * inv_freq[None, :], 2, axis=1)
    COS = np.cos(freqs)
    SIN2 = np.sin(freqs)
    SIN2[:, 0::2] *= -1.0
    tau_to_sp = np.empty(S_LEN, np.int64)
    for c in range(8):
        for sl in range(TH):
            tau_to_sp[c * TH + sl] = 8 * sl + c
    d["cost"] = np.ascontiguousarray(COS[tau_to_sp, :].T).astype(ml_dtypes.bfloat16)
    d["sint"] = np.ascontiguousarray(SIN2[tau_to_sp, :].T).astype(ml_dtypes.bfloat16)
    d["ident"] = np.eye(128, dtype=ml_dtypes.bfloat16)
    d["identf"] = np.eye(128, dtype=np.float32)
    r = np.zeros((128, 128), np.float32)
    for i in range(128):
        r[i ^ 1, i] = 1.0
    d["rmat"] = r.astype(ml_dtypes.bfloat16)
    consts = np.zeros((128, 8), np.float32)
    for i, v in enumerate(CV):
        consts[:, i] = v
    d["consts"] = consts
    return d


def build(n_heads=NH, do_attn=True, do_ffn=True, taps=(), stage=9, p0_tr=True):
    nc = bass.Bass("TRN2", target_bir_lowering=False, detect_race_conditions=False)

    x_h = nc.dram_tensor("x", [S_LEN, E], F32, kind="ExternalInput").ap()
    w_h = {nm: nc.dram_tensor(nm, [E, E], F32, kind="ExternalInput").ap()
           for nm in ("w1", "w2", "w3", "wo")}
    rw_h = nc.dram_tensor("rw", [E, 16], F32, kind="ExternalInput").ap()
    wg_r_h = nc.dram_tensor("wg_r", [16, E, 256], F32, kind="ExternalInput").ap()
    wu_r_h = nc.dram_tensor("wu_r", [16, E, 256], F32, kind="ExternalInput").ap()
    wd_r_h = nc.dram_tensor("wd_r", [16, 256, E], F32, kind="ExternalInput").ap()
    wg_s_h = nc.dram_tensor("wg_s", [2, E, 512], F32, kind="ExternalInput").ap()
    wu_s_h = nc.dram_tensor("wu_s", [2, E, 512], F32, kind="ExternalInput").ap()
    wd_s_h = nc.dram_tensor("wd_s", [2, 512, E], F32, kind="ExternalInput").ap()
    cost_h = nc.dram_tensor("cost", [128, S_LEN], BF16, kind="ExternalInput").ap()
    sint_h = nc.dram_tensor("sint", [128, S_LEN], BF16, kind="ExternalInput").ap()
    ident_h = nc.dram_tensor("ident", [128, 128], BF16, kind="ExternalInput").ap()
    identf_h = nc.dram_tensor("identf", [128, 128], F32, kind="ExternalInput").ap()
    rmat_h = nc.dram_tensor("rmat", [128, 128], BF16, kind="ExternalInput").ap()
    consts_h = nc.dram_tensor("consts", [128, 8], F32, kind="ExternalInput").ap()
    out_h = nc.dram_tensor("out", [S_LEN, E], F32, kind="ExternalOutput").ap()
    combd_h = nc.dram_tensor("combd", [16, S_LEN], F32, kind="Internal").ap()

    tap_h = {}

    def mktap(name, shape, dtype):
        if name in taps:
            tap_h[name] = nc.dram_tensor("tap_" + name, shape, dtype,
                                         kind="ExternalOutput").ap()

    mktap("xnt", [128, 8 * S_LEN], BF16)
    mktap("qr0", [128, S_LEN], BF16)
    mktap("kr0", [128, S_LEN], BF16)
    mktap("vt0", [128, S_LEN], BF16)
    mktap("ot0", [128, S_LEN], BF16)
    mktap("acc1", [128, 16 * E], F32)
    mktap("comb", [128, 256], F32)
    mktap("accf", [128, 16 * E], F32)

    top = contextlib.ExitStack()
    with top:
        R = Rec(nc, top)
        sb = lambda name, shape, dtype: top.enter_context(nc.sbuf_tensor(name, shape, dtype))

        ACC = sb("ACC", [128, 16 * E], F32)
        XNT = sb("XNT", [128, 8 * S_LEN], BF16)
        CONSTS = sb("CONSTS", [128, 8], F32)
        IDENT = sb("IDENT", [128, 128], BF16)
        IDENTF = sb("IDENTF", [128, 128], F32)
        RMAT = sb("RMAT", [128, 128], BF16)
        ONES = sb("ONESB", [128, 1], BF16)
        for i, v in enumerate(CV):
            nc.const_aps.register(F32, v, CONSTS[:, i:i + 1])

        v_c = R.op("sy", lambda h: h.dma_start(CONSTS[:], consts_h[:]), inc="cs", n=16)
        v_c = R.op("sy", lambda h: h.dma_start(IDENT[:], ident_h[:]), inc="cs", n=16)
        v_c = R.op("sy", lambda h: h.dma_start(IDENTF[:], identf_h[:]), inc="cs", n=16)
        v_c = R.op("sy", lambda h: h.dma_start(RMAT[:], rmat_h[:]), inc="cs", n=16)
        v_ones = R.op("g", lambda h: h.memset(ONES[:], 1.0), inc="cgp")
        for e in ("t", "sc", "v", "g", "sy"):
            R.wait(e, "cs", v_c)
            R.wait(e, "cgp", v_ones)

        p0 = contextlib.ExitStack()

        def phase_p0_ln():
            psb0 = lambda n_, s_, d_: p0.enter_context(nc.psum_tensor(n_, s_, d_))
            sb0 = lambda n_, s_, d_: p0.enter_context(nc.sbuf_tensor(n_, s_, d_))
            XIN = [sb0(f"XIN{i}", [128, E], F32) for i in range(2)]
            XNM = [sb0(f"XNM{i}", [128, E], BF16) for i in range(2)]
            JUNK = sb0("JUNK", [128, E], BF16)
            MB = sb0("MB", [128, 8], F32)
            psT = [psb0(f"psT0{i}", [128, 1024], F32) for i in range(2)]

            v_ln = [0] * 16
            v_ev = [0] * 16
            v_tr = [0] * 16
            for tc in range(16):
                par = tc % 2
                if tc >= 2:
                    R.wait("sy", "cv", v_ln[tc - 2])
                v_x = R.op("sy", lambda h, tc=tc, par=par: h.dma_start(
                    XIN[par][:], x_h[tc * 128:(tc + 1) * 128, :]), inc="cs", n=16)
                R.wait("v", "cs", v_x)
                R.op("v", lambda h, par=par: h.reduce_sum(MB[:, 6:7], XIN[par][:], AX.X),
                     inc="cv")
                vv = R.op("v", lambda h: h.tensor_scalar(
                    MB[:, 0:1], MB[:, 6:7], CV[0], None, ALU.mult), inc="cv")
                R.wait("sc", "cs", v_x)
                if tc >= 1:
                    R.wait("sc", "cv", v_ln[tc - 1])
                vs = R.op("sc", lambda h, par=par: h.activation(
                    JUNK[:], XIN[par][:], AF.Square, accum_out=MB[:, 1:2]), inc="cc")
                R.wait("v", "cc", vs)
                R.op("v", lambda h: h.tensor_scalar(
                    MB[:, 2:3], MB[:, 1:2], CV[0], None, ALU.mult), inc="cv")
                R.wait("v", "cv", vv)
                vv = R.op("v", lambda h: h.tensor_tensor(
                    MB[:, 6:7], MB[:, 0:1], MB[:, 0:1], ALU.mult), inc="cv")
                R.wait("v", "cv", vv)
                vv = R.op("v", lambda h: h.tensor_tensor(
                    MB[:, 2:3], MB[:, 2:3], MB[:, 6:7], ALU.subtract), inc="cv")
                R.wait("sc", "cv", vv)
                vs = R.op("sc", lambda h: h.activation(
                    MB[:, 3:4], MB[:, 2:3], AF.Sqrt, bias=CV[1]), inc="cc")
                R.wait("v", "cc", vs)
                vv = R.op("v", lambda h: h.reciprocal(MB[:, 4:5], MB[:, 3:4]), inc="cv")
                R.wait("v", "cv", vv)
                if tc >= 2:
                    R.wait("v", "ct", v_tr[tc - 2])
                v_ln[tc] = R.op("v", lambda h, tc=tc, par=par: h.tensor_scalar(
                    XNM[par][:], XIN[par][:], MB[:, 0:1], MB[:, 4:5],
                    ALU.subtract, ALU.mult), inc="cv")
                R.wait("t", "cv", v_ln[tc])
                if tc >= 2:
                    R.wait("t", "cc", v_ev[tc - 2])
                for ec in range(8):
                    R.op("t", lambda h, ec=ec, tc=tc, par=par: h.matmul(
                        psT[par][:, ec * 128:(ec + 1) * 128],
                        XNM[par][:, ec * 128:(ec + 1) * 128],
                        IDENT[:], start=True, stop=True))
                v_tr[tc] = R.op("t", None, inc="ct")
                R.wait("sc", "ct", v_tr[tc])
                v_ev[tc] = R.op("sc", lambda h, tc=tc, par=par: h.activation(
                    bass.AP(XNT, tc * 128, [[8 * S_LEN, 128], [S_LEN, 8], [1, 128]]),
                    bass.AP(psT[par], 0, [[1024, 128], [128, 8], [1, 128]]),
                    AF.Copy), inc="cc")
            if "xnt" in taps:
                R.wait("sy", "cc", v_ev[15])
                R.op("sy", lambda h: h.dma_start(tap_h["xnt"], XNT[:]), inc="cs", n=16)
            return v_ev

        pA = contextlib.ExitStack()
        sbA = lambda name, shape, dtype: pA.enter_context(nc.sbuf_tensor(name, shape, dtype))
        WOTP = sbA("WOTP", [128, 8 * E], BF16)
        W123T = {nm: sbA(nm + "T", [128, 8 * E], BF16) for nm in ("w1", "w2", "w3")}
        def phase_w():
          with contextlib.ExitStack() as pw:
            psbw = lambda n_, s_, d_: pw.enter_context(nc.psum_tensor(n_, s_, d_))
            sbw = lambda n_, s_, d_: pw.enter_context(nc.sbuf_tensor(n_, s_, d_))
            WNAT = [sbw(f"WNAT{i}", [128, 8 * E], BF16) for i in range(2)]
            psTW = psbw("psTW", [128, 1024], F32)
            names = ["w1", "w2", "w3", "wo"]
            v_tdone = [0] * 4
            cc_ev = 0
            for i, nm in enumerate(names):
                par = i % 2
                if i >= 2:
                    R.wait("g", "ct", v_tdone[i - 2])
                v_wl = R.op("g", lambda h, nm=nm, par=par: h.dma_start(
                    WNAT[par][:].rearrange("p (pc e) -> p pc e", pc=8),
                    w_h[nm].rearrange("(pc p) e -> p pc e", p=128)),
                    inc="cg", n=16)
                R.wait("t", "cg", v_wl)
                for blk in range(8):
                    R.wait("t", "cc", cc_ev)
                    for j in range(8):
                        if nm != "wo":
                            R.op("t", lambda h, blk=blk, j=j, par=par: h.matmul(
                                psTW[:, j * 128:(j + 1) * 128],
                                WNAT[par][:, j * E + blk * 128: j * E + blk * 128 + 128],
                                IDENT[:], start=True, stop=True))
                        else:
                            R.op("t", lambda h, blk=blk, j=j, par=par: h.matmul(
                                psTW[:, j * 128:(j + 1) * 128],
                                WNAT[par][:, j * E + blk: j * E + blk + 1017:8],
                                IDENT[:], start=True, stop=True))
                    vt = R.op("t", None, inc="ct")
                    R.wait("sc", "ct", vt)
                    if nm != "wo":
                        cc_ev = R.op("sc", lambda h, nm=nm, blk=blk: h.activation(
                            W123T[nm][:, blk * E:(blk + 1) * E], psTW[:], AF.Copy),
                            inc="cc")
                    else:
                        cc_ev = R.op("sc", lambda h, blk=blk: h.activation(
                            WOTP[:, blk * E:(blk + 1) * E], psTW[:], AF.Copy), inc="cc")
                v_tdone[i] = R.vals["ct"]
            R.wait("t", "cc", cc_ev)
        if stage >= 1:
            _v_ev_saved = phase_p0_ln()
        if stage >= 2:
            phase_w()
        if stage < 1:
            vz = R.op("v", lambda h: h.memset(XNT[:], 0.0), inc="cv")
            R.wait("t", "cv", vz)
        p0.close()
        R.barrier()

        if do_attn and stage >= 3:
            COST = sbA("COST", [128, S_LEN], BF16)
            SINT = sbA("SINT", [128, S_LEN], BF16)
            R.op("sy", lambda h: h.dma_start(COST[:], cost_h[:]), inc="cs", n=16)
            v_tab = R.op("sy", lambda h: h.dma_start(SINT[:], sint_h[:]), inc="cs", n=16)

            QHB = sbA("QHB", [128, S_LEN], BF16)
            KHB = sbA("KHB", [128, S_LEN], BF16)
            QR = sbA("QR", [128, S_LEN], BF16)
            KR = sbA("KR", [128, S_LEN], BF16)
            VT = sbA("VT", [128, S_LEN], BF16)
            T1 = sbA("T1", [128, 512], BF16)
            T2 = sbA("T2", [128, 512], BF16)
            PT = [sbA(f"PT{i}", [128, 1024], BF16) for i in range(2)]
            OSB = sbA("OSB", [128, S_LEN], BF16)
            OT = QHB
            ASB = OSB
            RSROW = sbA("RSROW", [1, S_LEN], F32)
            RECT = sbA("RECT", [128, 16], F32)

            v_rope_prev = 0
            v_accev_prev = 0
            v_exp_carry = [0, 0]
            v_pv_carry = [0, 0]
            v_oe = 0
            for head in range(n_heads):
                with contextlib.ExitStack() as p1a:
                    psba = lambda n_, s_, d_: p1a.enter_context(nc.psum_tensor(n_ + f"_{head}", s_, d_))
                    psQK = psba("psQK", [128, S_LEN], F32)
                    psV = psba("psV", [128, 1024], F32)
                    psSH = psba("psSH", [128, 1024], F32)

                    def proj_qk(wt, dst, head=head):
                        for pc in range(8):
                            for ec in range(8):
                                R.op("t", lambda h, pc=pc, ec=ec, wt=wt, head=head: h.matmul(
                                    psQK[:, pc * TH:(pc + 1) * TH],
                                    W123T[wt][:, ec * E + pc * 128: ec * E + pc * 128 + 128],
                                    XNT[:, ec * S_LEN + head * TH: ec * S_LEN + head * TH + TH],
                                    start=(ec == 0), stop=(ec == 7)))
                        vt = R.op("t", None, inc="ct")
                        R.wait("sc", "ct", vt)
                        return R.op("sc", lambda h, dst=dst: h.activation(
                            dst[:], psQK[:], AF.Copy), inc="cc")

                    R.wait("sc", "cv", v_rope_prev)
                    v_q = proj_qk("w1", QHB)
                    R.wait("t", "cc", v_q)
                    v_k = proj_qk("w2", KHB)

                    v_vev = 0
                    for blk in range(2):
                        R.wait("t", "cc", v_vev)
                        for ec in range(8):
                            for hf in range(2):
                                R.op("t", lambda h, ec=ec, hf=hf, blk=blk, head=head: h.matmul(
                                    psV[:, hf * 512:(hf + 1) * 512],
                                    XNT[:, ec * S_LEN + head * TH + blk * 128:
                                        ec * S_LEN + head * TH + blk * 128 + 128],
                                    W123T["w3"][:, ec * E + hf * 512: ec * E + hf * 512 + 512],
                                    start=(ec == 0), stop=(ec == 7)))
                        vt = R.op("t", None, inc="ct")
                        R.wait("sc", "ct", vt)
                        v_vev = R.op("sc", lambda h, blk=blk: h.activation(
                            bass.AP(VT, blk * 128, [[S_LEN, 128], [TH, 8], [1, 128]]),
                            bass.AP(psV, 0, [[1024, 128], [128, 8], [1, 128]]),
                            AF.Copy), inc="cc")

                    def rope(src, dst, v_src):
                        vq = 0
                        for j in range(4):
                            par = j % 2
                            R.wait("t", "cc", v_src)
                            R.wait("t", "cv", vq)
                            vt = R.op("t", lambda h, j=j, par=par, src=src: h.matmul(
                                psSH[:, par * 512:(par + 1) * 512], RMAT[:],
                                src[:, j * 512:(j + 1) * 512], start=True, stop=True),
                                inc="ct")
                            R.wait("v", "cs", v_tab)
                            R.wait("v", "cc", v_src)
                            R.op("v", lambda h, j=j, src=src: h.tensor_tensor(
                                T1[:], src[:, j * 512:(j + 1) * 512],
                                COST[:, j * 512:(j + 1) * 512], ALU.mult), inc="cv")
                            R.wait("v", "ct", vt)
                            vv = R.op("v", lambda h, j=j, par=par: h.tensor_tensor(
                                T2[:], psSH[:, par * 512:(par + 1) * 512],
                                SINT[:, j * 512:(j + 1) * 512], ALU.mult), inc="cv")
                            R.wait("v", "cv", vv)
                            vq = R.op("v", lambda h, j=j, dst=dst: h.tensor_tensor(
                                dst[:, j * 512:(j + 1) * 512], T1[:], T2[:], ALU.add),
                                inc="cv")
                        return vq

                    v_qr = rope(QHB, QR, v_q)
                    v_kr = rope(KHB, KR, v_k)
                    v_rope_prev = v_kr

                if head == 0 and "qr0" in taps:
                    R.wait("sy", "cv", v_qr)
                    R.op("sy", lambda h: h.dma_start(tap_h["qr0"], QR[:]), inc="cs", n=16)
                    R.wait("sy", "cv", v_kr)
                    R.op("sy", lambda h: h.dma_start(tap_h["kr0"], KR[:]), inc="cs", n=16)
                if head == 0 and "vt0" in taps:
                    R.wait("sy", "cc", R.vals["cc"])
                    R.op("sy", lambda h: h.dma_start(tap_h["vt0"], VT[:]), inc="cs", n=16)

                with contextlib.ExitStack() as p1b:
                    psbb = lambda n_, s_, d_: p1b.enter_context(nc.psum_tensor(n_ + f"_{head}", s_, d_))
                    psS = [psbb(f"psS{i}", [128, 1024], F32) for i in range(2)]
                    psO = psbb("psO", [128, 1024], F32)
                    psR = psbb("psR", [1, 1024], F32)
                    R.wait("t", "cv", v_qr)
                    R.wait("t", "cv", v_kr)
                    v_exp = [0] * 32
                    v_pv = [0] * 32
                    v_s = [0] * 32
                    for kk in range(33):
                        if kk < 32:
                            qh, kt = kk // 16, kk % 16
                            par = kk % 2
                            R.wait("t", "cc",
                                   v_exp[kk - 2] if kk >= 2 else v_exp_carry[kk])
                            for hf in range(2):
                                R.op("t", lambda h, kt=kt, par=par, qh=qh, hf=hf: h.matmul(
                                    psS[par][:, hf * 512:(hf + 1) * 512],
                                    KR[:, kt * 128:(kt + 1) * 128],
                                    QR[:, qh * 1024 + hf * 512: qh * 1024 + hf * 512 + 512],
                                    start=True, stop=True))
                            v_s[kk] = R.op("t", None, inc="ct")
                            R.wait("sc", "ct", v_s[kk])
                            R.wait("sc", "ct",
                                   v_pv[kk - 2] if kk >= 2 else v_pv_carry[kk])
                            v_exp[kk] = R.op("sc", lambda h, par=par: h.activation(
                                PT[par][:], psS[par][:], AF.Exp,
                                bias=SOFTMAX_SHIFT, scale=INV_SQRT8), inc="cc")
                        if kk >= 1:
                            pk = kk - 1
                            qh, kt = pk // 16, pk % 16
                            par = pk % 2
                            R.wait("t", "cc", v_exp[pk])
                            if kt == 0:
                                R.wait("t", "cc", v_oe)
                            for hf in range(2):
                                R.op("t", lambda h, kt=kt, par=par, hf=hf: h.matmul(
                                    psO[:, hf * 512:(hf + 1) * 512],
                                    VT[:, kt * 128:(kt + 1) * 128],
                                    PT[par][:, hf * 512:(hf + 1) * 512],
                                    start=(kt == 0), stop=(kt == 15),
                                    skip_group_check=True))
                            for hf in range(2):
                                R.op("t", lambda h, kt=kt, par=par, hf=hf: h.matmul(
                                    psR[:, hf * 512:(hf + 1) * 512],
                                    ONES[:],
                                    PT[par][:, hf * 512:(hf + 1) * 512],
                                    start=(kt == 0), stop=(kt == 15),
                                    skip_group_check=True))
                            v_pv[pk] = R.op("t", None, inc="ct")
                            if kt == 15:
                                R.wait("sc", "ct", v_pv[pk])
                                R.op("sc", lambda h, qh=qh: h.activation(
                                    RSROW[0:1, qh * 1024:(qh + 1) * 1024], psR[0:1, :],
                                    AF.Copy), inc="cc")
                                v_oe = R.op("sc", lambda h, qh=qh: h.activation(
                                    OSB[:, qh * 1024:(qh + 1) * 1024], psO[:],
                                    AF.Copy), inc="cc")
                    v_exp_carry = [v_exp[30], v_exp[31]]
                    v_pv_carry = [v_pv[30], v_pv[31]]

                with contextlib.ExitStack() as p1c:
                    psbc = lambda n_, s_, d_: p1c.enter_context(nc.psum_tensor(n_ + f"_{head}", s_, d_))
                    psAm = psbc("psAm", [128, 1024], F32)
                    psTT = psbc("psTT", [128, 1024], F32)
                    R.wait("t", "cc", v_oe)
                    cc_ev = v_accev_prev
                    for j in range(16):
                        R.op("t", lambda h, j=j: h.matmul(
                            psAm[:, j:j + 1],
                            RSROW[0:1, j * 128:(j + 1) * 128], IDENTF[0:1, 0:1], start=True, stop=True))
                    vt_rs = R.op("t", None, inc="ct")
                    R.wait("v", "ct", vt_rs)
                    v_rec = R.op("v", lambda h: h.reciprocal(
                        RECT[:], psAm[:, 0:16]), inc="cv")
                    for r in range(2):
                        R.wait("t", "cc", cc_ev)
                        for j in range(8):
                            R.op("t", lambda h, r=r, j=j: h.matmul(
                                psTT[:, j * 128:(j + 1) * 128],
                                OSB[:, (r * 8 + j) * 128:(r * 8 + j) * 128 + 128],
                                IDENT[:], start=True, stop=True))
                        vt = R.op("t", None, inc="ct")
                        R.wait("sc", "ct", vt)
                        R.wait("sc", "cv", v_rec)
                        for j in range(8):
                            kt2 = r * 8 + j
                            cc_ev = R.op("sc", lambda h, kt2=kt2, j=j: h.activation(
                                OT[:, kt2 * 128:(kt2 + 1) * 128],
                                psTT[:, j * 128:(j + 1) * 128], AF.Copy,
                                scale=RECT[:, kt2:kt2 + 1]), inc="cc")
                    if head == 0 and "ot0" in taps:
                        R.wait("sy", "cc", cc_ev)
                        R.op("sy", lambda h: h.dma_start(tap_h["ot0"], OT[:]), inc="cs",
                             n=16)
                    for u in range(2):
                        R.wait("t", "cc", cc_ev)
                        R.wait("t", "cv", v_rec)
                        for i, kt in enumerate(range(u, 16, 2)):
                            for hf in range(2):
                                R.op("t", lambda h, kt=kt, i=i, hf=hf: h.matmul(
                                    psAm[:, hf * 512:(hf + 1) * 512],
                                    OT[:, kt * 128:(kt + 1) * 128],
                                    WOTP[:, (kt // 2) * E + hf * 512:
                                         (kt // 2) * E + hf * 512 + 512],
                                    start=(i == 0), stop=(i == 7)))
                        vt = R.op("t", None, inc="ct")
                        R.wait("sc", "ct", vt)
                        v_asb = R.op("sc", lambda h: h.activation(
                            ASB[:, 0:1024], psAm[:], AF.Copy), inc="cc")
                        R.wait("t", "cc", v_asb)
                        for cc_i in range(8):
                            R.op("t", lambda h, cc_i=cc_i: h.matmul(
                                psTT[:, cc_i * 128:(cc_i + 1) * 128],
                                ASB[:, cc_i * 128:(cc_i + 1) * 128], IDENT[:], start=True, stop=True))
                        vt = R.op("t", None, inc="ct")
                        R.wait("sc", "ct", vt)
                        cc_ev = R.op("sc", lambda h, u=u, head=head: h.activation(
                            bass.AP(ACC, (u * 8) * E + head * 128,
                                    [[16 * E, 128], [E, 8], [1, 128]]),
                            bass.AP(psTT, 0, [[1024, 128], [128, 8], [1, 128]]),
                            AF.Copy), inc="cc")
                    v_accev_prev = cc_ev
            if "acc1" in taps:
                R.wait("sy", "cc", R.vals["cc"])
                R.op("sy", lambda h: h.dma_start(tap_h["acc1"], ACC[:]), inc="cs", n=16)
        if not (do_attn and stage >= 3):
            vz = R.op("v", lambda h: h.memset(ACC[:], 0.0), inc="cv")
            R.wait("t", "cv", vz)
        R.barrier()

        pA.close()

        if do_ffn and stage >= 4:
            COMBF = sb("COMBF", [16, S_LEN], F32)
            with contextlib.ExitStack() as p2r:
                psbr = lambda n_, s_, d_: p2r.enter_context(nc.psum_tensor(n_, s_, d_))
                sbr = lambda n_, s_, d_: p2r.enter_context(nc.sbuf_tensor(n_, s_, d_))
                RWB = sbr("RWB", [128, 8 * 16], BF16)
                LGF = sbr("LGF", [16, S_LEN], F32)
                LGT = sbr("LGT", [128, 256], F32)
                COMBT = sbr("COMBT", [128, 256], F32)
                RS = sbr("RS", [128, 8], F32)
                L2 = sbr("L2", [128, 16], F32)
                MS1 = sbr("MS1", [128, 16], F32)
                MS2 = sbr("MS2", [128, 16], F32)
                psL = psbr("psL", [16, S_LEN], F32)
                psLT = psbr("psLT", [128, 256], F32)
                v_rw = R.op("g", lambda h: h.dma_start(
                    RWB[:].rearrange("e (ec x) -> e ec x", ec=8),
                    rw_h.rearrange("(ec e) x -> e ec x", e=128)), inc="cg", n=16)
                R.wait("t", "cg", v_rw)
                for ec in range(8):
                    for hf in range(4):
                        R.op("t", lambda h, ec=ec, hf=hf: h.matmul(
                            psL[:, hf * 512:(hf + 1) * 512],
                            RWB[:, ec * 16:(ec + 1) * 16],
                            XNT[:, ec * S_LEN + hf * 512: ec * S_LEN + hf * 512 + 512],
                            start=(ec == 0), stop=(ec == 7)))
                vt = R.op("t", None, inc="ct")
                R.wait("sc", "ct", vt)
                v_lg = R.op("sc", lambda h: h.activation(LGF[:], psL[:], AF.Copy),
                            inc="cc")
                R.wait("t", "cc", v_lg)
                for tc in range(16):
                    R.op("t", lambda h, tc=tc: h.matmul(
                        psLT[:, tc * 16:(tc + 1) * 16],
                        LGF[:, tc * 128:(tc + 1) * 128], IDENTF[0:16, 0:16], start=True, stop=True))
                vt = R.op("t", None, inc="ct")
                R.wait("sc", "ct", vt)
                v_lgt = R.op("sc", lambda h: h.activation(LGT[:], psLT[:], AF.Copy),
                             inc="cc")
                R.wait("v", "cc", v_lgt)
                for tc in range(16):
                    vv = R.op("v", lambda h, tc=tc: h.reduce_max(
                        RS[:, 0:1], LGT[:, tc * 16:(tc + 1) * 16], AX.X), inc="cv")
                    R.wait("v", "cv", vv)
                    vv = R.op("v", lambda h, tc=tc: h.tensor_scalar(
                        MS1[:], LGT[:, tc * 16:(tc + 1) * 16], RS[:, 0:1], None,
                        ALU.is_equal), inc="cv")
                    R.wait("v", "cv", vv)
                    vv = R.op("v", lambda h, tc=tc: h.scalar_tensor_tensor(
                        L2[:], MS1[:], CV[2], LGT[:, tc * 16:(tc + 1) * 16],
                        ALU.mult, ALU.add), inc="cv")
                    R.wait("v", "cv", vv)
                    vv = R.op("v", lambda h: h.reduce_max(RS[:, 1:2], L2[:], AX.X),
                              inc="cv")
                    R.wait("v", "cv", vv)
                    vv = R.op("v", lambda h: h.tensor_scalar(
                        MS2[:], L2[:], RS[:, 1:2], None, ALU.is_equal), inc="cv")
                    R.wait("v", "cv", vv)
                    vv = R.op("v", lambda h: h.tensor_tensor(
                        RS[:, 2:3], RS[:, 1:2], RS[:, 0:1], ALU.subtract), inc="cv")
                    R.wait("sc", "cv", vv)
                    vs = R.op("sc", lambda h: h.activation(
                        RS[:, 4:5], RS[:, 2:3], AF.Sigmoid), inc="cc")
                    R.wait("sc", "cc", vs)
                    vs = R.op("sc", lambda h: h.activation(
                        RS[:, 3:4], RS[:, 2:3], AF.Sigmoid, scale=CV[3]), inc="cc")
                    R.wait("v", "cc", vs)
                    vv = R.op("v", lambda h, tc=tc: h.tensor_scalar(
                        COMBT[:, tc * 16:(tc + 1) * 16], MS1[:], RS[:, 3:4], None,
                        ALU.mult), inc="cv")
                    R.wait("v", "cv", vv)
                    R.op("v", lambda h, tc=tc: h.scalar_tensor_tensor(
                        COMBT[:, tc * 16:(tc + 1) * 16], MS2[:], RS[:, 4:5],
                        COMBT[:, tc * 16:(tc + 1) * 16], ALU.mult, ALU.add), inc="cv")
                v_combt = R.vals["cv"]
                if "comb" in taps:
                    R.wait("sy", "cv", v_combt)
                    R.op("sy", lambda h: h.dma_start(tap_h["comb"], COMBT[:]),
                         inc="cs", n=16)
                R.wait("t", "cv", v_combt)
                for tc in range(16):
                    R.op("t", lambda h, tc=tc: h.matmul(
                        psL[:, tc * 128:(tc + 1) * 128],
                        COMBT[:, tc * 16:(tc + 1) * 16], IDENTF[:], start=True, stop=True))
                vt = R.op("t", None, inc="ct")
                R.wait("sc", "ct", vt)
                v_combf = R.op("sc", lambda h: h.activation(
                    COMBF[:], psL[:], AF.Copy), inc="cc")
                R.wait("g", "cc", v_combf)
                v_combd = R.op("g", lambda h: h.dma_start(combd_h, COMBF[:]),
                               inc="cg", n=16)
            R.barrier()

            with contextlib.ExitStack() as p2e:
                psbe = lambda n_, s_, d_: p2e.enter_context(nc.psum_tensor(n_, s_, d_))
                sbe = lambda n_, s_, d_: p2e.enter_context(nc.sbuf_tensor(n_, s_, d_))
                psG1 = psbe("psG1", [128, 1024], F32)
                psG2 = psbe("psG2", [128, 1024], F32)
                psD = [psbe(f"psD{i}", [128, 1024], F32) for i in range(2)]
                HARE = sbe("HARE", [128, 8 * S_LEN], BF16)
                SG = sbe("SG", [128, S_LEN], BF16)
                TFFN = sbe("TFFN", [128, S_LEN], BF16)
                WGA = sbe("WGA", [128, 8 * 512], BF16)
                WUA = sbe("WUA", [128, 8 * 512], BF16)
                WDA = sbe("WDA", [128, 8 * 1024], BF16)
                CB = [sbe(f"CB{i}", [128, S_LEN], F32) for i in range(2)]

                groups = [("r", [4 * g + i for i in range(4)]) for g in range(4)]
                groups.append(("s", [0, 1]))
                v_wg_used = {}
                v_wd_used = {}
                v_cb_used = {}
                v_prev_add = [0, 0]
                v_silu_prev = 0
                v_h_prev = 0
                for gi, (gkind, gl) in enumerate(groups):
                    shared = gkind == "s"
                    nmc = 4 if shared else 2
                    mid = 512 if shared else 256
                    ndslots = 2 if shared else 4
                    v_h = []
                    v_ld_all = 0
                    for li, e in enumerate(gl):
                        sl_g = 0 if shared else li % 2
                        sl_d = li % ndslots
                        wg_off = sl_g * 8 * 256 if not shared else 0
                        wd_off = sl_d * (4 * E if shared else 2 * E)
                        R.wait("g", "cv", v_wg_used.get(sl_g, 0))
                        src_g = (wg_s_h if shared else wg_r_h)[e].rearrange(
                            "(ec e2) m -> e2 ec m", e2=128)
                        v_lg_ = R.op("g", lambda h, src=src_g, off=wg_off, mid=mid:
                                     h.dma_start(WGA[:, off:off + 8 * mid].rearrange(
                                         "p (ec m) -> p ec m", ec=8), src),
                                     inc="cg", n=16)
                        src_u = (wu_s_h if shared else wu_r_h)[e].rearrange(
                            "(ec e2) m -> e2 ec m", e2=128)
                        v_lu_ = R.op("g", lambda h, src=src_u, off=wg_off, mid=mid:
                                     h.dma_start(WUA[:, off:off + 8 * mid].rearrange(
                                         "p (ec m) -> p ec m", ec=8), src),
                                     inc="cg", n=16)
                        R.wait("g", "cv", v_wd_used.get(sl_d, 0))
                        src_d = (wd_s_h if shared else wd_r_h)[e].rearrange(
                            "(mc m) x -> m mc x", m=128)
                        v_ld_all = R.op("g", lambda h, src=src_d, off=wd_off, nmc=nmc:
                                        h.dma_start(WDA[:, off:off + nmc * E].rearrange(
                                            "p (mc x) -> p mc x", mc=nmc), src),
                                        inc="cg", n=16)
                        v_cb_ = 0
                        if not shared:
                            R.wait("g", "cv", v_cb_used.get(sl_g, 0))
                            v_cb_ = R.op("g", lambda h, e=e, sl=sl_g: h.dma_start(
                                CB[sl][:], combd_h[e:e + 1, :].partition_broadcast(128)),
                                inc="cg", n=16)
                        for mc in range(nmc):
                            for th in range(2):
                                R.wait("t", "cg", v_lg_)
                                R.wait("t", "cc", v_silu_prev)
                                for ec in range(8):
                                    for hf in range(2):
                                        R.op("t", lambda h, ec=ec, hf=hf, mc=mc, th=th,
                                             off=wg_off, mid=mid: h.matmul(
                                            psG1[:, hf * 512:(hf + 1) * 512],
                                            WGA[:, off + ec * mid + mc * 128:
                                                off + ec * mid + mc * 128 + 128],
                                            XNT[:, ec * S_LEN + th * 1024 + hf * 512:
                                                ec * S_LEN + th * 1024 + hf * 512 + 512],
                                            start=(ec == 0), stop=(ec == 7)))
                                vt_g = R.op("t", None, inc="ct")
                                R.wait("sc", "ct", vt_g)
                                R.wait("sc", "cv", v_h_prev)
                                v_silu_prev = R.op("sc", lambda h, th=th: h.activation(
                                    SG[:, th * 1024:(th + 1) * 1024], psG1[:],
                                    AF.Silu), inc="cc")
                                R.wait("t", "cg", v_lu_)
                                R.wait("t", "cv", v_h_prev)
                                for ec in range(8):
                                    for hf in range(2):
                                        R.op("t", lambda h, ec=ec, hf=hf, mc=mc, th=th,
                                             off=wg_off, mid=mid: h.matmul(
                                            psG2[:, hf * 512:(hf + 1) * 512],
                                            WUA[:, off + ec * mid + mc * 128:
                                                off + ec * mid + mc * 128 + 128],
                                            XNT[:, ec * S_LEN + th * 1024 + hf * 512:
                                                ec * S_LEN + th * 1024 + hf * 512 + 512],
                                            start=(ec == 0), stop=(ec == 7)))
                                vt_u = R.op("t", None, inc="ct")
                                R.wait("v", "ct", vt_u)
                                R.wait("v", "cc", v_silu_prev)
                                hslot = li * nmc + mc
                                base = hslot * S_LEN + th * 1024
                                if not shared:
                                    R.wait("v", "cg", v_cb_)
                                    vv = R.op("v", lambda h, th=th: h.tensor_tensor(
                                        TFFN[:, 0:1024], SG[:, th * 1024:(th + 1) * 1024],
                                        psG2[:], ALU.mult), inc="cv")
                                    R.wait("v", "cv", vv)
                                    vv = R.op("v", lambda h, base=base, th=th, sl=sl_g:
                                              h.tensor_tensor(
                                        HARE[:, base:base + 1024], TFFN[:, 0:1024],
                                        CB[sl][:, th * 1024:(th + 1) * 1024],
                                        ALU.mult), inc="cv")
                                else:
                                    vv = R.op("v", lambda h, base=base, th=th:
                                              h.tensor_tensor(
                                        HARE[:, base:base + 1024],
                                        SG[:, th * 1024:(th + 1) * 1024],
                                        psG2[:], ALU.mult), inc="cv")
                                v_h_prev = vv
                                v_h.append(vv)
                        v_wg_used[sl_g] = v_h_prev
                        if not shared:
                            v_cb_used[sl_g] = v_h_prev
                    v_hall = max(v_h)
                    R.wait("t", "cv", v_hall)
                    R.wait("t", "cg", v_ld_all)
                    v_add = [0] * 16
                    for tc in range(16):
                        par = tc % 2
                        prev = v_add[tc - 2] if tc >= 2 else v_prev_add[par]
                        R.wait("t", "cv", prev)
                        first = True
                        nexp = len(gl)
                        for li in range(nexp):
                            sl_d = li % ndslots
                            wd_off2 = sl_d * (4 * E if shared else 2 * E)
                            for mc in range(nmc):
                                hslot = li * nmc + mc
                                last = (li == nexp - 1 and mc == nmc - 1)
                                for hf in range(2):
                                    R.op("t", lambda h, tc=tc, par=par, hslot=hslot,
                                         off=wd_off2, mc=mc, first=first, hf=hf, last=last:
                                         h.matmul(
                                        psD[par][:, hf * 512:(hf + 1) * 512],
                                        HARE[:, hslot * S_LEN + tc * 128:
                                             hslot * S_LEN + tc * 128 + 128],
                                        WDA[:, off + mc * E + hf * 512:
                                            off + mc * E + hf * 512 + 512],
                                        start=first, stop=last,
                                        skip_group_check=True))
                                first = False
                        vt = R.op("t", None, inc="ct")
                        R.wait("v", "ct", vt)
                        v_add[tc] = R.op("v", lambda h, tc=tc, par=par: h.tensor_tensor(
                            ACC[:, tc * E:(tc + 1) * E], psD[par][:],
                            ACC[:, tc * E:(tc + 1) * E], ALU.add), inc="cv")
                    v_prev_add = [v_add[14], v_add[15]]
                    for sl in range(ndslots):
                        v_wd_used[sl] = v_add[15]
                    v_add_final = list(v_add)
                    v_hall_last = v_hall
        if (not do_ffn or stage < 4) or "accf" in taps:
            R.barrier()
            v_add_final = [R.vals["cv"]] * 16
            v_hall_last = R.vals["cv"]
        if "accf" in taps:
            R.op("sy", lambda h: h.dma_start(tap_h["accf"], ACC[:]), inc="cs", n=16)

        if True:
            XI = [XNT[:, 0:2048].bitcast(F32), XNT[:, 2048:4096].bitcast(F32)]
            OU = [XNT[:, 4096:6144].bitcast(F32), XNT[:, 6144:8192].bitcast(F32)]
            v_st = [0] * 16
            v_ad = [0] * 16
            for tc in range(16):
                par = tc % 2
                if tc < 2:
                    R.wait("sy", "cv", v_hall_last)
                if tc >= 2:
                    R.wait("sy", "cv", v_ad[tc - 2])
                    R.wait("sy", "cs", v_st[tc - 2])
                v_xl = R.op("sy", lambda h, tc=tc, par=par: h.dma_start(
                    XI[par], x_h[tc * 128:(tc + 1) * 128, :]), inc="cs", n=16)
                R.wait("v", "cs", v_xl)
                R.wait("v", "cv", v_add_final[tc])
                v_ad[tc] = R.op("v", lambda h, tc=tc, par=par: h.tensor_tensor(
                    OU[par], ACC[:, tc * E:(tc + 1) * E], XI[par], ALU.add),
                    inc="cv")
                R.wait("sy", "cv", v_ad[tc])
                v_st[tc] = R.op("sy", lambda h, tc=tc, par=par: h.dma_start(
                    out_h[tc * 128:(tc + 1) * 128, :], OU[par]), inc="cs", n=16)
            R.wait("sy", "cs", R.vals["cs"])
            R.wait("g", "cs", R.vals["cs"])

        with nc.Block() as block:
            R.replay(block)
    return nc


def make_in_map(inputs, core_id):
    hc = host_constants()
    m = {
        "x": np.ascontiguousarray(np.asarray(inputs["L0_attn_emb"])[core_id]),
        "w1": np.asarray(inputs["L0_attn_W1"]),
        "w2": np.asarray(inputs["L0_attn_W2"]),
        "w3": np.asarray(inputs["L0_attn_W3"]),
        "wo": np.asarray(inputs["L0_attn_Wo"]),
        "rw": np.asarray(inputs["L0_ffn_router_w"]),
        "wg_r": np.asarray(inputs["L0_ffn_wg_r"]),
        "wu_r": np.asarray(inputs["L0_ffn_wu_r"]),
        "wd_r": np.asarray(inputs["L0_ffn_wd_r"]),
        "wg_s": np.asarray(inputs["L0_ffn_wg_s"]),
        "wu_s": np.asarray(inputs["L0_ffn_wu_s"]),
        "wd_s": np.asarray(inputs["L0_ffn_wd_s"]),
        "cost": hc["cost"], "sint": hc["sint"], "ident": hc["ident"],
        "identf": hc["identf"], "rmat": hc["rmat"], "consts": hc["consts"],
    }
    return m


def kernel(**inputs):
    nc = build()
    in_maps = [make_in_map(inputs, c) for c in range(8)]
    res = run_bass_kernel_spmd(nc, in_maps, core_ids=list(range(8)))
    import numpy as _np
    return _np.stack([_np.asarray(r["out"]) for r in res.results], axis=0)


# version 03
# speedup vs baseline: 1.3954x; 1.0130x over previous
import contextlib
import numpy as np

import concourse.bass as bass
import concourse.mybir as mybir
from concourse.bass_utils import run_bass_kernel_spmd

F32 = mybir.dt.float32
BF16 = mybir.dt.bfloat16
AF = mybir.ActivationFunctionType
ALU = mybir.AluOpType
AX = mybir.AxisListType

S_LEN = 2048
E = 1024
NH = 8
TH = 256

INV_SQRT8 = 0.35355339059327373
SOFTMAX_SHIFT = -30.0
CV = [1.0 / 1024.0, 1e-5, -1e30, -1.0, INV_SQRT8, SOFTMAX_SHIFT]

ENG_NAMES = {"t": "tensor", "sc": "scalar", "v": "vector", "g": "gpsimd", "sy": "sync"}


class Rec:
    def __init__(self, nc, stack):
        self.nc = nc
        self.q = {e: [] for e in ENG_NAMES}
        self.hwm = {}
        self.sems = {}
        self.vals = {}
        for name in ("cg", "cs", "ct", "cc", "cv", "cgp",
                     "et", "ec", "ev2", "eg"):
            self.sems[name] = stack.enter_context(nc.semaphore(name))
            self.vals[name] = 0

    EVT = {"sc": "ec", "v": "ev2", "g": "eg"}

    def op(self, eng, fn, inc=None, n=1):
        if inc is None:
            if eng == "t":
                self.q[eng].append({"fn": fn, "inc": None})
            else:
                self.q[eng].append(fn)
            return None
        sem = self.sems[inc]
        self.vals[inc] += n
        v = self.vals[inc]
        if eng == "t":
            if fn is None:
                last = self.q[eng][-1]
                assert isinstance(last, dict) and last["inc"] is None, \
                    "no tensor instruction to attach inc to"
                last["inc"] = (sem, n)
            else:
                self.q[eng].append({"fn": fn, "inc": (sem, n)})
            return v
        if inc in ("cs", "cg"):
            def wrapped(h, fn=fn, sem=sem, n=n):
                fn(h).then_inc(sem, n)

            self.q[eng].append(wrapped)
        else:
            en = self.EVT[eng]
            esem = self.sems[en]
            self.vals[en] += 1
            ev = self.vals[en]

            def wrapped(h, fn=fn, sem=sem, n=n, esem=esem, ev=ev):
                fn(h).then_inc(esem, 1)
                h.wait_ge(esem, ev)
                h.nop().then_inc(sem, n)

            self.q[eng].append(wrapped)
        return v

    def wait(self, eng, cname, v):
        if not v or v <= 0:
            return
        hw = self.hwm.setdefault((eng, cname), 0)
        if v <= hw:
            return
        self.hwm[(eng, cname)] = v
        sem = self.sems[cname]
        self.q[eng].append(lambda h, sem=sem, v=v: h.wait_ge(sem, v))

    def barrier(self):
        engs = ("t", "sc", "v", "g", "sy")
        for e in ("sc", "v", "g"):
            self.op(e, lambda h: h.nop(),
                    inc={"sc": "cc", "v": "cv", "g": "cgp"}[e])
        if self.q["t"] and isinstance(self.q["t"][-1], dict) \
                and self.q["t"][-1]["inc"] is None:
            self.op("t", None, inc="ct")
        snap = dict(self.vals)
        for e in engs:
            for cn in ("ct", "cc", "cv", "cgp", "cg", "cs"):
                if e == "t" and cn == "ct":
                    continue
                self.wait(e, cn, snap[cn])

    def replay(self, block):
        for eng, name in ENG_NAMES.items():
            ops = self.q[eng]

            def body(h, ops=ops):
                for f in ops:
                    if isinstance(f, dict):
                        instr = f["fn"](h)
                        if f["inc"] is not None:
                            instr.then_inc(*f["inc"])
                    else:
                        f(h)

            getattr(block, name)(body)


def host_constants():
    import ml_dtypes
    DH = 128
    d = {}
    inv_freq = 1.0 / 10000.0 ** (np.arange(0, DH, 2, dtype=np.float64) / DH)
    t = np.arange(S_LEN, dtype=np.float64)
    freqs = np.repeat(t[:, None] * inv_freq[None, :], 2, axis=1)
    COS = np.cos(freqs)
    SIN2 = np.sin(freqs)
    SIN2[:, 0::2] *= -1.0
    tau_to_sp = np.empty(S_LEN, np.int64)
    for c in range(8):
        for sl in range(TH):
            tau_to_sp[c * TH + sl] = 8 * sl + c
    d["cost"] = np.ascontiguousarray(COS[tau_to_sp, :].T).astype(ml_dtypes.bfloat16)
    d["sint"] = np.ascontiguousarray(SIN2[tau_to_sp, :].T).astype(ml_dtypes.bfloat16)
    d["ident"] = np.eye(128, dtype=ml_dtypes.bfloat16)
    d["identf"] = np.eye(128, dtype=np.float32)
    r = np.zeros((128, 128), np.float32)
    for i in range(128):
        r[i ^ 1, i] = 1.0
    d["rmat"] = r.astype(ml_dtypes.bfloat16)
    consts = np.zeros((128, 8), np.float32)
    for i, v in enumerate(CV):
        consts[:, i] = v
    d["consts"] = consts
    return d


def build(n_heads=NH, do_attn=True, do_ffn=True, taps=(), stage=9, p0_tr=True):
    nc = bass.Bass("TRN2", target_bir_lowering=False, detect_race_conditions=False)

    x_h = nc.dram_tensor("x", [S_LEN, E], F32, kind="ExternalInput").ap()
    w_h = {nm: nc.dram_tensor(nm, [E, E], F32, kind="ExternalInput").ap()
           for nm in ("w1", "w2", "w3", "wo")}
    rw_h = nc.dram_tensor("rw", [E, 16], F32, kind="ExternalInput").ap()
    wg_r_h = nc.dram_tensor("wg_r", [16, E, 256], F32, kind="ExternalInput").ap()
    wu_r_h = nc.dram_tensor("wu_r", [16, E, 256], F32, kind="ExternalInput").ap()
    wd_r_h = nc.dram_tensor("wd_r", [16, 256, E], F32, kind="ExternalInput").ap()
    wg_s_h = nc.dram_tensor("wg_s", [2, E, 512], F32, kind="ExternalInput").ap()
    wu_s_h = nc.dram_tensor("wu_s", [2, E, 512], F32, kind="ExternalInput").ap()
    wd_s_h = nc.dram_tensor("wd_s", [2, 512, E], F32, kind="ExternalInput").ap()
    cost_h = nc.dram_tensor("cost", [128, S_LEN], BF16, kind="ExternalInput").ap()
    sint_h = nc.dram_tensor("sint", [128, S_LEN], BF16, kind="ExternalInput").ap()
    ident_h = nc.dram_tensor("ident", [128, 128], BF16, kind="ExternalInput").ap()
    identf_h = nc.dram_tensor("identf", [128, 128], F32, kind="ExternalInput").ap()
    rmat_h = nc.dram_tensor("rmat", [128, 128], BF16, kind="ExternalInput").ap()
    consts_h = nc.dram_tensor("consts", [128, 8], F32, kind="ExternalInput").ap()
    out_h = nc.dram_tensor("out", [S_LEN, E], F32, kind="ExternalOutput").ap()
    combd_h = nc.dram_tensor("combd", [16, S_LEN], BF16, kind="Internal").ap()

    tap_h = {}

    def mktap(name, shape, dtype):
        if name in taps:
            tap_h[name] = nc.dram_tensor("tap_" + name, shape, dtype,
                                         kind="ExternalOutput").ap()

    mktap("xnt", [128, 8 * S_LEN], BF16)
    mktap("qr0", [128, S_LEN], BF16)
    mktap("kr0", [128, S_LEN], BF16)
    mktap("vt0", [128, S_LEN], BF16)
    mktap("ot0", [128, S_LEN], BF16)
    mktap("acc1", [128, 16 * E], F32)
    mktap("comb", [128, 256], F32)
    mktap("accf", [128, 16 * E], F32)

    top = contextlib.ExitStack()
    with top:
        R = Rec(nc, top)
        sb = lambda name, shape, dtype: top.enter_context(nc.sbuf_tensor(name, shape, dtype))

        ACC = sb("ACC", [128, 16 * E], F32)
        XNT = sb("XNT", [128, 8 * S_LEN], BF16)
        CONSTS = sb("CONSTS", [128, 8], F32)
        IDENT = sb("IDENT", [128, 128], BF16)
        IDENTF = sb("IDENTF", [128, 128], F32)
        RMAT = sb("RMAT", [128, 128], BF16)
        ONES = sb("ONESB", [128, 1], BF16)
        COMBF = sb("COMBF", [16, S_LEN], BF16)
        for i, v in enumerate(CV):
            nc.const_aps.register(F32, v, CONSTS[:, i:i + 1])

        v_c = R.op("sy", lambda h: h.dma_start(CONSTS[:], consts_h[:]), inc="cs", n=16)
        v_c = R.op("sy", lambda h: h.dma_start(IDENT[:], ident_h[:]), inc="cs", n=16)
        v_c = R.op("sy", lambda h: h.dma_start(IDENTF[:], identf_h[:]), inc="cs", n=16)
        v_c = R.op("sy", lambda h: h.dma_start(RMAT[:], rmat_h[:]), inc="cs", n=16)
        v_ones = R.op("g", lambda h: h.memset(ONES[:], 1.0), inc="cgp")
        for e in ("t", "sc", "v", "g", "sy"):
            R.wait(e, "cs", v_c)
            R.wait(e, "cgp", v_ones)

        p0 = contextlib.ExitStack()

        def phase_p0_ln():
            psb0 = lambda n_, s_, d_: p0.enter_context(nc.psum_tensor(n_, s_, d_))
            sb0 = lambda n_, s_, d_: p0.enter_context(nc.sbuf_tensor(n_, s_, d_))
            XIN = [sb0(f"XIN{i}", [128, E], F32) for i in range(2)]
            XNM = [sb0(f"XNM{i}", [128, E], BF16) for i in range(2)]
            JUNK = sb0("JUNK", [128, E], BF16)
            MB = sb0("MB", [128, 8], F32)
            psT = [psb0(f"psT0{i}", [128, 1024], F32) for i in range(2)]

            v_ln = [0] * 16
            v_ev = [0] * 16
            v_tr = [0] * 16
            for tc in range(16):
                par = tc % 2
                if tc >= 2:
                    R.wait("sy", "cv", v_ln[tc - 2])
                v_x = R.op("sy", lambda h, tc=tc, par=par: h.dma_start(
                    XIN[par][:], x_h[tc * 128:(tc + 1) * 128, :]), inc="cs", n=16)
                R.wait("v", "cs", v_x)
                R.op("v", lambda h, par=par: h.reduce_sum(MB[:, 6:7], XIN[par][:], AX.X),
                     inc="cv")
                vv = R.op("v", lambda h: h.tensor_scalar(
                    MB[:, 0:1], MB[:, 6:7], CV[0], None, ALU.mult), inc="cv")
                R.wait("sc", "cs", v_x)
                if tc >= 1:
                    R.wait("sc", "cv", v_ln[tc - 1])
                vs = R.op("sc", lambda h, par=par: h.activation(
                    JUNK[:], XIN[par][:], AF.Square, accum_out=MB[:, 1:2]), inc="cc")
                R.wait("v", "cc", vs)
                R.op("v", lambda h: h.tensor_scalar(
                    MB[:, 2:3], MB[:, 1:2], CV[0], None, ALU.mult), inc="cv")
                R.wait("v", "cv", vv)
                vv = R.op("v", lambda h: h.tensor_tensor(
                    MB[:, 6:7], MB[:, 0:1], MB[:, 0:1], ALU.mult), inc="cv")
                R.wait("v", "cv", vv)
                vv = R.op("v", lambda h: h.tensor_tensor(
                    MB[:, 2:3], MB[:, 2:3], MB[:, 6:7], ALU.subtract), inc="cv")
                R.wait("sc", "cv", vv)
                vs = R.op("sc", lambda h: h.activation(
                    MB[:, 3:4], MB[:, 2:3], AF.Sqrt, bias=CV[1]), inc="cc")
                R.wait("v", "cc", vs)
                vv = R.op("v", lambda h: h.reciprocal(MB[:, 4:5], MB[:, 3:4]), inc="cv")
                R.wait("v", "cv", vv)
                if tc >= 2:
                    R.wait("v", "ct", v_tr[tc - 2])
                v_ln[tc] = R.op("v", lambda h, tc=tc, par=par: h.tensor_scalar(
                    XNM[par][:], XIN[par][:], MB[:, 0:1], MB[:, 4:5],
                    ALU.subtract, ALU.mult), inc="cv")
                R.wait("t", "cv", v_ln[tc])
                if tc >= 2:
                    R.wait("t", "cc", v_ev[tc - 2])
                for ec in range(8):
                    R.op("t", lambda h, ec=ec, tc=tc, par=par: h.matmul(
                        psT[par][:, ec * 128:(ec + 1) * 128],
                        XNM[par][:, ec * 128:(ec + 1) * 128],
                        IDENT[:], start=True, stop=True))
                v_tr[tc] = R.op("t", None, inc="ct")
                R.wait("sc", "ct", v_tr[tc])
                v_ev[tc] = R.op("sc", lambda h, tc=tc, par=par: h.activation(
                    bass.AP(XNT, tc * 128, [[8 * S_LEN, 128], [S_LEN, 8], [1, 128]]),
                    bass.AP(psT[par], 0, [[1024, 128], [128, 8], [1, 128]]),
                    AF.Copy), inc="cc")
            if "xnt" in taps:
                R.wait("sy", "cc", v_ev[15])
                R.op("sy", lambda h: h.dma_start(tap_h["xnt"], XNT[:]), inc="cs", n=16)
            return v_ev

        pA = contextlib.ExitStack()
        sbA = lambda name, shape, dtype: pA.enter_context(nc.sbuf_tensor(name, shape, dtype))
        WOTP = sbA("WOTP", [128, 8 * E], BF16)
        W123T = {nm: sbA(nm + "T", [128, 8 * E], BF16) for nm in ("w1", "w2", "w3")}
        def phase_w():
          with contextlib.ExitStack() as pw:
            psbw = lambda n_, s_, d_: pw.enter_context(nc.psum_tensor(n_, s_, d_))
            sbw = lambda n_, s_, d_: pw.enter_context(nc.sbuf_tensor(n_, s_, d_))
            WNAT = [sbw("WNAT0", [128, 8 * E], BF16)]
            psTW = psbw("psTW", [128, 1024], F32)
            names = ["w1", "w2", "w3", "wo"]
            v_tdone = [0] * 4
            cc_ev = 0
            for i, nm in enumerate(names):
                par = 0
                if i >= 1:
                    R.wait("g", "ct", v_tdone[i - 1])
                v_wl = R.op("g", lambda h, nm=nm, par=par: h.dma_start(
                    WNAT[par][:].rearrange("p (pc e) -> p pc e", pc=8),
                    w_h[nm].rearrange("(pc p) e -> p pc e", p=128)),
                    inc="cg", n=16)
                R.wait("t", "cg", v_wl)
                for blk in range(8):
                    R.wait("t", "cc", cc_ev)
                    for j in range(8):
                        if nm != "wo":
                            R.op("t", lambda h, blk=blk, j=j, par=par: h.matmul(
                                psTW[:, j * 128:(j + 1) * 128],
                                WNAT[par][:, j * E + blk * 128: j * E + blk * 128 + 128],
                                IDENT[:], start=True, stop=True))
                        else:
                            R.op("t", lambda h, blk=blk, j=j, par=par: h.matmul(
                                psTW[:, j * 128:(j + 1) * 128],
                                WNAT[par][:, j * E + blk: j * E + blk + 1017:8],
                                IDENT[:], start=True, stop=True))
                    vt = R.op("t", None, inc="ct")
                    R.wait("sc", "ct", vt)
                    if nm != "wo":
                        cc_ev = R.op("sc", lambda h, nm=nm, blk=blk: h.activation(
                            W123T[nm][:, blk * E:(blk + 1) * E], psTW[:], AF.Copy),
                            inc="cc")
                    else:
                        cc_ev = R.op("sc", lambda h, blk=blk: h.activation(
                            WOTP[:, blk * E:(blk + 1) * E], psTW[:], AF.Copy), inc="cc")
                v_tdone[i] = R.vals["ct"]
            R.wait("t", "cc", cc_ev)
        def phase_router():
            R.wait("g", "ct", R.vals["ct"])
            with contextlib.ExitStack() as p2r:
                psbr = lambda n_, s_, d_: p2r.enter_context(nc.psum_tensor(n_, s_, d_))
                sbr = lambda n_, s_, d_: p2r.enter_context(nc.sbuf_tensor(n_, s_, d_))
                RWB = sbr("RWB", [128, 8 * 16], BF16)
                LGF = sbr("LGF", [16, S_LEN], F32)
                LGT = sbr("LGT", [128, 256], F32)
                COMBT = sbr("COMBT", [128, 256], F32)
                RS = sbr("RS", [128, 8], F32)
                L2 = sbr("L2", [128, 16], F32)
                MS1 = sbr("MS1", [128, 16], F32)
                MS2 = sbr("MS2", [128, 16], F32)
                psL = psbr("psL", [16, 1024], F32)
                psLT = psbr("psLT", [128, 256], F32)
                v_rw = R.op("g", lambda h: h.dma_start(
                    RWB[:].rearrange("e (ec x) -> e ec x", ec=8),
                    rw_h.rearrange("(ec e) x -> e ec x", e=128)), inc="cg", n=16)
                R.wait("t", "cg", v_rw)
                v_lg = 0
                for hf2 in range(2):
                    R.wait("t", "cc", v_lg)
                    for ec in range(8):
                        for hf in range(2):
                            R.op("t", lambda h, ec=ec, hf=hf, hf2=hf2: h.matmul(
                                psL[:, hf * 512:(hf + 1) * 512],
                                RWB[:, ec * 16:(ec + 1) * 16],
                                XNT[:, ec * S_LEN + hf2 * 1024 + hf * 512:
                                    ec * S_LEN + hf2 * 1024 + hf * 512 + 512],
                                start=(ec == 0), stop=(ec == 7)))
                    vt = R.op("t", None, inc="ct")
                    R.wait("sc", "ct", vt)
                    v_lg = R.op("sc", lambda h, hf2=hf2: h.activation(
                        LGF[:, hf2 * 1024:(hf2 + 1) * 1024], psL[:], AF.Copy),
                        inc="cc")
                R.wait("t", "cc", v_lg)
                for tc in range(16):
                    R.op("t", lambda h, tc=tc: h.matmul(
                        psLT[:, tc * 16:(tc + 1) * 16],
                        LGF[:, tc * 128:(tc + 1) * 128], IDENTF[0:16, 0:16], start=True, stop=True))
                vt = R.op("t", None, inc="ct")
                R.wait("sc", "ct", vt)
                v_lgt = R.op("sc", lambda h: h.activation(LGT[:], psLT[:], AF.Copy),
                             inc="cc")
                R.wait("v", "cc", v_lgt)
                for tc in range(16):
                    vv = R.op("v", lambda h, tc=tc: h.reduce_max(
                        RS[:, 0:1], LGT[:, tc * 16:(tc + 1) * 16], AX.X), inc="cv")
                    R.wait("v", "cv", vv)
                    vv = R.op("v", lambda h, tc=tc: h.tensor_scalar(
                        MS1[:], LGT[:, tc * 16:(tc + 1) * 16], RS[:, 0:1], None,
                        ALU.is_equal), inc="cv")
                    R.wait("v", "cv", vv)
                    vv = R.op("v", lambda h, tc=tc: h.scalar_tensor_tensor(
                        L2[:], MS1[:], CV[2], LGT[:, tc * 16:(tc + 1) * 16],
                        ALU.mult, ALU.add), inc="cv")
                    R.wait("v", "cv", vv)
                    vv = R.op("v", lambda h: h.reduce_max(RS[:, 1:2], L2[:], AX.X),
                              inc="cv")
                    R.wait("v", "cv", vv)
                    vv = R.op("v", lambda h: h.tensor_scalar(
                        MS2[:], L2[:], RS[:, 1:2], None, ALU.is_equal), inc="cv")
                    R.wait("v", "cv", vv)
                    vv = R.op("v", lambda h: h.tensor_tensor(
                        RS[:, 2:3], RS[:, 1:2], RS[:, 0:1], ALU.subtract), inc="cv")
                    R.wait("sc", "cv", vv)
                    vs = R.op("sc", lambda h: h.activation(
                        RS[:, 4:5], RS[:, 2:3], AF.Sigmoid), inc="cc")
                    R.wait("sc", "cc", vs)
                    vs = R.op("sc", lambda h: h.activation(
                        RS[:, 3:4], RS[:, 2:3], AF.Sigmoid, scale=CV[3]), inc="cc")
                    R.wait("v", "cc", vs)
                    vv = R.op("v", lambda h, tc=tc: h.tensor_scalar(
                        COMBT[:, tc * 16:(tc + 1) * 16], MS1[:], RS[:, 3:4], None,
                        ALU.mult), inc="cv")
                    R.wait("v", "cv", vv)
                    R.op("v", lambda h, tc=tc: h.scalar_tensor_tensor(
                        COMBT[:, tc * 16:(tc + 1) * 16], MS2[:], RS[:, 4:5],
                        COMBT[:, tc * 16:(tc + 1) * 16], ALU.mult, ALU.add), inc="cv")
                v_combt = R.vals["cv"]
                if "comb" in taps:
                    R.wait("sy", "cv", v_combt)
                    R.op("sy", lambda h: h.dma_start(tap_h["comb"], COMBT[:]),
                         inc="cs", n=16)
                R.wait("t", "cv", v_combt)
                v_combf = 0
                for r2 in range(2):
                    R.wait("t", "cc", v_combf)
                    for j in range(8):
                        tc = r2 * 8 + j
                        R.op("t", lambda h, tc=tc, j=j: h.matmul(
                            psL[:, j * 128:(j + 1) * 128],
                            COMBT[:, tc * 16:(tc + 1) * 16], IDENTF[:],
                            start=True, stop=True))
                    vt = R.op("t", None, inc="ct")
                    R.wait("sc", "ct", vt)
                    v_combf = R.op("sc", lambda h, r2=r2: h.activation(
                        COMBF[:, r2 * 1024:(r2 + 1) * 1024], psL[:], AF.Copy),
                        inc="cc")
                R.wait("g", "cc", v_combf)
                v_combd = R.op("g", lambda h: h.dma_start(combd_h, COMBF[:]),
                               inc="cg", n=16)
            return v_combd
        if stage >= 1:
            _v_ev_saved = phase_p0_ln()
        if stage >= 2:
            phase_w()
        _v_combd = phase_router() if (do_ffn and stage >= 4) else 0
        if stage < 1:
            vz = R.op("v", lambda h: h.memset(XNT[:], 0.0), inc="cv")
            R.wait("t", "cv", vz)
        p0.close()
        R.barrier()

        if do_attn and stage >= 3:
            COST = sbA("COST", [128, S_LEN], BF16)
            SINT = sbA("SINT", [128, S_LEN], BF16)
            R.op("sy", lambda h: h.dma_start(COST[:], cost_h[:]), inc="cs", n=16)
            v_tab = R.op("sy", lambda h: h.dma_start(SINT[:], sint_h[:]), inc="cs", n=16)

            QHB = sbA("QHB", [128, S_LEN], BF16)
            KHB = sbA("KHB", [128, S_LEN], BF16)
            QR = sbA("QR", [128, S_LEN], BF16)
            KR = sbA("KR", [128, S_LEN], BF16)
            VT = sbA("VT", [128, S_LEN], BF16)
            T1 = sbA("T1", [128, 512], BF16)
            T2 = sbA("T2", [128, 512], BF16)
            PT = [sbA(f"PT{i}", [128, 1024], BF16) for i in range(3)]
            OSB = KHB
            OT = QHB
            ASB = OSB
            RSROW = sbA("RSROW", [1, S_LEN], BF16)
            RECT = sbA("RECT", [128, 16], F32)

            v_rope_prev = 0
            v_accev_prev = 0
            v_exp_carry = [0, 0]
            v_pv_carry3 = [0, 0, 0]
            v_oe = 0
            for head in range(n_heads):
                with contextlib.ExitStack() as p1a:
                    psba = lambda n_, s_, d_: p1a.enter_context(nc.psum_tensor(n_ + f"_{head}", s_, d_))
                    psQK = psba("psQK", [128, S_LEN], F32)
                    psV = psba("psV", [128, 1024], F32)
                    psSH = psba("psSH", [128, 1024], F32)

                    def proj_qk(wt, dst, head=head):
                        for pc in range(8):
                            for ec in range(8):
                                R.op("t", lambda h, pc=pc, ec=ec, wt=wt, head=head: h.matmul(
                                    psQK[:, pc * TH:(pc + 1) * TH],
                                    W123T[wt][:, ec * E + pc * 128: ec * E + pc * 128 + 128],
                                    XNT[:, ec * S_LEN + head * TH: ec * S_LEN + head * TH + TH],
                                    start=(ec == 0), stop=(ec == 7)))
                        vt = R.op("t", None, inc="ct")
                        R.wait("sc", "ct", vt)
                        return R.op("sc", lambda h, dst=dst: h.activation(
                            dst[:], psQK[:], AF.Copy), inc="cc")

                    R.wait("sc", "cv", v_rope_prev)
                    v_q = proj_qk("w1", QHB)
                    R.wait("t", "cc", v_q)
                    v_k = proj_qk("w2", KHB)

                    v_vev = 0
                    for blk in range(2):
                        R.wait("t", "cc", v_vev)
                        for ec in range(8):
                            for hf in range(2):
                                R.op("t", lambda h, ec=ec, hf=hf, blk=blk, head=head: h.matmul(
                                    psV[:, hf * 512:(hf + 1) * 512],
                                    XNT[:, ec * S_LEN + head * TH + blk * 128:
                                        ec * S_LEN + head * TH + blk * 128 + 128],
                                    W123T["w3"][:, ec * E + hf * 512: ec * E + hf * 512 + 512],
                                    start=(ec == 0), stop=(ec == 7)))
                        vt = R.op("t", None, inc="ct")
                        R.wait("sc", "ct", vt)
                        v_vev = R.op("sc", lambda h, blk=blk: h.activation(
                            bass.AP(VT, blk * 128, [[S_LEN, 128], [TH, 8], [1, 128]]),
                            bass.AP(psV, 0, [[1024, 128], [128, 8], [1, 128]]),
                            AF.Copy), inc="cc")

                    def rope(src, dst, v_src):
                        vq = 0
                        for j in range(4):
                            par = j % 2
                            R.wait("t", "cc", v_src)
                            R.wait("t", "cv", vq)
                            vt = R.op("t", lambda h, j=j, par=par, src=src: h.matmul(
                                psSH[:, par * 512:(par + 1) * 512], RMAT[:],
                                src[:, j * 512:(j + 1) * 512], start=True, stop=True),
                                inc="ct")
                            R.wait("v", "cs", v_tab)
                            R.wait("v", "cc", v_src)
                            R.op("v", lambda h, j=j, src=src: h.tensor_tensor(
                                T1[:], src[:, j * 512:(j + 1) * 512],
                                COST[:, j * 512:(j + 1) * 512], ALU.mult), inc="cv")
                            R.wait("v", "ct", vt)
                            vv = R.op("v", lambda h, j=j, par=par: h.tensor_tensor(
                                T2[:], psSH[:, par * 512:(par + 1) * 512],
                                SINT[:, j * 512:(j + 1) * 512], ALU.mult), inc="cv")
                            R.wait("v", "cv", vv)
                            vq = R.op("v", lambda h, j=j, dst=dst: h.tensor_tensor(
                                dst[:, j * 512:(j + 1) * 512], T1[:], T2[:], ALU.add),
                                inc="cv")
                        return vq

                    v_qr = rope(QHB, QR, v_q)
                    v_kr = rope(KHB, KR, v_k)
                    v_rope_prev = v_kr

                if head == 0 and "qr0" in taps:
                    R.wait("sy", "cv", v_qr)
                    R.op("sy", lambda h: h.dma_start(tap_h["qr0"], QR[:]), inc="cs", n=16)
                    R.wait("sy", "cv", v_kr)
                    R.op("sy", lambda h: h.dma_start(tap_h["kr0"], KR[:]), inc="cs", n=16)
                if head == 0 and "vt0" in taps:
                    R.wait("sy", "cc", R.vals["cc"])
                    R.op("sy", lambda h: h.dma_start(tap_h["vt0"], VT[:]), inc="cs", n=16)

                with contextlib.ExitStack() as p1b:
                    psbb = lambda n_, s_, d_: p1b.enter_context(nc.psum_tensor(n_ + f"_{head}", s_, d_))
                    psS = [psbb(f"psS{i}", [128, 1024], F32) for i in range(2)]
                    psO = psbb("psO", [128, 1024], F32)
                    psR = psbb("psR", [1, 1024], F32)
                    R.wait("t", "cv", v_qr)
                    R.wait("t", "cv", v_kr)
                    v_exp = [0] * 32
                    v_pv = [0] * 32
                    v_s = [0] * 32
                    for kk in range(34):
                        if kk < 32:
                            qh, kt = kk // 16, kk % 16
                            par = kk % 2
                            R.wait("t", "cc",
                                   v_exp[kk - 2] if kk >= 2 else v_exp_carry[kk])
                            for hf in range(2):
                                R.op("t", lambda h, kt=kt, par=par, qh=qh, hf=hf: h.matmul(
                                    psS[par][:, hf * 512:(hf + 1) * 512],
                                    KR[:, kt * 128:(kt + 1) * 128],
                                    QR[:, qh * 1024 + hf * 512: qh * 1024 + hf * 512 + 512],
                                    start=True, stop=True))
                            v_s[kk] = R.op("t", None, inc="ct")
                            R.wait("sc", "ct", v_s[kk])
                            R.wait("sc", "ct",
                                   v_pv[kk - 3] if kk >= 3 else v_pv_carry3[kk])
                            v_exp[kk] = R.op("sc", lambda h, par3=kk % 3, par=par: h.activation(
                                PT[par3][:], psS[par][:], AF.Exp,
                                bias=SOFTMAX_SHIFT, scale=INV_SQRT8), inc="cc")
                        if kk >= 2:
                            pk = kk - 2
                            qh, kt = pk // 16, pk % 16
                            par = pk % 2
                            R.wait("t", "cc", v_exp[pk])
                            if kt == 0:
                                R.wait("t", "cc", v_oe)
                            par3 = pk % 3
                            for hf in range(2):
                                R.op("t", lambda h, kt=kt, par3=par3, hf=hf: h.matmul(
                                    psO[:, hf * 512:(hf + 1) * 512],
                                    VT[:, kt * 128:(kt + 1) * 128],
                                    PT[par3][:, hf * 512:(hf + 1) * 512],
                                    start=(kt == 0), stop=(kt == 15),
                                    skip_group_check=True))
                            for hf in range(2):
                                R.op("t", lambda h, kt=kt, par3=par3, hf=hf: h.matmul(
                                    psR[:, hf * 512:(hf + 1) * 512],
                                    ONES[:],
                                    PT[par3][:, hf * 512:(hf + 1) * 512],
                                    start=(kt == 0), stop=(kt == 15),
                                    skip_group_check=True))
                            v_pv[pk] = R.op("t", None, inc="ct")
                            if kt == 15:
                                R.wait("sc", "ct", v_pv[pk])
                                R.op("sc", lambda h, qh=qh: h.activation(
                                    RSROW[0:1, qh * 1024:(qh + 1) * 1024], psR[0:1, :],
                                    AF.Copy), inc="cc")
                                v_oe = R.op("sc", lambda h, qh=qh: h.activation(
                                    OSB[:, qh * 1024:(qh + 1) * 1024], psO[:],
                                    AF.Copy), inc="cc")
                    v_exp_carry = [v_exp[30], v_exp[31]]
                    v_pv_carry3 = [v_pv[31]] * 3

                with contextlib.ExitStack() as p1c:
                    psbc = lambda n_, s_, d_: p1c.enter_context(nc.psum_tensor(n_ + f"_{head}", s_, d_))
                    psAm = psbc("psAm", [128, 1024], F32)
                    psTT = psbc("psTT", [128, 1024], F32)
                    R.wait("t", "cc", v_oe)
                    cc_ev = v_accev_prev
                    for j in range(16):
                        R.op("t", lambda h, j=j: h.matmul(
                            psAm[:, j:j + 1],
                            RSROW[0:1, j * 128:(j + 1) * 128], IDENT[0:1, 0:1], start=True, stop=True))
                    vt_rs = R.op("t", None, inc="ct")
                    R.wait("v", "ct", vt_rs)
                    v_rec = R.op("v", lambda h: h.reciprocal(
                        RECT[:], psAm[:, 0:16]), inc="cv")
                    for r in range(2):
                        R.wait("t", "cc", cc_ev)
                        for j in range(8):
                            R.op("t", lambda h, r=r, j=j: h.matmul(
                                psTT[:, j * 128:(j + 1) * 128],
                                OSB[:, (r * 8 + j) * 128:(r * 8 + j) * 128 + 128],
                                IDENT[:], start=True, stop=True))
                        vt = R.op("t", None, inc="ct")
                        R.wait("sc", "ct", vt)
                        R.wait("sc", "cv", v_rec)
                        for j in range(8):
                            kt2 = r * 8 + j
                            cc_ev = R.op("sc", lambda h, kt2=kt2, j=j: h.activation(
                                OT[:, kt2 * 128:(kt2 + 1) * 128],
                                psTT[:, j * 128:(j + 1) * 128], AF.Copy,
                                scale=RECT[:, kt2:kt2 + 1]), inc="cc")
                    if head == 0 and "ot0" in taps:
                        R.wait("sy", "cc", cc_ev)
                        R.op("sy", lambda h: h.dma_start(tap_h["ot0"], OT[:]), inc="cs",
                             n=16)
                    for u in range(2):
                        R.wait("t", "cc", cc_ev)
                        R.wait("t", "cv", v_rec)
                        for i, kt in enumerate(range(u, 16, 2)):
                            for hf in range(2):
                                R.op("t", lambda h, kt=kt, i=i, hf=hf: h.matmul(
                                    psAm[:, hf * 512:(hf + 1) * 512],
                                    OT[:, kt * 128:(kt + 1) * 128],
                                    WOTP[:, (kt // 2) * E + hf * 512:
                                         (kt // 2) * E + hf * 512 + 512],
                                    start=(i == 0), stop=(i == 7)))
                        vt = R.op("t", None, inc="ct")
                        R.wait("sc", "ct", vt)
                        v_asb = R.op("sc", lambda h: h.activation(
                            ASB[:, 0:1024], psAm[:], AF.Copy), inc="cc")
                        R.wait("t", "cc", v_asb)
                        for cc_i in range(8):
                            R.op("t", lambda h, cc_i=cc_i: h.matmul(
                                psTT[:, cc_i * 128:(cc_i + 1) * 128],
                                ASB[:, cc_i * 128:(cc_i + 1) * 128], IDENT[:], start=True, stop=True))
                        vt = R.op("t", None, inc="ct")
                        R.wait("sc", "ct", vt)
                        cc_ev = R.op("sc", lambda h, u=u, head=head: h.activation(
                            bass.AP(ACC, (u * 8) * E + head * 128,
                                    [[16 * E, 128], [E, 8], [1, 128]]),
                            bass.AP(psTT, 0, [[1024, 128], [128, 8], [1, 128]]),
                            AF.Copy), inc="cc")
                    v_accev_prev = cc_ev
            if "acc1" in taps:
                R.wait("sy", "cc", R.vals["cc"])
                R.op("sy", lambda h: h.dma_start(tap_h["acc1"], ACC[:]), inc="cs", n=16)
        if not (do_attn and stage >= 3):
            vz = R.op("v", lambda h: h.memset(ACC[:], 0.0), inc="cv")
            R.wait("t", "cv", vz)
        R.barrier()

        pA.close()

        if do_ffn and stage >= 4:
            pass
            R.barrier()

            with contextlib.ExitStack() as p2e:
                psbe = lambda n_, s_, d_: p2e.enter_context(nc.psum_tensor(n_, s_, d_))
                sbe = lambda n_, s_, d_: p2e.enter_context(nc.sbuf_tensor(n_, s_, d_))
                psG1 = psbe("psG1", [128, 1024], F32)
                psG2 = psbe("psG2", [128, 1024], F32)
                psD = [psbe(f"psD{i}", [128, 1024], F32) for i in range(2)]
                HARE = sbe("HARE", [128, 8 * S_LEN], BF16)
                SG = sbe("SG", [128, S_LEN], BF16)
                TFFN = sbe("TFFN", [128, S_LEN], BF16)
                WGA = sbe("WGA", [128, 8 * 512], BF16)
                WUA = sbe("WUA", [128, 8 * 512], BF16)
                WDA = sbe("WDA", [128, 8 * 1024], BF16)
                CB = [sbe(f"CB{i}", [128, S_LEN], BF16) for i in range(2)]

                groups = [("r", [4 * g + i for i in range(4)]) for g in range(4)]
                groups.append(("s", [0, 1]))
                v_wg_used = {}
                v_wd_used = {}
                v_cb_used = {}
                v_prev_add = [0, 0]
                v_silu_prev = 0
                v_h_prev = 0
                for gi, (gkind, gl) in enumerate(groups):
                    shared = gkind == "s"
                    nmc = 4 if shared else 2
                    mid = 512 if shared else 256
                    ndslots = 2 if shared else 4
                    v_h = []
                    v_ld_all = 0
                    for li, e in enumerate(gl):
                        sl_g = 0 if shared else li % 2
                        sl_d = li % ndslots
                        wg_off = sl_g * 8 * 256 if not shared else 0
                        wd_off = sl_d * (4 * E if shared else 2 * E)
                        R.wait("g", "cv", v_wg_used.get(sl_g, 0))
                        src_g = (wg_s_h if shared else wg_r_h)[e].rearrange(
                            "(ec e2) m -> e2 ec m", e2=128)
                        v_lg_ = R.op("g", lambda h, src=src_g, off=wg_off, mid=mid:
                                     h.dma_start(WGA[:, off:off + 8 * mid].rearrange(
                                         "p (ec m) -> p ec m", ec=8), src),
                                     inc="cg", n=16)
                        src_u = (wu_s_h if shared else wu_r_h)[e].rearrange(
                            "(ec e2) m -> e2 ec m", e2=128)
                        v_lu_ = R.op("g", lambda h, src=src_u, off=wg_off, mid=mid:
                                     h.dma_start(WUA[:, off:off + 8 * mid].rearrange(
                                         "p (ec m) -> p ec m", ec=8), src),
                                     inc="cg", n=16)
                        R.wait("g", "cv", v_wd_used.get(sl_d, 0))
                        src_d = (wd_s_h if shared else wd_r_h)[e].rearrange(
                            "(mc m) x -> m mc x", m=128)
                        v_ld_all = R.op("g", lambda h, src=src_d, off=wd_off, nmc=nmc:
                                        h.dma_start(WDA[:, off:off + nmc * E].rearrange(
                                            "p (mc x) -> p mc x", mc=nmc), src),
                                        inc="cg", n=16)
                        v_cb_ = 0
                        if not shared:
                            R.wait("g", "cv", v_cb_used.get(sl_g, 0))
                            v_cb_ = R.op("g", lambda h, e=e, sl=sl_g: h.dma_start(
                                CB[sl][:], combd_h[e:e + 1, :].partition_broadcast(128)),
                                inc="cg", n=16)
                        for mc in range(nmc):
                            for th in range(2):
                                R.wait("t", "cg", v_lg_)
                                R.wait("t", "cc", v_silu_prev)
                                for ec in range(8):
                                    for hf in range(2):
                                        R.op("t", lambda h, ec=ec, hf=hf, mc=mc, th=th,
                                             off=wg_off, mid=mid: h.matmul(
                                            psG1[:, hf * 512:(hf + 1) * 512],
                                            WGA[:, off + ec * mid + mc * 128:
                                                off + ec * mid + mc * 128 + 128],
                                            XNT[:, ec * S_LEN + th * 1024 + hf * 512:
                                                ec * S_LEN + th * 1024 + hf * 512 + 512],
                                            start=(ec == 0), stop=(ec == 7)))
                                vt_g = R.op("t", None, inc="ct")
                                R.wait("sc", "ct", vt_g)
                                R.wait("sc", "cv", v_h_prev)
                                v_silu_prev = R.op("sc", lambda h, th=th: h.activation(
                                    SG[:, th * 1024:(th + 1) * 1024], psG1[:],
                                    AF.Silu), inc="cc")
                                R.wait("t", "cg", v_lu_)
                                R.wait("t", "cv", v_h_prev)
                                for ec in range(8):
                                    for hf in range(2):
                                        R.op("t", lambda h, ec=ec, hf=hf, mc=mc, th=th,
                                             off=wg_off, mid=mid: h.matmul(
                                            psG2[:, hf * 512:(hf + 1) * 512],
                                            WUA[:, off + ec * mid + mc * 128:
                                                off + ec * mid + mc * 128 + 128],
                                            XNT[:, ec * S_LEN + th * 1024 + hf * 512:
                                                ec * S_LEN + th * 1024 + hf * 512 + 512],
                                            start=(ec == 0), stop=(ec == 7)))
                                vt_u = R.op("t", None, inc="ct")
                                R.wait("v", "ct", vt_u)
                                R.wait("v", "cc", v_silu_prev)
                                hslot = li * nmc + mc
                                base = hslot * S_LEN + th * 1024
                                if not shared:
                                    R.wait("v", "cg", v_cb_)
                                    vv = R.op("v", lambda h, th=th: h.tensor_tensor(
                                        TFFN[:, 0:1024], SG[:, th * 1024:(th + 1) * 1024],
                                        psG2[:], ALU.mult), inc="cv")
                                    R.wait("v", "cv", vv)
                                    vv = R.op("v", lambda h, base=base, th=th, sl=sl_g:
                                              h.tensor_tensor(
                                        HARE[:, base:base + 1024], TFFN[:, 0:1024],
                                        CB[sl][:, th * 1024:(th + 1) * 1024],
                                        ALU.mult), inc="cv")
                                else:
                                    vv = R.op("v", lambda h, base=base, th=th:
                                              h.tensor_tensor(
                                        HARE[:, base:base + 1024],
                                        SG[:, th * 1024:(th + 1) * 1024],
                                        psG2[:], ALU.mult), inc="cv")
                                v_h_prev = vv
                                v_h.append(vv)
                        v_wg_used[sl_g] = v_h_prev
                        if not shared:
                            v_cb_used[sl_g] = v_h_prev
                    v_hall = max(v_h)
                    R.wait("t", "cv", v_hall)
                    R.wait("t", "cg", v_ld_all)
                    v_add = [0] * 16
                    for tc in range(16):
                        par = tc % 2
                        prev = v_add[tc - 2] if tc >= 2 else v_prev_add[par]
                        R.wait("t", "cv", prev)
                        first = True
                        nexp = len(gl)
                        for li in range(nexp):
                            sl_d = li % ndslots
                            wd_off2 = sl_d * (4 * E if shared else 2 * E)
                            for mc in range(nmc):
                                hslot = li * nmc + mc
                                last = (li == nexp - 1 and mc == nmc - 1)
                                for hf in range(2):
                                    R.op("t", lambda h, tc=tc, par=par, hslot=hslot,
                                         off=wd_off2, mc=mc, first=first, hf=hf, last=last:
                                         h.matmul(
                                        psD[par][:, hf * 512:(hf + 1) * 512],
                                        HARE[:, hslot * S_LEN + tc * 128:
                                             hslot * S_LEN + tc * 128 + 128],
                                        WDA[:, off + mc * E + hf * 512:
                                            off + mc * E + hf * 512 + 512],
                                        start=first, stop=last,
                                        skip_group_check=True))
                                first = False
                        vt = R.op("t", None, inc="ct")
                        R.wait("v", "ct", vt)
                        v_add[tc] = R.op("v", lambda h, tc=tc, par=par: h.tensor_tensor(
                            ACC[:, tc * E:(tc + 1) * E], psD[par][:],
                            ACC[:, tc * E:(tc + 1) * E], ALU.add), inc="cv")
                    v_prev_add = [v_add[14], v_add[15]]
                    for sl in range(ndslots):
                        v_wd_used[sl] = v_add[15]
                    v_add_final = list(v_add)
                    v_hall_last = v_hall
        if (not do_ffn or stage < 4) or "accf" in taps:
            R.barrier()
            v_add_final = [R.vals["cv"]] * 16
            v_hall_last = R.vals["cv"]
        if "accf" in taps:
            R.op("sy", lambda h: h.dma_start(tap_h["accf"], ACC[:]), inc="cs", n=16)

        if True:
            XI = [XNT[:, 0:2048].bitcast(F32), XNT[:, 2048:4096].bitcast(F32)]
            OU = [XNT[:, 4096:6144].bitcast(F32), XNT[:, 6144:8192].bitcast(F32)]
            v_st = [0] * 16
            v_ad = [0] * 16
            for tc in range(16):
                par = tc % 2
                if tc < 2:
                    R.wait("sy", "cv", v_hall_last)
                if tc >= 2:
                    R.wait("sy", "cv", v_ad[tc - 2])
                    R.wait("sy", "cs", v_st[tc - 2])
                v_xl = R.op("sy", lambda h, tc=tc, par=par: h.dma_start(
                    XI[par], x_h[tc * 128:(tc + 1) * 128, :]), inc="cs", n=16)
                R.wait("v", "cs", v_xl)
                R.wait("v", "cv", v_add_final[tc])
                v_ad[tc] = R.op("v", lambda h, tc=tc, par=par: h.tensor_tensor(
                    OU[par], ACC[:, tc * E:(tc + 1) * E], XI[par], ALU.add),
                    inc="cv")
                R.wait("sc", "cv", v_ad[tc])
                v_st[tc] = R.op("sc", lambda h, tc=tc, par=par: h.dma_start(
                    out_h[tc * 128:(tc + 1) * 128, :], OU[par]), inc="cs", n=16)
            R.wait("sy", "cs", R.vals["cs"])
            R.wait("g", "cs", R.vals["cs"])

        with nc.Block() as block:
            R.replay(block)
    return nc


def make_in_map(inputs, core_id):
    hc = host_constants()
    m = {
        "x": np.ascontiguousarray(np.asarray(inputs["L0_attn_emb"])[core_id]),
        "w1": np.asarray(inputs["L0_attn_W1"]),
        "w2": np.asarray(inputs["L0_attn_W2"]),
        "w3": np.asarray(inputs["L0_attn_W3"]),
        "wo": np.asarray(inputs["L0_attn_Wo"]),
        "rw": np.asarray(inputs["L0_ffn_router_w"]),
        "wg_r": np.asarray(inputs["L0_ffn_wg_r"]),
        "wu_r": np.asarray(inputs["L0_ffn_wu_r"]),
        "wd_r": np.asarray(inputs["L0_ffn_wd_r"]),
        "wg_s": np.asarray(inputs["L0_ffn_wg_s"]),
        "wu_s": np.asarray(inputs["L0_ffn_wu_s"]),
        "wd_s": np.asarray(inputs["L0_ffn_wd_s"]),
        "cost": hc["cost"], "sint": hc["sint"], "ident": hc["ident"],
        "identf": hc["identf"], "rmat": hc["rmat"], "consts": hc["consts"],
    }
    return m


def kernel(**inputs):
    nc = build()
    in_maps = [make_in_map(inputs, c) for c in range(8)]
    res = run_bass_kernel_spmd(nc, in_maps, core_ids=list(range(8)))
    import numpy as _np
    return _np.stack([_np.asarray(r["out"]) for r in res.results], axis=0)


# version 08
# speedup vs baseline: 1.4494x; 1.0387x over previous
import contextlib
import numpy as np

import concourse.bass as bass
import concourse.mybir as mybir
from concourse.bass_utils import run_bass_kernel_spmd

F32 = mybir.dt.float32
BF16 = mybir.dt.bfloat16
AF = mybir.ActivationFunctionType
ALU = mybir.AluOpType
AX = mybir.AxisListType

S_LEN = 2048
E = 1024
NH = 8
TH = 256

INV_SQRT8 = 0.35355339059327373
SOFTMAX_SHIFT = -30.0
CV = [1.0 / 1024.0, 1e-5, -1e30, -1.0, INV_SQRT8, SOFTMAX_SHIFT]

ENG_NAMES = {"t": "tensor", "sc": "scalar", "v": "vector", "g": "gpsimd", "sy": "sync"}


class Rec:
    def __init__(self, nc, stack):
        self.nc = nc
        self.q = {e: [] for e in ENG_NAMES}
        self.hwm = {}
        self.sems = {}
        self.vals = {}
        for name in ("cg", "cs", "ct", "cc", "cv", "cgp",
                     "et", "ec", "ev2", "eg"):
            self.sems[name] = stack.enter_context(nc.semaphore(name))
            self.vals[name] = 0

    EVT = {"sc": "ec", "v": "ev2", "g": "eg"}

    def op(self, eng, fn, inc=None, n=1):
        if inc is None:
            if eng == "t":
                self.q[eng].append({"fn": fn, "inc": None})
            else:
                self.q[eng].append(fn)
            return None
        sem = self.sems[inc]
        self.vals[inc] += n
        v = self.vals[inc]
        if eng == "t":
            if fn is None:
                last = self.q[eng][-1]
                assert isinstance(last, dict) and last["inc"] is None, \
                    "no tensor instruction to attach inc to"
                last["inc"] = (sem, n)
            else:
                self.q[eng].append({"fn": fn, "inc": (sem, n)})
            return v
        if inc in ("cs", "cg"):
            def wrapped(h, fn=fn, sem=sem, n=n):
                fn(h).then_inc(sem, n)

            self.q[eng].append(wrapped)
        else:
            en = self.EVT[eng]
            esem = self.sems[en]
            self.vals[en] += 1
            ev = self.vals[en]

            def wrapped(h, fn=fn, sem=sem, n=n, esem=esem, ev=ev):
                fn(h).then_inc(esem, 1)
                h.wait_ge(esem, ev)
                h.nop().then_inc(sem, n)

            self.q[eng].append(wrapped)
        return v

    def wait(self, eng, cname, v):
        if not v or v <= 0:
            return
        hw = self.hwm.setdefault((eng, cname), 0)
        if v <= hw:
            return
        self.hwm[(eng, cname)] = v
        sem = self.sems[cname]
        self.q[eng].append(lambda h, sem=sem, v=v: h.wait_ge(sem, v))

    def barrier(self):
        engs = ("t", "sc", "v", "g", "sy")
        for e in ("sc", "v", "g"):
            self.op(e, lambda h: h.nop(),
                    inc={"sc": "cc", "v": "cv", "g": "cgp"}[e])
        if self.q["t"] and isinstance(self.q["t"][-1], dict) \
                and self.q["t"][-1]["inc"] is None:
            self.op("t", None, inc="ct")
        snap = dict(self.vals)
        for e in engs:
            for cn in ("ct", "cc", "cv", "cgp", "cg", "cs"):
                if e == "t" and cn == "ct":
                    continue
                self.wait(e, cn, snap[cn])

    def replay(self, block):
        for eng, name in ENG_NAMES.items():
            ops = self.q[eng]

            def body(h, ops=ops):
                for f in ops:
                    if isinstance(f, dict):
                        instr = f["fn"](h)
                        if f["inc"] is not None:
                            instr.then_inc(*f["inc"])
                    else:
                        f(h)

            getattr(block, name)(body)


def host_constants():
    import ml_dtypes
    DH = 128
    d = {}
    inv_freq = 1.0 / 10000.0 ** (np.arange(0, DH, 2, dtype=np.float64) / DH)
    t = np.arange(S_LEN, dtype=np.float64)
    freqs = np.repeat(t[:, None] * inv_freq[None, :], 2, axis=1)
    COS = np.cos(freqs)
    SIN2 = np.sin(freqs)
    SIN2[:, 0::2] *= -1.0
    tau_to_sp = np.empty(S_LEN, np.int64)
    for c in range(8):
        for sl in range(TH):
            tau_to_sp[c * TH + sl] = 8 * sl + c
    d["cost"] = np.ascontiguousarray(COS[tau_to_sp, :].T).astype(ml_dtypes.bfloat16)
    d["sint"] = np.ascontiguousarray(SIN2[tau_to_sp, :].T).astype(ml_dtypes.bfloat16)
    d["ident"] = np.eye(128, dtype=ml_dtypes.bfloat16)
    d["identf"] = np.eye(128, dtype=np.float32)
    r = np.zeros((128, 128), np.float32)
    for i in range(128):
        r[i ^ 1, i] = 1.0
    d["rmat"] = r.astype(ml_dtypes.bfloat16)
    consts = np.zeros((128, 8), np.float32)
    for i, v in enumerate(CV):
        consts[:, i] = v
    d["consts"] = consts
    return d


def build(n_heads=NH, do_attn=True, do_ffn=True, taps=(), stage=9, p0_tr=True):
    nc = bass.Bass("TRN2", target_bir_lowering=False, detect_race_conditions=False)

    x_h = nc.dram_tensor("x", [S_LEN, E], F32, kind="ExternalInput").ap()
    w_h = {nm: nc.dram_tensor(nm, [E, E], F32, kind="ExternalInput").ap()
           for nm in ("w1", "w2", "w3", "wo")}
    rw_h = nc.dram_tensor("rw", [E, 16], F32, kind="ExternalInput").ap()
    wg_r_h = nc.dram_tensor("wg_r", [16, E, 256], F32, kind="ExternalInput").ap()
    wu_r_h = nc.dram_tensor("wu_r", [16, E, 256], F32, kind="ExternalInput").ap()
    wd_r_h = nc.dram_tensor("wd_r", [16, 256, E], F32, kind="ExternalInput").ap()
    wg_s_h = nc.dram_tensor("wg_s", [2, E, 512], F32, kind="ExternalInput").ap()
    wu_s_h = nc.dram_tensor("wu_s", [2, E, 512], F32, kind="ExternalInput").ap()
    wd_s_h = nc.dram_tensor("wd_s", [2, 512, E], F32, kind="ExternalInput").ap()
    cost_h = nc.dram_tensor("cost", [128, S_LEN], BF16, kind="ExternalInput").ap()
    sint_h = nc.dram_tensor("sint", [128, S_LEN], BF16, kind="ExternalInput").ap()
    ident_h = nc.dram_tensor("ident", [128, 128], BF16, kind="ExternalInput").ap()
    identf_h = nc.dram_tensor("identf", [128, 128], F32, kind="ExternalInput").ap()
    rmat_h = nc.dram_tensor("rmat", [128, 128], BF16, kind="ExternalInput").ap()
    consts_h = nc.dram_tensor("consts", [128, 8], F32, kind="ExternalInput").ap()
    out_h = nc.dram_tensor("out", [S_LEN, E], F32, kind="ExternalOutput").ap()
    combd_h = nc.dram_tensor("combd", [16, S_LEN], BF16, kind="Internal").ap()

    tap_h = {}

    def mktap(name, shape, dtype):
        if name in taps:
            tap_h[name] = nc.dram_tensor("tap_" + name, shape, dtype,
                                         kind="ExternalOutput").ap()

    mktap("xnt", [128, 8 * S_LEN], BF16)
    mktap("qr0", [128, S_LEN], BF16)
    mktap("kr0", [128, S_LEN], BF16)
    mktap("vt0", [128, S_LEN], BF16)
    mktap("ot0", [128, S_LEN], BF16)
    mktap("acc1", [128, 16 * E], F32)
    mktap("comb", [128, 256], F32)
    mktap("accf", [128, 16 * E], F32)

    top = contextlib.ExitStack()
    with top:
        R = Rec(nc, top)
        sb = lambda name, shape, dtype: top.enter_context(nc.sbuf_tensor(name, shape, dtype))

        ACC = sb("ACC", [128, 16 * E], F32)
        XNT = sb("XNT", [128, 8 * S_LEN], BF16)
        CONSTS = sb("CONSTS", [128, 8], F32)
        IDENT = sb("IDENT", [128, 128], BF16)
        IDENTF = sb("IDENTF", [128, 128], F32)
        RMAT = sb("RMAT", [128, 128], BF16)
        ONES = sb("ONESB", [128, 1], BF16)
        COMBF = sb("COMBF", [16, S_LEN], BF16)
        for i, v in enumerate(CV):
            nc.const_aps.register(F32, v, CONSTS[:, i:i + 1])

        v_c = R.op("sy", lambda h: h.dma_start(CONSTS[:], consts_h[:]), inc="cs", n=16)
        v_c = R.op("sy", lambda h: h.dma_start(IDENT[:], ident_h[:]), inc="cs", n=16)
        v_c = R.op("sy", lambda h: h.dma_start(IDENTF[:], identf_h[:]), inc="cs", n=16)
        v_c = R.op("sy", lambda h: h.dma_start(RMAT[:], rmat_h[:]), inc="cs", n=16)
        v_ones = R.op("g", lambda h: h.memset(ONES[:], 1.0), inc="cgp")
        for e in ("t", "sc", "v", "g", "sy"):
            R.wait(e, "cs", v_c)
            R.wait(e, "cgp", v_ones)

        p0 = contextlib.ExitStack()

        def phase_p0_ln():
            psb0 = lambda n_, s_, d_: p0.enter_context(nc.psum_tensor(n_, s_, d_))
            sb0 = lambda n_, s_, d_: p0.enter_context(nc.sbuf_tensor(n_, s_, d_))
            XIN = [sb0(f"XIN{i}", [128, E], F32) for i in range(2)]
            XNM = [sb0(f"XNM{i}", [128, E], BF16) for i in range(2)]
            JUNK = sb0("JUNK", [128, E], BF16)
            MB = sb0("MB", [128, 8], F32)
            psT = [psb0(f"psT0{i}", [128, 1024], F32) for i in range(2)]

            v_ln = [0] * 16
            v_ev = [0] * 16
            v_tr = [0] * 16
            if w_units:
                w_alloc()
                w_units[0]()
            for tc in range(16):
                par = tc % 2
                if tc >= 2:
                    R.wait("sy", "cv", v_ln[tc - 2])
                v_x = R.op("sy", lambda h, tc=tc, par=par: h.dma_start(
                    XIN[par][:], x_h[tc * 128:(tc + 1) * 128, :]), inc="cs", n=16)
                R.wait("v", "cs", v_x)
                R.op("v", lambda h, par=par: h.reduce_sum(MB[:, 6:7], XIN[par][:], AX.X),
                     inc="cv")
                vv = R.op("v", lambda h: h.tensor_scalar(
                    MB[:, 0:1], MB[:, 6:7], CV[0], None, ALU.mult), inc="cv")
                R.wait("sc", "cs", v_x)
                if tc >= 1:
                    R.wait("sc", "cv", v_ln[tc - 1])
                vs = R.op("sc", lambda h, par=par: h.activation(
                    JUNK[:], XIN[par][:], AF.Square, accum_out=MB[:, 1:2]), inc="cc")
                R.wait("v", "cc", vs)
                R.op("v", lambda h: h.tensor_scalar(
                    MB[:, 2:3], MB[:, 1:2], CV[0], None, ALU.mult), inc="cv")
                R.wait("v", "cv", vv)
                vv = R.op("v", lambda h: h.tensor_tensor(
                    MB[:, 6:7], MB[:, 0:1], MB[:, 0:1], ALU.mult), inc="cv")
                R.wait("v", "cv", vv)
                vv = R.op("v", lambda h: h.tensor_tensor(
                    MB[:, 2:3], MB[:, 2:3], MB[:, 6:7], ALU.subtract), inc="cv")
                R.wait("sc", "cv", vv)
                vs = R.op("sc", lambda h: h.activation(
                    MB[:, 3:4], MB[:, 2:3], AF.Sqrt, bias=CV[1]), inc="cc")
                R.wait("v", "cc", vs)
                vv = R.op("v", lambda h: h.reciprocal(MB[:, 4:5], MB[:, 3:4]), inc="cv")
                R.wait("v", "cv", vv)
                if tc >= 2:
                    R.wait("v", "ct", v_tr[tc - 2])
                v_ln[tc] = R.op("v", lambda h, tc=tc, par=par: h.tensor_scalar(
                    XNM[par][:], XIN[par][:], MB[:, 0:1], MB[:, 4:5],
                    ALU.subtract, ALU.mult), inc="cv")
                R.wait("t", "cv", v_ln[tc])
                if tc >= 2:
                    R.wait("t", "cc", v_ev[tc - 2])
                for ec in range(8):
                    R.op("t", lambda h, ec=ec, tc=tc, par=par: h.matmul(
                        psT[par][:, ec * 128:(ec + 1) * 128],
                        XNM[par][:, ec * 128:(ec + 1) * 128],
                        IDENT[:], start=True, stop=True))
                v_tr[tc] = R.op("t", None, inc="ct")
                R.wait("sc", "ct", v_tr[tc])
                v_ev[tc] = R.op("sc", lambda h, tc=tc, par=par: h.activation(
                    bass.AP(XNT, tc * 128, [[8 * S_LEN, 128], [S_LEN, 8], [1, 128]]),
                    bass.AP(psT[par], 0, [[1024, 128], [128, 8], [1, 128]]),
                    AF.Copy), inc="cc")
                for wu in w_units_for_chunk(tc):
                    wu()
            if "xnt" in taps:
                R.wait("sy", "cc", v_ev[15])
                R.op("sy", lambda h: h.dma_start(tap_h["xnt"], XNT[:]), inc="cs", n=16)
            return v_ev

        pA = contextlib.ExitStack()
        sbA = lambda name, shape, dtype: pA.enter_context(nc.sbuf_tensor(name, shape, dtype))
        WOTP = sbA("WOTP", [128, 8 * E], BF16)
        W123T = {nm: sbA(nm + "T", [128, 8 * E], BF16) for nm in ("w1", "w2", "w3")}
        pw = contextlib.ExitStack()
        wst = {"cc_ev": 0, "v_tdone": [0] * 4, "v_wl": 0}
        wbuf = {}

        def w_alloc():
            wbuf["WNAT"] = pw.enter_context(nc.sbuf_tensor("WNAT0", [128, 8 * E], BF16))
            wbuf["psTW"] = pw.enter_context(nc.psum_tensor("psTW", [128, 1024], F32))

        def w_dma_unit(i, nm):
            def go(i=i, nm=nm):
                if i >= 1:
                    R.wait("g", "ct", wst["v_tdone"][i - 1])
                wst["v_wl"] = R.op("g", lambda h, nm=nm: h.dma_start(
                    wbuf["WNAT"][:].rearrange("p (pc e) -> p pc e", pc=8),
                    w_h[nm].rearrange("(pc p) e -> p pc e", p=128)),
                    inc="cg", n=16)
            return go

        def w_blk_unit(i, nm, blk):
            def go(i=i, nm=nm, blk=blk):
                R.wait("t", "cg", wst["v_wl"])
                R.wait("t", "cc", wst["cc_ev"])
                for j in range(8):
                    if nm != "wo":
                        R.op("t", lambda h, blk=blk, j=j: h.matmul(
                            wbuf["psTW"][:, j * 128:(j + 1) * 128],
                            wbuf["WNAT"][:, j * E + blk * 128: j * E + blk * 128 + 128],
                            IDENT[:], start=True, stop=True))
                    else:
                        R.op("t", lambda h, blk=blk, j=j: h.matmul(
                            wbuf["psTW"][:, j * 128:(j + 1) * 128],
                            wbuf["WNAT"][:, j * E + blk: j * E + blk + 1017:8],
                            IDENT[:], start=True, stop=True))
                vt = R.op("t", None, inc="ct")
                R.wait("sc", "ct", vt)
                if nm != "wo":
                    wst["cc_ev"] = R.op("sc", lambda h, nm=nm, blk=blk: h.activation(
                        W123T[nm][:, blk * E:(blk + 1) * E], wbuf["psTW"][:], AF.Copy),
                        inc="cc")
                else:
                    wst["cc_ev"] = R.op("sc", lambda h, blk=blk: h.activation(
                        WOTP[:, blk * E:(blk + 1) * E], wbuf["psTW"][:], AF.Copy),
                        inc="cc")
                if blk == 7:
                    wst["v_tdone"][i] = R.vals["ct"]
            return go

        w_units = []
        if stage >= 2:
            for i, nm in enumerate(["w1", "w2", "w3", "wo"]):
                w_units.append(w_dma_unit(i, nm))
                for blk in range(8):
                    w_units.append(w_blk_unit(i, nm, blk))
        w_sched = {}
        for ui in range(1, len(w_units)):
            chunk = 2 + ((ui - 1) * 14) // max(len(w_units) - 1, 1)
            w_sched.setdefault(min(chunk, 15), []).append(w_units[ui])

        def w_units_for_chunk(tc):
            return w_sched.get(tc, [])

        def phase_w():
            R.wait("t", "cc", wst["cc_ev"])
        def phase_router():
            R.wait("g", "ct", R.vals["ct"])
            with contextlib.ExitStack() as p2r:
                psbr = lambda n_, s_, d_: p2r.enter_context(nc.psum_tensor(n_, s_, d_))
                sbr = lambda n_, s_, d_: p2r.enter_context(nc.sbuf_tensor(n_, s_, d_))
                RWB = sbr("RWB", [128, 8 * 16], BF16)
                LGF = sbr("LGF", [16, S_LEN], F32)
                LGT = sbr("LGT", [128, 256], F32)
                COMBT = sbr("COMBT", [128, 256], F32)
                RS = sbr("RS", [128, 8], F32)
                L2 = sbr("L2", [128, 16], F32)
                MS1 = sbr("MS1", [128, 16], F32)
                MS2 = sbr("MS2", [128, 16], F32)
                psL = psbr("psL", [16, 1024], F32)
                psLT = psbr("psLT", [128, 256], F32)
                v_rw = R.op("g", lambda h: h.dma_start(
                    RWB[:].rearrange("e (ec x) -> e ec x", ec=8),
                    rw_h.rearrange("(ec e) x -> e ec x", e=128)), inc="cg", n=16)
                R.wait("t", "cg", v_rw)
                v_lg = 0
                for hf2 in range(2):
                    R.wait("t", "cc", v_lg)
                    for ec in range(8):
                        for hf in range(2):
                            R.op("t", lambda h, ec=ec, hf=hf, hf2=hf2: h.matmul(
                                psL[:, hf * 512:(hf + 1) * 512],
                                RWB[:, ec * 16:(ec + 1) * 16],
                                XNT[:, ec * S_LEN + hf2 * 1024 + hf * 512:
                                    ec * S_LEN + hf2 * 1024 + hf * 512 + 512],
                                start=(ec == 0), stop=(ec == 7)))
                    vt = R.op("t", None, inc="ct")
                    R.wait("sc", "ct", vt)
                    v_lg = R.op("sc", lambda h, hf2=hf2: h.activation(
                        LGF[:, hf2 * 1024:(hf2 + 1) * 1024], psL[:], AF.Copy),
                        inc="cc")
                R.wait("t", "cc", v_lg)
                for tc in range(16):
                    R.op("t", lambda h, tc=tc: h.matmul(
                        psLT[:, tc * 16:(tc + 1) * 16],
                        LGF[:, tc * 128:(tc + 1) * 128], IDENTF[0:16, 0:16], start=True, stop=True))
                vt = R.op("t", None, inc="ct")
                R.wait("sc", "ct", vt)
                v_lgt = R.op("sc", lambda h: h.activation(LGT[:], psLT[:], AF.Copy),
                             inc="cc")
                R.wait("v", "cc", v_lgt)
                for tc in range(16):
                    vv = R.op("v", lambda h, tc=tc: h.reduce_max(
                        RS[:, 0:1], LGT[:, tc * 16:(tc + 1) * 16], AX.X), inc="cv")
                    R.wait("v", "cv", vv)
                    vv = R.op("v", lambda h, tc=tc: h.tensor_scalar(
                        MS1[:], LGT[:, tc * 16:(tc + 1) * 16], RS[:, 0:1], None,
                        ALU.is_equal), inc="cv")
                    R.wait("v", "cv", vv)
                    vv = R.op("v", lambda h, tc=tc: h.scalar_tensor_tensor(
                        L2[:], MS1[:], CV[2], LGT[:, tc * 16:(tc + 1) * 16],
                        ALU.mult, ALU.add), inc="cv")
                    R.wait("v", "cv", vv)
                    vv = R.op("v", lambda h: h.reduce_max(RS[:, 1:2], L2[:], AX.X),
                              inc="cv")
                    R.wait("v", "cv", vv)
                    vv = R.op("v", lambda h: h.tensor_scalar(
                        MS2[:], L2[:], RS[:, 1:2], None, ALU.is_equal), inc="cv")
                    R.wait("v", "cv", vv)
                    vv = R.op("v", lambda h: h.tensor_tensor(
                        RS[:, 2:3], RS[:, 1:2], RS[:, 0:1], ALU.subtract), inc="cv")
                    R.wait("sc", "cv", vv)
                    vs = R.op("sc", lambda h: h.activation(
                        RS[:, 4:5], RS[:, 2:3], AF.Sigmoid), inc="cc")
                    R.wait("sc", "cc", vs)
                    vs = R.op("sc", lambda h: h.activation(
                        RS[:, 3:4], RS[:, 2:3], AF.Sigmoid, scale=CV[3]), inc="cc")
                    R.wait("v", "cc", vs)
                    vv = R.op("v", lambda h, tc=tc: h.tensor_scalar(
                        COMBT[:, tc * 16:(tc + 1) * 16], MS1[:], RS[:, 3:4], None,
                        ALU.mult), inc="cv")
                    R.wait("v", "cv", vv)
                    R.op("v", lambda h, tc=tc: h.scalar_tensor_tensor(
                        COMBT[:, tc * 16:(tc + 1) * 16], MS2[:], RS[:, 4:5],
                        COMBT[:, tc * 16:(tc + 1) * 16], ALU.mult, ALU.add), inc="cv")
                v_combt = R.vals["cv"]
                if "comb" in taps:
                    R.wait("sy", "cv", v_combt)
                    R.op("sy", lambda h: h.dma_start(tap_h["comb"], COMBT[:]),
                         inc="cs", n=16)
                R.wait("t", "cv", v_combt)
                v_combf = 0
                for r2 in range(2):
                    R.wait("t", "cc", v_combf)
                    for j in range(8):
                        tc = r2 * 8 + j
                        R.op("t", lambda h, tc=tc, j=j: h.matmul(
                            psL[:, j * 128:(j + 1) * 128],
                            COMBT[:, tc * 16:(tc + 1) * 16], IDENTF[:],
                            start=True, stop=True))
                    vt = R.op("t", None, inc="ct")
                    R.wait("sc", "ct", vt)
                    v_combf = R.op("sc", lambda h, r2=r2: h.activation(
                        COMBF[:, r2 * 1024:(r2 + 1) * 1024], psL[:], AF.Copy),
                        inc="cc")
                R.wait("g", "cc", v_combf)
                v_combd = R.op("g", lambda h: h.dma_start(combd_h, COMBF[:]),
                               inc="cg", n=16)
            return v_combd
        if stage >= 1:
            _v_ev_saved = phase_p0_ln()
        if stage >= 2:
            phase_w()
        pw.close()
        _v_combd = phase_router() if (do_ffn and stage >= 4) else 0
        if stage < 1:
            vz = R.op("v", lambda h: h.memset(XNT[:], 0.0), inc="cv")
            R.wait("t", "cv", vz)
        p0.close()
        R.barrier()

        if do_attn and stage >= 3:
            COST = sbA("COST", [128, S_LEN], BF16)
            SINT = sbA("SINT", [128, S_LEN], BF16)
            R.op("sy", lambda h: h.dma_start(COST[:], cost_h[:]), inc="cs", n=16)
            v_tab = R.op("sy", lambda h: h.dma_start(SINT[:], sint_h[:]), inc="cs", n=16)

            QHB = sbA("QHB", [128, S_LEN], BF16)
            KHB = sbA("KHB", [128, S_LEN], BF16)
            QR = sbA("QR", [128, S_LEN], BF16)
            KR = sbA("KR", [128, S_LEN], BF16)
            VT = sbA("VT", [128, S_LEN], BF16)
            T1 = sbA("T1", [128, 512], BF16)
            T2 = sbA("T2", [128, 512], BF16)
            PT = [sbA(f"PT{i}", [128, 1024], BF16) for i in range(3)]
            OSB = KHB
            OT = QHB
            ASB = OSB
            RSROW = sbA("RSROW", [1, S_LEN], BF16)
            RECT = sbA("RECT", [128, 16], F32)

            v_rope_prev = 0
            v_accev_prev = 0
            v_exp_carry = [0, 0]
            v_pv_carry3 = [0, 0, 0]
            v_oe = 0
            for head in range(n_heads):
                with contextlib.ExitStack() as p1a:
                    psba = lambda n_, s_, d_: p1a.enter_context(nc.psum_tensor(n_ + f"_{head}", s_, d_))
                    psQK = psba("psQK", [128, S_LEN], F32)
                    psV = psba("psV", [128, 1024], F32)
                    psSH = psba("psSH", [128, 1024], F32)

                    def proj_qk(wt, dst, head=head):
                        for pc in range(8):
                            for ec in range(8):
                                R.op("t", lambda h, pc=pc, ec=ec, wt=wt, head=head: h.matmul(
                                    psQK[:, pc * TH:(pc + 1) * TH],
                                    W123T[wt][:, ec * E + pc * 128: ec * E + pc * 128 + 128],
                                    XNT[:, ec * S_LEN + head * TH: ec * S_LEN + head * TH + TH],
                                    start=(ec == 0), stop=(ec == 7)))
                        vt = R.op("t", None, inc="ct")
                        R.wait("sc", "ct", vt)
                        return R.op("sc", lambda h, dst=dst: h.activation(
                            dst[:], psQK[:], AF.Copy), inc="cc")

                    R.wait("sc", "cv", v_rope_prev)
                    v_q = proj_qk("w1", QHB)
                    R.wait("t", "cc", v_q)
                    v_k = proj_qk("w2", KHB)

                    v_vev = 0
                    for blk in range(2):
                        R.wait("t", "cc", v_vev)
                        for ec in range(8):
                            for hf in range(2):
                                R.op("t", lambda h, ec=ec, hf=hf, blk=blk, head=head: h.matmul(
                                    psV[:, hf * 512:(hf + 1) * 512],
                                    XNT[:, ec * S_LEN + head * TH + blk * 128:
                                        ec * S_LEN + head * TH + blk * 128 + 128],
                                    W123T["w3"][:, ec * E + hf * 512: ec * E + hf * 512 + 512],
                                    start=(ec == 0), stop=(ec == 7)))
                        vt = R.op("t", None, inc="ct")
                        R.wait("sc", "ct", vt)
                        v_vev = R.op("sc", lambda h, blk=blk: h.activation(
                            bass.AP(VT, blk * 128, [[S_LEN, 128], [TH, 8], [1, 128]]),
                            bass.AP(psV, 0, [[1024, 128], [128, 8], [1, 128]]),
                            AF.Copy), inc="cc")

                    def rope(src, dst, v_src):
                        vq = 0
                        for j in range(4):
                            par = j % 2
                            R.wait("t", "cc", v_src)
                            R.wait("t", "cv", vq)
                            vt = R.op("t", lambda h, j=j, par=par, src=src: h.matmul(
                                psSH[:, par * 512:(par + 1) * 512], RMAT[:],
                                src[:, j * 512:(j + 1) * 512], start=True, stop=True),
                                inc="ct")
                            R.wait("v", "cs", v_tab)
                            R.wait("v", "cc", v_src)
                            R.op("v", lambda h, j=j, src=src: h.tensor_tensor(
                                T1[:], src[:, j * 512:(j + 1) * 512],
                                COST[:, j * 512:(j + 1) * 512], ALU.mult), inc="cv")
                            R.wait("v", "ct", vt)
                            vv = R.op("v", lambda h, j=j, par=par: h.tensor_tensor(
                                T2[:], psSH[:, par * 512:(par + 1) * 512],
                                SINT[:, j * 512:(j + 1) * 512], ALU.mult), inc="cv")
                            R.wait("v", "cv", vv)
                            vq = R.op("v", lambda h, j=j, dst=dst: h.tensor_tensor(
                                dst[:, j * 512:(j + 1) * 512], T1[:], T2[:], ALU.add),
                                inc="cv")
                        return vq

                    v_qr = rope(QHB, QR, v_q)
                    v_kr = rope(KHB, KR, v_k)
                    v_rope_prev = v_kr

                if head == 0 and "qr0" in taps:
                    R.wait("sy", "cv", v_qr)
                    R.op("sy", lambda h: h.dma_start(tap_h["qr0"], QR[:]), inc="cs", n=16)
                    R.wait("sy", "cv", v_kr)
                    R.op("sy", lambda h: h.dma_start(tap_h["kr0"], KR[:]), inc="cs", n=16)
                if head == 0 and "vt0" in taps:
                    R.wait("sy", "cc", R.vals["cc"])
                    R.op("sy", lambda h: h.dma_start(tap_h["vt0"], VT[:]), inc="cs", n=16)

                with contextlib.ExitStack() as p1b:
                    psbb = lambda n_, s_, d_: p1b.enter_context(nc.psum_tensor(n_ + f"_{head}", s_, d_))
                    psS = [psbb(f"psS{i}", [128, 1024], F32) for i in range(2)]
                    psO = psbb("psO", [128, 1024], F32)
                    psR = psbb("psR", [1, 1024], F32)
                    R.wait("t", "cv", v_qr)
                    R.wait("t", "cv", v_kr)
                    v_exp = [0] * 32
                    v_pv = [0] * 32
                    v_s = [0] * 32
                    for kk in range(34):
                        if kk < 32:
                            qh, kt = kk // 16, kk % 16
                            par = kk % 2
                            R.wait("t", "cc",
                                   v_exp[kk - 2] if kk >= 2 else v_exp_carry[kk])
                            for hf in range(2):
                                R.op("t", lambda h, kt=kt, par=par, qh=qh, hf=hf: h.matmul(
                                    psS[par][:, hf * 512:(hf + 1) * 512],
                                    KR[:, kt * 128:(kt + 1) * 128],
                                    QR[:, qh * 1024 + hf * 512: qh * 1024 + hf * 512 + 512],
                                    start=True, stop=True))
                            v_s[kk] = R.op("t", None, inc="ct")
                            R.wait("sc", "ct", v_s[kk])
                            R.wait("sc", "ct",
                                   v_pv[kk - 3] if kk >= 3 else v_pv_carry3[kk])
                            v_exp[kk] = R.op("sc", lambda h, par3=kk % 3, par=par: h.activation(
                                PT[par3][:], psS[par][:], AF.Exp,
                                bias=SOFTMAX_SHIFT, scale=INV_SQRT8), inc="cc")
                        if kk >= 2:
                            pk = kk - 2
                            qh, kt = pk // 16, pk % 16
                            par = pk % 2
                            R.wait("t", "cc", v_exp[pk])
                            if kt == 0:
                                R.wait("t", "cc", v_oe)
                            par3 = pk % 3
                            for hf in range(2):
                                R.op("t", lambda h, kt=kt, par3=par3, hf=hf: h.matmul(
                                    psO[:, hf * 512:(hf + 1) * 512],
                                    VT[:, kt * 128:(kt + 1) * 128],
                                    PT[par3][:, hf * 512:(hf + 1) * 512],
                                    start=(kt == 0), stop=(kt == 15),
                                    skip_group_check=True))
                            for hf in range(2):
                                R.op("t", lambda h, kt=kt, par3=par3, hf=hf: h.matmul(
                                    psR[:, hf * 512:(hf + 1) * 512],
                                    ONES[:],
                                    PT[par3][:, hf * 512:(hf + 1) * 512],
                                    start=(kt == 0), stop=(kt == 15),
                                    skip_group_check=True))
                            v_pv[pk] = R.op("t", None, inc="ct")
                            if kt == 15:
                                R.wait("sc", "ct", v_pv[pk])
                                R.op("sc", lambda h, qh=qh: h.activation(
                                    RSROW[0:1, qh * 1024:(qh + 1) * 1024], psR[0:1, :],
                                    AF.Copy), inc="cc")
                                v_oe = R.op("sc", lambda h, qh=qh: h.activation(
                                    OSB[:, qh * 1024:(qh + 1) * 1024], psO[:],
                                    AF.Copy), inc="cc")
                    v_exp_carry = [v_exp[30], v_exp[31]]
                    v_pv_carry3 = [v_pv[31]] * 3

                with contextlib.ExitStack() as p1c:
                    psbc = lambda n_, s_, d_: p1c.enter_context(nc.psum_tensor(n_ + f"_{head}", s_, d_))
                    psAm = psbc("psAm", [128, 1024], F32)
                    psTT = psbc("psTT", [128, 1024], F32)
                    R.wait("t", "cc", v_oe)
                    cc_ev = v_accev_prev
                    for j in range(16):
                        R.op("t", lambda h, j=j: h.matmul(
                            psAm[:, j:j + 1],
                            RSROW[0:1, j * 128:(j + 1) * 128], IDENT[0:1, 0:1], start=True, stop=True))
                    vt_rs = R.op("t", None, inc="ct")
                    R.wait("v", "ct", vt_rs)
                    v_rec = R.op("v", lambda h: h.reciprocal(
                        RECT[:], psAm[:, 0:16]), inc="cv")
                    for r in range(2):
                        R.wait("t", "cc", cc_ev)
                        for j in range(8):
                            R.op("t", lambda h, r=r, j=j: h.matmul(
                                psTT[:, j * 128:(j + 1) * 128],
                                OSB[:, (r * 8 + j) * 128:(r * 8 + j) * 128 + 128],
                                IDENT[:], start=True, stop=True))
                        vt = R.op("t", None, inc="ct")
                        R.wait("sc", "ct", vt)
                        R.wait("sc", "cv", v_rec)
                        for j in range(8):
                            kt2 = r * 8 + j
                            cc_ev = R.op("sc", lambda h, kt2=kt2, j=j: h.activation(
                                OT[:, kt2 * 128:(kt2 + 1) * 128],
                                psTT[:, j * 128:(j + 1) * 128], AF.Copy,
                                scale=RECT[:, kt2:kt2 + 1]), inc="cc")
                    if head == 0 and "ot0" in taps:
                        R.wait("sy", "cc", cc_ev)
                        R.op("sy", lambda h: h.dma_start(tap_h["ot0"], OT[:]), inc="cs",
                             n=16)
                    for u in range(2):
                        R.wait("t", "cc", cc_ev)
                        R.wait("t", "cv", v_rec)
                        for i, kt in enumerate(range(u, 16, 2)):
                            for hf in range(2):
                                R.op("t", lambda h, kt=kt, i=i, hf=hf: h.matmul(
                                    psAm[:, hf * 512:(hf + 1) * 512],
                                    OT[:, kt * 128:(kt + 1) * 128],
                                    WOTP[:, (kt // 2) * E + hf * 512:
                                         (kt // 2) * E + hf * 512 + 512],
                                    start=(i == 0), stop=(i == 7)))
                        vt = R.op("t", None, inc="ct")
                        R.wait("sc", "ct", vt)
                        v_asb = R.op("sc", lambda h: h.activation(
                            ASB[:, 0:1024], psAm[:], AF.Copy), inc="cc")
                        R.wait("t", "cc", v_asb)
                        for cc_i in range(8):
                            R.op("t", lambda h, cc_i=cc_i: h.matmul(
                                psTT[:, cc_i * 128:(cc_i + 1) * 128],
                                ASB[:, cc_i * 128:(cc_i + 1) * 128], IDENT[:], start=True, stop=True))
                        vt = R.op("t", None, inc="ct")
                        R.wait("sc", "ct", vt)
                        cc_ev = R.op("sc", lambda h, u=u, head=head: h.activation(
                            bass.AP(ACC, (u * 8) * E + head * 128,
                                    [[16 * E, 128], [E, 8], [1, 128]]),
                            bass.AP(psTT, 0, [[1024, 128], [128, 8], [1, 128]]),
                            AF.Copy), inc="cc")
                    v_accev_prev = cc_ev
            if "acc1" in taps:
                R.wait("sy", "cc", R.vals["cc"])
                R.op("sy", lambda h: h.dma_start(tap_h["acc1"], ACC[:]), inc="cs", n=16)
        if not (do_attn and stage >= 3):
            vz = R.op("v", lambda h: h.memset(ACC[:], 0.0), inc="cv")
            R.wait("t", "cv", vz)
        R.barrier()

        pA.close()

        if do_ffn and stage >= 4:
            pass
            R.barrier()

            with contextlib.ExitStack() as p2e:
                psbe = lambda n_, s_, d_: p2e.enter_context(nc.psum_tensor(n_, s_, d_))
                sbe = lambda n_, s_, d_: p2e.enter_context(nc.sbuf_tensor(n_, s_, d_))
                psG1 = psbe("psG1", [128, 1024], F32)
                psG2 = psbe("psG2", [128, 1024], F32)
                psD = [psbe(f"psD{i}", [128, 1024], F32) for i in range(2)]
                HARE = sbe("HARE", [128, 8 * S_LEN], BF16)
                SG = sbe("SG", [128, S_LEN], BF16)
                TFFN = sbe("TFFN", [128, S_LEN], BF16)
                WGA = sbe("WGA", [128, 8 * 512], BF16)
                WUA = sbe("WUA", [128, 8 * 512], BF16)
                WDA = sbe("WDA", [128, 8 * 1024], BF16)
                CB = [sbe(f"CB{i}", [128, S_LEN], BF16) for i in range(2)]

                groups = [("r", [4 * g + i for i in range(4)]) for g in range(4)]
                groups.append(("s", [0, 1]))
                XI = [XNT[:, 0:2048].bitcast(F32), XNT[:, 2048:4096].bitcast(F32)]
                OU = [XNT[:, 4096:6144].bitcast(F32), XNT[:, 6144:8192].bitcast(F32)]
                v_ad3 = [0] * 16
                v_st3 = [0] * 16
                v_wg_used = {}
                v_wd_used = {}
                v_cb_used = {}
                v_prev_add = [0, 0]
                v_silu_prev = 0
                v_h_prev = 0
                for gi, (gkind, gl) in enumerate(groups):
                    shared = gkind == "s"
                    nmc = 4 if shared else 2
                    mid = 512 if shared else 256
                    ndslots = 2 if shared else 4
                    v_h = []
                    v_ld_all = 0
                    for li, e in enumerate(gl):
                        sl_g = 0 if shared else li % 2
                        sl_d = li % ndslots
                        wg_off = sl_g * 8 * 256 if not shared else 0
                        wd_off = sl_d * (4 * E if shared else 2 * E)
                        R.wait("g", "cv", v_wg_used.get(sl_g, 0))
                        src_g = (wg_s_h if shared else wg_r_h)[e].rearrange(
                            "(ec e2) m -> e2 ec m", e2=128)
                        v_lg_ = R.op("g", lambda h, src=src_g, off=wg_off, mid=mid:
                                     h.dma_start(WGA[:, off:off + 8 * mid].rearrange(
                                         "p (ec m) -> p ec m", ec=8), src),
                                     inc="cg", n=16)
                        src_u = (wu_s_h if shared else wu_r_h)[e].rearrange(
                            "(ec e2) m -> e2 ec m", e2=128)
                        v_lu_ = R.op("g", lambda h, src=src_u, off=wg_off, mid=mid:
                                     h.dma_start(WUA[:, off:off + 8 * mid].rearrange(
                                         "p (ec m) -> p ec m", ec=8), src),
                                     inc="cg", n=16)
                        R.wait("g", "cv", v_wd_used.get(sl_d, 0))
                        src_d = (wd_s_h if shared else wd_r_h)[e].rearrange(
                            "(mc m) x -> m mc x", m=128)
                        v_ld_all = R.op("g", lambda h, src=src_d, off=wd_off, nmc=nmc:
                                        h.dma_start(WDA[:, off:off + nmc * E].rearrange(
                                            "p (mc x) -> p mc x", mc=nmc), src),
                                        inc="cg", n=16)
                        v_cb_ = 0
                        if not shared:
                            R.wait("g", "cv", v_cb_used.get(sl_g, 0))
                            v_cb_ = R.op("g", lambda h, e=e, sl=sl_g: h.dma_start(
                                CB[sl][:], combd_h[e:e + 1, :].partition_broadcast(128)),
                                inc="cg", n=16)
                        for mc in range(nmc):
                            for th in range(2):
                                R.wait("t", "cg", v_lg_)
                                R.wait("t", "cc", v_silu_prev)
                                for ec in range(8):
                                    for hf in range(2):
                                        R.op("t", lambda h, ec=ec, hf=hf, mc=mc, th=th,
                                             off=wg_off, mid=mid: h.matmul(
                                            psG1[:, hf * 512:(hf + 1) * 512],
                                            WGA[:, off + ec * mid + mc * 128:
                                                off + ec * mid + mc * 128 + 128],
                                            XNT[:, ec * S_LEN + th * 1024 + hf * 512:
                                                ec * S_LEN + th * 1024 + hf * 512 + 512],
                                            start=(ec == 0), stop=(ec == 7)))
                                vt_g = R.op("t", None, inc="ct")
                                R.wait("sc", "ct", vt_g)
                                R.wait("sc", "cv", v_h_prev)
                                v_silu_prev = R.op("sc", lambda h, th=th: h.activation(
                                    SG[:, th * 1024:(th + 1) * 1024], psG1[:],
                                    AF.Silu), inc="cc")
                                R.wait("t", "cg", v_lu_)
                                R.wait("t", "cv", v_h_prev)
                                for ec in range(8):
                                    for hf in range(2):
                                        R.op("t", lambda h, ec=ec, hf=hf, mc=mc, th=th,
                                             off=wg_off, mid=mid: h.matmul(
                                            psG2[:, hf * 512:(hf + 1) * 512],
                                            WUA[:, off + ec * mid + mc * 128:
                                                off + ec * mid + mc * 128 + 128],
                                            XNT[:, ec * S_LEN + th * 1024 + hf * 512:
                                                ec * S_LEN + th * 1024 + hf * 512 + 512],
                                            start=(ec == 0), stop=(ec == 7)))
                                vt_u = R.op("t", None, inc="ct")
                                R.wait("v", "ct", vt_u)
                                R.wait("v", "cc", v_silu_prev)
                                hslot = li * nmc + mc
                                base = hslot * S_LEN + th * 1024
                                if not shared:
                                    R.wait("v", "cg", v_cb_)
                                    vv = R.op("v", lambda h, th=th: h.tensor_tensor(
                                        TFFN[:, 0:1024], SG[:, th * 1024:(th + 1) * 1024],
                                        psG2[:], ALU.mult), inc="cv")
                                    R.wait("v", "cv", vv)
                                    vv = R.op("v", lambda h, base=base, th=th, sl=sl_g:
                                              h.tensor_tensor(
                                        HARE[:, base:base + 1024], TFFN[:, 0:1024],
                                        CB[sl][:, th * 1024:(th + 1) * 1024],
                                        ALU.mult), inc="cv")
                                else:
                                    vv = R.op("v", lambda h, base=base, th=th:
                                              h.tensor_tensor(
                                        HARE[:, base:base + 1024],
                                        SG[:, th * 1024:(th + 1) * 1024],
                                        psG2[:], ALU.mult), inc="cv")
                                v_h_prev = vv
                                v_h.append(vv)
                        v_wg_used[sl_g] = v_h_prev
                        if not shared:
                            v_cb_used[sl_g] = v_h_prev
                    v_hall = max(v_h)
                    R.wait("t", "cv", v_hall)
                    R.wait("t", "cg", v_ld_all)
                    v_add = [0] * 16
                    for tc in range(16):
                        par = tc % 2
                        prev = v_add[tc - 2] if tc >= 2 else v_prev_add[par]
                        R.wait("t", "cv", prev)
                        first = True
                        nexp = len(gl)
                        for li in range(nexp):
                            sl_d = li % ndslots
                            wd_off2 = sl_d * (4 * E if shared else 2 * E)
                            for mc in range(nmc):
                                hslot = li * nmc + mc
                                last = (li == nexp - 1 and mc == nmc - 1)
                                for hf in range(2):
                                    R.op("t", lambda h, tc=tc, par=par, hslot=hslot,
                                         off=wd_off2, mc=mc, first=first, hf=hf, last=last:
                                         h.matmul(
                                        psD[par][:, hf * 512:(hf + 1) * 512],
                                        HARE[:, hslot * S_LEN + tc * 128:
                                             hslot * S_LEN + tc * 128 + 128],
                                        WDA[:, off + mc * E + hf * 512:
                                            off + mc * E + hf * 512 + 512],
                                        start=first, stop=last,
                                        skip_group_check=True))
                                first = False
                        vt = R.op("t", None, inc="ct")
                        R.wait("v", "ct", vt)
                        v_add[tc] = R.op("v", lambda h, tc=tc, par=par: h.tensor_tensor(
                            ACC[:, tc * E:(tc + 1) * E], psD[par][:],
                            ACC[:, tc * E:(tc + 1) * E], ALU.add), inc="cv")
                        if shared:
                            if tc < 2:
                                R.wait("sy", "cv", v_hall)
                            else:
                                R.wait("sy", "cv", v_ad3[tc - 2])
                                R.wait("sy", "cs", v_st3[tc - 2])
                            v_xl = R.op("sy", lambda h, tc=tc, par=par: h.dma_start(
                                XI[par], x_h[tc * 128:(tc + 1) * 128, :]),
                                inc="cs", n=16)
                            R.wait("v", "cs", v_xl)
                            v_ad3[tc] = R.op("v", lambda h, tc=tc, par=par:
                                             h.tensor_tensor(
                                OU[par], ACC[:, tc * E:(tc + 1) * E], XI[par],
                                ALU.add), inc="cv")
                            R.wait("sc", "cv", v_ad3[tc])
                            v_st3[tc] = R.op("sc", lambda h, tc=tc, par=par:
                                             h.dma_start(
                                out_h[tc * 128:(tc + 1) * 128, :], OU[par]),
                                inc="cs", n=16)
                    v_prev_add = [v_add[14], v_add[15]]
                    for sl in range(ndslots):
                        v_wd_used[sl] = v_add[15]
                    v_add_final = list(v_add)
                    v_hall_last = v_hall
        if (not do_ffn or stage < 4) or "accf" in taps:
            R.barrier()
            v_add_final = [R.vals["cv"]] * 16
            v_hall_last = R.vals["cv"]
        if "accf" in taps:
            R.op("sy", lambda h: h.dma_start(tap_h["accf"], ACC[:]), inc="cs", n=16)

        if not (do_ffn and stage >= 4):
            XI = [XNT[:, 0:2048].bitcast(F32), XNT[:, 2048:4096].bitcast(F32)]
            OU = [XNT[:, 4096:6144].bitcast(F32), XNT[:, 6144:8192].bitcast(F32)]
            v_st = [0] * 16
            v_ad = [0] * 16
            for tc in range(16):
                par = tc % 2
                if tc < 2:
                    R.wait("sy", "cv", v_hall_last)
                if tc >= 2:
                    R.wait("sy", "cv", v_ad[tc - 2])
                    R.wait("sy", "cs", v_st[tc - 2])
                v_xl = R.op("sy", lambda h, tc=tc, par=par: h.dma_start(
                    XI[par], x_h[tc * 128:(tc + 1) * 128, :]), inc="cs", n=16)
                R.wait("v", "cs", v_xl)
                R.wait("v", "cv", v_add_final[tc])
                v_ad[tc] = R.op("v", lambda h, tc=tc, par=par: h.tensor_tensor(
                    OU[par], ACC[:, tc * E:(tc + 1) * E], XI[par], ALU.add),
                    inc="cv")
                R.wait("sc", "cv", v_ad[tc])
                v_st[tc] = R.op("sc", lambda h, tc=tc, par=par: h.dma_start(
                    out_h[tc * 128:(tc + 1) * 128, :], OU[par]), inc="cs", n=16)
            R.wait("sy", "cs", R.vals["cs"])
            R.wait("g", "cs", R.vals["cs"])
        R.wait("sy", "cs", R.vals["cs"])
        R.wait("g", "cs", R.vals["cs"])

        with nc.Block() as block:
            R.replay(block)
    return nc


def make_in_map(inputs, core_id):
    hc = host_constants()
    m = {
        "x": np.ascontiguousarray(np.asarray(inputs["L0_attn_emb"])[core_id]),
        "w1": np.asarray(inputs["L0_attn_W1"]),
        "w2": np.asarray(inputs["L0_attn_W2"]),
        "w3": np.asarray(inputs["L0_attn_W3"]),
        "wo": np.asarray(inputs["L0_attn_Wo"]),
        "rw": np.asarray(inputs["L0_ffn_router_w"]),
        "wg_r": np.asarray(inputs["L0_ffn_wg_r"]),
        "wu_r": np.asarray(inputs["L0_ffn_wu_r"]),
        "wd_r": np.asarray(inputs["L0_ffn_wd_r"]),
        "wg_s": np.asarray(inputs["L0_ffn_wg_s"]),
        "wu_s": np.asarray(inputs["L0_ffn_wu_s"]),
        "wd_s": np.asarray(inputs["L0_ffn_wd_s"]),
        "cost": hc["cost"], "sint": hc["sint"], "ident": hc["ident"],
        "identf": hc["identf"], "rmat": hc["rmat"], "consts": hc["consts"],
    }
    return m


def kernel(**inputs):
    nc = build()
    in_maps = [make_in_map(inputs, c) for c in range(8)]
    res = run_bass_kernel_spmd(nc, in_maps, core_ids=list(range(8)))
    import numpy as _np
    return _np.stack([_np.asarray(r["out"]) for r in res.results], axis=0)


# version 10
# speedup vs baseline: 1.4561x; 1.0046x over previous
import contextlib
import numpy as np

import concourse.bass as bass
import concourse.mybir as mybir
from concourse.bass_utils import run_bass_kernel_spmd

F32 = mybir.dt.float32
BF16 = mybir.dt.bfloat16
AF = mybir.ActivationFunctionType
ALU = mybir.AluOpType
AX = mybir.AxisListType

S_LEN = 2048
E = 1024
NH = 8
TH = 256

INV_SQRT8 = 0.35355339059327373
SOFTMAX_SHIFT = -30.0
CV = [1.0 / 1024.0, 1e-5, -1e30, -1.0, INV_SQRT8, SOFTMAX_SHIFT]

ENG_NAMES = {"t": "tensor", "sc": "scalar", "v": "vector", "g": "gpsimd", "sy": "sync"}


class Rec:
    def __init__(self, nc, stack):
        self.nc = nc
        self.q = {e: [] for e in ENG_NAMES}
        self.hwm = {}
        self.sems = {}
        self.vals = {}
        for name in ("cg", "cs", "ct", "cc", "cv", "cgp",
                     "et", "ec", "ev2", "eg"):
            self.sems[name] = stack.enter_context(nc.semaphore(name))
            self.vals[name] = 0

    EVT = {"sc": "ec", "v": "ev2", "g": "eg"}

    def op(self, eng, fn, inc=None, n=1):
        if inc is None:
            if eng == "t":
                self.q[eng].append({"fn": fn, "inc": None})
            else:
                self.q[eng].append(fn)
            return None
        sem = self.sems[inc]
        self.vals[inc] += n
        v = self.vals[inc]
        if eng == "t":
            if fn is None:
                last = self.q[eng][-1]
                assert isinstance(last, dict) and last["inc"] is None, \
                    "no tensor instruction to attach inc to"
                last["inc"] = (sem, n)
            else:
                self.q[eng].append({"fn": fn, "inc": (sem, n)})
            return v
        if inc in ("cs", "cg"):
            def wrapped(h, fn=fn, sem=sem, n=n):
                fn(h).then_inc(sem, n)

            self.q[eng].append(wrapped)
        else:
            en = self.EVT[eng]
            esem = self.sems[en]
            self.vals[en] += 1
            ev = self.vals[en]

            def wrapped(h, fn=fn, sem=sem, n=n, esem=esem, ev=ev):
                fn(h).then_inc(esem, 1)
                h.wait_ge(esem, ev)
                h.nop().then_inc(sem, n)

            self.q[eng].append(wrapped)
        return v

    def wait(self, eng, cname, v):
        if not v or v <= 0:
            return
        hw = self.hwm.setdefault((eng, cname), 0)
        if v <= hw:
            return
        self.hwm[(eng, cname)] = v
        sem = self.sems[cname]
        self.q[eng].append(lambda h, sem=sem, v=v: h.wait_ge(sem, v))

    def barrier(self):
        engs = ("t", "sc", "v", "g", "sy")
        for e in ("sc", "v", "g"):
            self.op(e, lambda h: h.nop(),
                    inc={"sc": "cc", "v": "cv", "g": "cgp"}[e])
        if self.q["t"] and isinstance(self.q["t"][-1], dict) \
                and self.q["t"][-1]["inc"] is None:
            self.op("t", None, inc="ct")
        snap = dict(self.vals)
        for e in engs:
            for cn in ("ct", "cc", "cv", "cgp", "cg", "cs"):
                if e == "t" and cn == "ct":
                    continue
                self.wait(e, cn, snap[cn])

    def replay(self, block):
        for eng, name in ENG_NAMES.items():
            ops = self.q[eng]

            def body(h, ops=ops):
                for f in ops:
                    if isinstance(f, dict):
                        instr = f["fn"](h)
                        if f["inc"] is not None:
                            instr.then_inc(*f["inc"])
                    else:
                        f(h)

            getattr(block, name)(body)


def host_constants():
    import ml_dtypes
    DH = 128
    d = {}
    inv_freq = 1.0 / 10000.0 ** (np.arange(0, DH, 2, dtype=np.float64) / DH)
    t = np.arange(S_LEN, dtype=np.float64)
    freqs = np.repeat(t[:, None] * inv_freq[None, :], 2, axis=1)
    COS = np.cos(freqs)
    SIN2 = np.sin(freqs)
    SIN2[:, 0::2] *= -1.0
    tau_to_sp = np.empty(S_LEN, np.int64)
    for c in range(8):
        for sl in range(TH):
            tau_to_sp[c * TH + sl] = 8 * sl + c
    d["cost"] = np.ascontiguousarray(COS[tau_to_sp, :].T).astype(ml_dtypes.bfloat16)
    d["sint"] = np.ascontiguousarray(SIN2[tau_to_sp, :].T).astype(ml_dtypes.bfloat16)
    d["ident"] = np.eye(128, dtype=ml_dtypes.bfloat16)
    d["identf"] = np.eye(128, dtype=np.float32)
    r = np.zeros((128, 128), np.float32)
    for i in range(128):
        r[i ^ 1, i] = 1.0
    d["rmat"] = r.astype(ml_dtypes.bfloat16)
    consts = np.zeros((128, 8), np.float32)
    for i, v in enumerate(CV):
        consts[:, i] = v
    d["consts"] = consts
    return d


def build(n_heads=NH, do_attn=True, do_ffn=True, taps=(), stage=9, p0_tr=True):
    nc = bass.Bass("TRN2", target_bir_lowering=False, detect_race_conditions=False)

    x_h = nc.dram_tensor("x", [S_LEN, E], F32, kind="ExternalInput").ap()
    w_h = {nm: nc.dram_tensor(nm, [E, E], F32, kind="ExternalInput").ap()
           for nm in ("w1", "w2", "w3", "wo")}
    rw_h = nc.dram_tensor("rw", [E, 16], F32, kind="ExternalInput").ap()
    wg_r_h = nc.dram_tensor("wg_r", [16, E, 256], F32, kind="ExternalInput").ap()
    wu_r_h = nc.dram_tensor("wu_r", [16, E, 256], F32, kind="ExternalInput").ap()
    wd_r_h = nc.dram_tensor("wd_r", [16, 256, E], F32, kind="ExternalInput").ap()
    wg_s_h = nc.dram_tensor("wg_s", [2, E, 512], F32, kind="ExternalInput").ap()
    wu_s_h = nc.dram_tensor("wu_s", [2, E, 512], F32, kind="ExternalInput").ap()
    wd_s_h = nc.dram_tensor("wd_s", [2, 512, E], F32, kind="ExternalInput").ap()
    cost_h = nc.dram_tensor("cost", [128, S_LEN], BF16, kind="ExternalInput").ap()
    sint_h = nc.dram_tensor("sint", [128, S_LEN], BF16, kind="ExternalInput").ap()
    ident_h = nc.dram_tensor("ident", [128, 128], BF16, kind="ExternalInput").ap()
    identf_h = nc.dram_tensor("identf", [128, 128], F32, kind="ExternalInput").ap()
    rmat_h = nc.dram_tensor("rmat", [128, 128], BF16, kind="ExternalInput").ap()
    consts_h = nc.dram_tensor("consts", [128, 8], F32, kind="ExternalInput").ap()
    out_h = nc.dram_tensor("out", [S_LEN, E], F32, kind="ExternalOutput").ap()
    combd_h = nc.dram_tensor("combd", [16, S_LEN], BF16, kind="Internal").ap()

    tap_h = {}

    def mktap(name, shape, dtype):
        if name in taps:
            tap_h[name] = nc.dram_tensor("tap_" + name, shape, dtype,
                                         kind="ExternalOutput").ap()

    mktap("xnt", [128, 8 * S_LEN], BF16)
    mktap("qr0", [128, S_LEN], BF16)
    mktap("kr0", [128, S_LEN], BF16)
    mktap("vt0", [128, S_LEN], BF16)
    mktap("ot0", [128, S_LEN], BF16)
    mktap("acc1", [128, 16 * E], F32)
    mktap("comb", [128, 256], F32)
    mktap("accf", [128, 16 * E], F32)

    top = contextlib.ExitStack()
    with top:
        R = Rec(nc, top)
        sb = lambda name, shape, dtype: top.enter_context(nc.sbuf_tensor(name, shape, dtype))

        ACC = sb("ACC", [128, 16 * E], F32)
        XNT = sb("XNT", [128, 8 * S_LEN], BF16)
        CONSTS = sb("CONSTS", [128, 8], F32)
        IDENT = sb("IDENT", [128, 128], BF16)
        IDENTF = sb("IDENTF", [128, 128], F32)
        RMAT = sb("RMAT", [128, 128], BF16)
        ONES = sb("ONESB", [128, 1], BF16)
        COMBF = sb("COMBF", [16, S_LEN], BF16)
        for i, v in enumerate(CV):
            nc.const_aps.register(F32, v, CONSTS[:, i:i + 1])

        v_c = R.op("sy", lambda h: h.dma_start(CONSTS[:], consts_h[:]), inc="cs", n=16)
        v_c = R.op("sy", lambda h: h.dma_start(IDENT[:], ident_h[:]), inc="cs", n=16)
        v_c = R.op("sy", lambda h: h.dma_start(IDENTF[:], identf_h[:]), inc="cs", n=16)
        v_c = R.op("sy", lambda h: h.dma_start(RMAT[:], rmat_h[:]), inc="cs", n=16)
        v_ones = R.op("g", lambda h: h.memset(ONES[:], 1.0), inc="cgp")
        for e in ("t", "sc", "v", "g", "sy"):
            R.wait(e, "cs", v_c)
            R.wait(e, "cgp", v_ones)

        p0 = contextlib.ExitStack()

        def phase_p0_ln():
            psb0 = lambda n_, s_, d_: p0.enter_context(nc.psum_tensor(n_, s_, d_))
            sb0 = lambda n_, s_, d_: p0.enter_context(nc.sbuf_tensor(n_, s_, d_))
            XIN = [sb0(f"XIN{i}", [128, E], F32) for i in range(2)]
            XNM = [sb0(f"XNM{i}", [128, E], BF16) for i in range(2)]
            JUNK = [sb0(f"JUNK{i}", [128, E], BF16) for i in range(2)]
            MB = sb0("MB", [128, 16], F32)
            psT = [psb0(f"psT0{i}", [128, 1024], F32) for i in range(2)]

            v_ln = [0] * 16
            v_ev = [0] * 16
            v_tr = [0] * 16
            if w_units:
                w_alloc()
                w_units[0]()
            for tc in range(16):
                par = tc % 2
                if tc >= 2:
                    R.wait("sy", "cv", v_ln[tc - 2])
                v_x = R.op("sy", lambda h, tc=tc, par=par: h.dma_start(
                    XIN[par][:], x_h[tc * 128:(tc + 1) * 128, :]), inc="cs", n=16)
                mb = par * 8
                R.wait("v", "cs", v_x)
                vv = R.op("v", lambda h, par=par, mb=mb: h.reduce_sum(
                    MB[:, mb + 6:mb + 7], XIN[par][:], AX.X), inc="cv")
                R.wait("v", "cv", vv)
                vv = R.op("v", lambda h, mb=mb: h.tensor_scalar(
                    MB[:, mb + 0:mb + 1], MB[:, mb + 6:mb + 7], CV[0], None,
                    ALU.mult), inc="cv")
                R.wait("sc", "cs", v_x)
                if tc >= 2:
                    R.wait("sc", "cv", v_ln[tc - 2])
                vs = R.op("sc", lambda h, par=par, mb=mb: h.activation(
                    JUNK[par][:], XIN[par][:], AF.Square,
                    accum_out=MB[:, mb + 1:mb + 2]), inc="cc")
                R.wait("v", "cv", vv)
                vv = R.op("v", lambda h, mb=mb: h.tensor_tensor(
                    MB[:, mb + 2:mb + 3], MB[:, mb + 0:mb + 1],
                    MB[:, mb + 0:mb + 1], ALU.mult), inc="cv")
                R.wait("v", "cc", vs)
                R.wait("v", "cv", vv)
                vv = R.op("v", lambda h, mb=mb: h.scalar_tensor_tensor(
                    MB[:, mb + 3:mb + 4], MB[:, mb + 1:mb + 2], CV[0],
                    MB[:, mb + 2:mb + 3], ALU.mult, ALU.subtract), inc="cv")
                R.wait("sc", "cv", vv)
                vs = R.op("sc", lambda h, mb=mb: h.activation(
                    MB[:, mb + 5:mb + 6], MB[:, mb + 3:mb + 4], AF.Sqrt,
                    bias=CV[1]), inc="cc")
                R.wait("v", "cc", vs)
                vv = R.op("v", lambda h, mb=mb: h.reciprocal(
                    MB[:, mb + 4:mb + 5], MB[:, mb + 5:mb + 6]), inc="cv")
                R.wait("v", "cv", vv)
                if tc >= 2:
                    R.wait("v", "ct", v_tr[tc - 2])
                v_ln[tc] = R.op("v", lambda h, tc=tc, par=par, mb=mb: h.tensor_scalar(
                    XNM[par][:], XIN[par][:], MB[:, mb + 0:mb + 1],
                    MB[:, mb + 4:mb + 5], ALU.subtract, ALU.mult), inc="cv")
                R.wait("t", "cv", v_ln[tc])
                if tc >= 2:
                    R.wait("t", "cc", v_ev[tc - 2])
                for ec in range(8):
                    R.op("t", lambda h, ec=ec, tc=tc, par=par: h.matmul(
                        psT[par][:, ec * 128:(ec + 1) * 128],
                        XNM[par][:, ec * 128:(ec + 1) * 128],
                        IDENT[:], start=True, stop=True))
                v_tr[tc] = R.op("t", None, inc="ct")
                R.wait("sc", "ct", v_tr[tc])
                v_ev[tc] = R.op("sc", lambda h, tc=tc, par=par: h.activation(
                    bass.AP(XNT, tc * 128, [[8 * S_LEN, 128], [S_LEN, 8], [1, 128]]),
                    bass.AP(psT[par], 0, [[1024, 128], [128, 8], [1, 128]]),
                    AF.Copy), inc="cc")
                for wu in w_units_for_chunk(tc):
                    wu()
            if "xnt" in taps:
                R.wait("sy", "cc", v_ev[15])
                R.op("sy", lambda h: h.dma_start(tap_h["xnt"], XNT[:]), inc="cs", n=16)
            return v_ev

        pA = contextlib.ExitStack()
        sbA = lambda name, shape, dtype: pA.enter_context(nc.sbuf_tensor(name, shape, dtype))
        WOTP = sbA("WOTP", [128, 8 * E], BF16)
        W123T = {nm: sbA(nm + "T", [128, 8 * E], BF16) for nm in ("w1", "w2", "w3")}
        pw = contextlib.ExitStack()
        wst = {"cc_ev": 0, "v_tdone": [0] * 4, "v_wl": 0}
        wbuf = {}

        def w_alloc():
            wbuf["WNAT"] = pw.enter_context(nc.sbuf_tensor("WNAT0", [128, 8 * E], BF16))
            wbuf["psTW"] = pw.enter_context(nc.psum_tensor("psTW", [128, 1024], F32))

        def w_dma_unit(i, nm):
            def go(i=i, nm=nm):
                if i >= 1:
                    R.wait("g", "ct", wst["v_tdone"][i - 1])
                wst["v_wl"] = R.op("g", lambda h, nm=nm: h.dma_start(
                    wbuf["WNAT"][:].rearrange("p (pc e) -> p pc e", pc=8),
                    w_h[nm].rearrange("(pc p) e -> p pc e", p=128)),
                    inc="cg", n=16)
            return go

        def w_blk_unit(i, nm, blk):
            def go(i=i, nm=nm, blk=blk):
                R.wait("t", "cg", wst["v_wl"])
                R.wait("t", "cc", wst["cc_ev"])
                for j in range(8):
                    if nm != "wo":
                        R.op("t", lambda h, blk=blk, j=j: h.matmul(
                            wbuf["psTW"][:, j * 128:(j + 1) * 128],
                            wbuf["WNAT"][:, j * E + blk * 128: j * E + blk * 128 + 128],
                            IDENT[:], start=True, stop=True))
                    else:
                        R.op("t", lambda h, blk=blk, j=j: h.matmul(
                            wbuf["psTW"][:, j * 128:(j + 1) * 128],
                            wbuf["WNAT"][:, j * E + blk: j * E + blk + 1017:8],
                            IDENT[:], start=True, stop=True))
                vt = R.op("t", None, inc="ct")
                R.wait("sc", "ct", vt)
                if nm != "wo":
                    wst["cc_ev"] = R.op("sc", lambda h, nm=nm, blk=blk: h.activation(
                        W123T[nm][:, blk * E:(blk + 1) * E], wbuf["psTW"][:], AF.Copy),
                        inc="cc")
                else:
                    wst["cc_ev"] = R.op("sc", lambda h, blk=blk: h.activation(
                        WOTP[:, blk * E:(blk + 1) * E], wbuf["psTW"][:], AF.Copy),
                        inc="cc")
                if blk == 7:
                    wst["v_tdone"][i] = R.vals["ct"]
            return go

        w_units = []
        if stage >= 2:
            for i, nm in enumerate(["w1", "w2", "w3", "wo"]):
                w_units.append(w_dma_unit(i, nm))
                for blk in range(8):
                    w_units.append(w_blk_unit(i, nm, blk))
        w_sched = {}
        for ui in range(1, len(w_units)):
            chunk = 2 + ((ui - 1) * 14) // max(len(w_units) - 1, 1)
            w_sched.setdefault(min(chunk, 15), []).append(w_units[ui])

        def w_units_for_chunk(tc):
            return w_sched.get(tc, [])

        def phase_w():
            R.wait("t", "cc", wst["cc_ev"])
        def phase_router():
            R.wait("g", "ct", R.vals["ct"])
            with contextlib.ExitStack() as p2r:
                psbr = lambda n_, s_, d_: p2r.enter_context(nc.psum_tensor(n_, s_, d_))
                sbr = lambda n_, s_, d_: p2r.enter_context(nc.sbuf_tensor(n_, s_, d_))
                RWB = sbr("RWB", [128, 8 * 16], BF16)
                LGF = sbr("LGF", [16, S_LEN], F32)
                LGT = sbr("LGT", [128, 256], F32)
                COMBT = sbr("COMBT", [128, 256], F32)
                RS = sbr("RS", [128, 16], F32)
                L2 = sbr("L2", [128, 32], F32)
                MS1 = sbr("MS1", [128, 32], F32)
                MS2 = sbr("MS2", [128, 32], F32)
                psL = psbr("psL", [16, 1024], F32)
                psLT = psbr("psLT", [128, 256], F32)
                v_rw = R.op("g", lambda h: h.dma_start(
                    RWB[:].rearrange("e (ec x) -> e ec x", ec=8),
                    rw_h.rearrange("(ec e) x -> e ec x", e=128)), inc="cg", n=16)
                R.wait("t", "cg", v_rw)
                v_lg = 0
                for hf2 in range(2):
                    R.wait("t", "cc", v_lg)
                    for ec in range(8):
                        for hf in range(2):
                            R.op("t", lambda h, ec=ec, hf=hf, hf2=hf2: h.matmul(
                                psL[:, hf * 512:(hf + 1) * 512],
                                RWB[:, ec * 16:(ec + 1) * 16],
                                XNT[:, ec * S_LEN + hf2 * 1024 + hf * 512:
                                    ec * S_LEN + hf2 * 1024 + hf * 512 + 512],
                                start=(ec == 0), stop=(ec == 7)))
                    vt = R.op("t", None, inc="ct")
                    R.wait("sc", "ct", vt)
                    v_lg = R.op("sc", lambda h, hf2=hf2: h.activation(
                        LGF[:, hf2 * 1024:(hf2 + 1) * 1024], psL[:], AF.Copy),
                        inc="cc")
                R.wait("t", "cc", v_lg)
                for tc in range(16):
                    R.op("t", lambda h, tc=tc: h.matmul(
                        psLT[:, tc * 16:(tc + 1) * 16],
                        LGF[:, tc * 128:(tc + 1) * 128], IDENTF[0:16, 0:16], start=True, stop=True))
                vt = R.op("t", None, inc="ct")
                R.wait("sc", "ct", vt)
                v_lgt = R.op("sc", lambda h: h.activation(LGT[:], psLT[:], AF.Copy),
                             inc="cc")
                R.wait("v", "cc", v_lgt)
                for tc in range(16):
                    rs = (tc % 2) * 8
                    mo = (tc % 2) * 16
                    vv = R.op("v", lambda h, tc=tc, rs=rs: h.reduce_max(
                        RS[:, rs + 0:rs + 1], LGT[:, tc * 16:(tc + 1) * 16], AX.X),
                        inc="cv")
                    R.wait("v", "cv", vv)
                    vv = R.op("v", lambda h, tc=tc, rs=rs, mo=mo: h.tensor_scalar(
                        MS1[:, mo:mo + 16], LGT[:, tc * 16:(tc + 1) * 16],
                        RS[:, rs + 0:rs + 1], None, ALU.is_equal), inc="cv")
                    R.wait("v", "cv", vv)
                    vv = R.op("v", lambda h, tc=tc, mo=mo: h.scalar_tensor_tensor(
                        L2[:, mo:mo + 16], MS1[:, mo:mo + 16], CV[2],
                        LGT[:, tc * 16:(tc + 1) * 16], ALU.mult, ALU.add), inc="cv")
                    R.wait("v", "cv", vv)
                    vv = R.op("v", lambda h, rs=rs, mo=mo: h.reduce_max(
                        RS[:, rs + 1:rs + 2], L2[:, mo:mo + 16], AX.X), inc="cv")
                    R.wait("v", "cv", vv)
                    vv = R.op("v", lambda h, rs=rs, mo=mo: h.tensor_scalar(
                        MS2[:, mo:mo + 16], L2[:, mo:mo + 16], RS[:, rs + 1:rs + 2],
                        None, ALU.is_equal), inc="cv")
                    R.wait("v", "cv", vv)
                    vv = R.op("v", lambda h, rs=rs: h.tensor_tensor(
                        RS[:, rs + 2:rs + 3], RS[:, rs + 1:rs + 2],
                        RS[:, rs + 0:rs + 1], ALU.subtract), inc="cv")
                    R.wait("sc", "cv", vv)
                    vs = R.op("sc", lambda h, rs=rs: h.activation(
                        RS[:, rs + 4:rs + 5], RS[:, rs + 2:rs + 3], AF.Sigmoid),
                        inc="cc")
                    R.wait("sc", "cc", vs)
                    vs = R.op("sc", lambda h, rs=rs: h.activation(
                        RS[:, rs + 3:rs + 4], RS[:, rs + 2:rs + 3], AF.Sigmoid,
                        scale=CV[3]), inc="cc")
                    R.wait("v", "cc", vs)
                    vv = R.op("v", lambda h, tc=tc, rs=rs, mo=mo: h.tensor_scalar(
                        COMBT[:, tc * 16:(tc + 1) * 16], MS1[:, mo:mo + 16],
                        RS[:, rs + 3:rs + 4], None, ALU.mult), inc="cv")
                    R.wait("v", "cv", vv)
                    R.op("v", lambda h, tc=tc, rs=rs, mo=mo: h.scalar_tensor_tensor(
                        COMBT[:, tc * 16:(tc + 1) * 16], MS2[:, mo:mo + 16],
                        RS[:, rs + 4:rs + 5], COMBT[:, tc * 16:(tc + 1) * 16],
                        ALU.mult, ALU.add), inc="cv")
                v_combt = R.vals["cv"]
                if "comb" in taps:
                    R.wait("sy", "cv", v_combt)
                    R.op("sy", lambda h: h.dma_start(tap_h["comb"], COMBT[:]),
                         inc="cs", n=16)
                R.wait("t", "cv", v_combt)
                v_combf = 0
                for r2 in range(2):
                    R.wait("t", "cc", v_combf)
                    for j in range(8):
                        tc = r2 * 8 + j
                        R.op("t", lambda h, tc=tc, j=j: h.matmul(
                            psL[:, j * 128:(j + 1) * 128],
                            COMBT[:, tc * 16:(tc + 1) * 16], IDENTF[:],
                            start=True, stop=True))
                    vt = R.op("t", None, inc="ct")
                    R.wait("sc", "ct", vt)
                    v_combf = R.op("sc", lambda h, r2=r2: h.activation(
                        COMBF[:, r2 * 1024:(r2 + 1) * 1024], psL[:], AF.Copy),
                        inc="cc")
                R.wait("g", "cc", v_combf)
                v_combd = R.op("g", lambda h: h.dma_start(combd_h, COMBF[:]),
                               inc="cg", n=16)
            return v_combd
        if stage >= 1:
            _v_ev_saved = phase_p0_ln()
        if stage >= 2:
            phase_w()
        pw.close()
        _v_combd = phase_router() if (do_ffn and stage >= 4) else 0
        if stage < 1:
            vz = R.op("v", lambda h: h.memset(XNT[:], 0.0), inc="cv")
            R.wait("t", "cv", vz)
        p0.close()
        R.barrier()

        if do_attn and stage >= 3:
            COST = sbA("COST", [128, S_LEN], BF16)
            SINT = sbA("SINT", [128, S_LEN], BF16)
            R.op("sy", lambda h: h.dma_start(COST[:], cost_h[:]), inc="cs", n=16)
            v_tab = R.op("sy", lambda h: h.dma_start(SINT[:], sint_h[:]), inc="cs", n=16)

            QHB = sbA("QHB", [128, S_LEN], BF16)
            KHB = sbA("KHB", [128, S_LEN], BF16)
            QR = sbA("QR", [128, S_LEN], BF16)
            KR = sbA("KR", [128, S_LEN], BF16)
            VT = sbA("VT", [128, S_LEN], BF16)
            T1 = sbA("T1", [128, 512], BF16)
            T2 = sbA("T2", [128, 512], BF16)
            PT = [sbA(f"PT{i}", [128, 1024], BF16) for i in range(3)]
            OSB = KHB
            OT = QHB
            ASB = OSB
            RSROW = sbA("RSROW", [1, S_LEN], BF16)
            RECT = sbA("RECT", [128, 16], F32)

            v_rope_prev = 0
            v_accev_prev = 0
            v_exp_carry = [0, 0]
            v_pv_carry3 = [0, 0, 0]
            v_oe = 0
            for head in range(n_heads):
                with contextlib.ExitStack() as p1a:
                    psba = lambda n_, s_, d_: p1a.enter_context(nc.psum_tensor(n_ + f"_{head}", s_, d_))
                    psQK = psba("psQK", [128, S_LEN], F32)
                    psV = psba("psV", [128, 1024], F32)
                    psSH = psba("psSH", [128, 1024], F32)

                    def proj_qk(wt, dst, head=head):
                        for pc in range(8):
                            for ec in range(8):
                                R.op("t", lambda h, pc=pc, ec=ec, wt=wt, head=head: h.matmul(
                                    psQK[:, pc * TH:(pc + 1) * TH],
                                    W123T[wt][:, ec * E + pc * 128: ec * E + pc * 128 + 128],
                                    XNT[:, ec * S_LEN + head * TH: ec * S_LEN + head * TH + TH],
                                    start=(ec == 0), stop=(ec == 7)))
                        vt = R.op("t", None, inc="ct")
                        R.wait("sc", "ct", vt)
                        return R.op("sc", lambda h, dst=dst: h.activation(
                            dst[:], psQK[:], AF.Copy), inc="cc")

                    R.wait("sc", "cv", v_rope_prev)
                    v_q = proj_qk("w1", QHB)
                    R.wait("t", "cc", v_q)
                    v_k = proj_qk("w2", KHB)

                    v_vev = 0
                    for blk in range(2):
                        R.wait("t", "cc", v_vev)
                        for ec in range(8):
                            for hf in range(2):
                                R.op("t", lambda h, ec=ec, hf=hf, blk=blk, head=head: h.matmul(
                                    psV[:, hf * 512:(hf + 1) * 512],
                                    XNT[:, ec * S_LEN + head * TH + blk * 128:
                                        ec * S_LEN + head * TH + blk * 128 + 128],
                                    W123T["w3"][:, ec * E + hf * 512: ec * E + hf * 512 + 512],
                                    start=(ec == 0), stop=(ec == 7)))
                        vt = R.op("t", None, inc="ct")
                        R.wait("sc", "ct", vt)
                        v_vev = R.op("sc", lambda h, blk=blk: h.activation(
                            bass.AP(VT, blk * 128, [[S_LEN, 128], [TH, 8], [1, 128]]),
                            bass.AP(psV, 0, [[1024, 128], [128, 8], [1, 128]]),
                            AF.Copy), inc="cc")

                    def rope(src, dst, v_src):
                        vq = 0
                        for j in range(4):
                            par = j % 2
                            R.wait("t", "cc", v_src)
                            R.wait("t", "cv", vq)
                            vt = R.op("t", lambda h, j=j, par=par, src=src: h.matmul(
                                psSH[:, par * 512:(par + 1) * 512], RMAT[:],
                                src[:, j * 512:(j + 1) * 512], start=True, stop=True),
                                inc="ct")
                            R.wait("v", "cs", v_tab)
                            R.wait("v", "cc", v_src)
                            R.op("v", lambda h, j=j, src=src: h.tensor_tensor(
                                T1[:], src[:, j * 512:(j + 1) * 512],
                                COST[:, j * 512:(j + 1) * 512], ALU.mult), inc="cv")
                            R.wait("v", "ct", vt)
                            vv = R.op("v", lambda h, j=j, par=par: h.tensor_tensor(
                                T2[:], psSH[:, par * 512:(par + 1) * 512],
                                SINT[:, j * 512:(j + 1) * 512], ALU.mult), inc="cv")
                            R.wait("v", "cv", vv)
                            vq = R.op("v", lambda h, j=j, dst=dst: h.tensor_tensor(
                                dst[:, j * 512:(j + 1) * 512], T1[:], T2[:], ALU.add),
                                inc="cv")
                        return vq

                    v_qr = rope(QHB, QR, v_q)
                    v_kr = rope(KHB, KR, v_k)
                    v_rope_prev = v_kr

                if head == 0 and "qr0" in taps:
                    R.wait("sy", "cv", v_qr)
                    R.op("sy", lambda h: h.dma_start(tap_h["qr0"], QR[:]), inc="cs", n=16)
                    R.wait("sy", "cv", v_kr)
                    R.op("sy", lambda h: h.dma_start(tap_h["kr0"], KR[:]), inc="cs", n=16)
                if head == 0 and "vt0" in taps:
                    R.wait("sy", "cc", R.vals["cc"])
                    R.op("sy", lambda h: h.dma_start(tap_h["vt0"], VT[:]), inc="cs", n=16)

                with contextlib.ExitStack() as p1b:
                    psbb = lambda n_, s_, d_: p1b.enter_context(nc.psum_tensor(n_ + f"_{head}", s_, d_))
                    psS = [psbb(f"psS{i}", [128, 1024], F32) for i in range(2)]
                    psO = psbb("psO", [128, 1024], F32)
                    psR = psbb("psR", [1, 1024], F32)
                    R.wait("t", "cv", v_qr)
                    R.wait("t", "cv", v_kr)
                    v_exp = [0] * 32
                    v_pv = [0] * 32
                    v_s = [0] * 32
                    for kk in range(34):
                        if kk < 32:
                            qh, kt = kk // 16, kk % 16
                            par = kk % 2
                            R.wait("t", "cc",
                                   v_exp[kk - 2] if kk >= 2 else v_exp_carry[kk])
                            for hf in range(2):
                                R.op("t", lambda h, kt=kt, par=par, qh=qh, hf=hf: h.matmul(
                                    psS[par][:, hf * 512:(hf + 1) * 512],
                                    KR[:, kt * 128:(kt + 1) * 128],
                                    QR[:, qh * 1024 + hf * 512: qh * 1024 + hf * 512 + 512],
                                    start=True, stop=True))
                            v_s[kk] = R.op("t", None, inc="ct")
                            R.wait("sc", "ct", v_s[kk])
                            R.wait("sc", "ct",
                                   v_pv[kk - 3] if kk >= 3 else v_pv_carry3[kk])
                            v_exp[kk] = R.op("sc", lambda h, par3=kk % 3, par=par: h.activation(
                                PT[par3][:], psS[par][:], AF.Exp,
                                bias=SOFTMAX_SHIFT, scale=INV_SQRT8), inc="cc")
                        if kk >= 2:
                            pk = kk - 2
                            qh, kt = pk // 16, pk % 16
                            par = pk % 2
                            R.wait("t", "cc", v_exp[pk])
                            if kt == 0:
                                R.wait("t", "cc", v_oe)
                            par3 = pk % 3
                            for hf in range(2):
                                R.op("t", lambda h, kt=kt, par3=par3, hf=hf: h.matmul(
                                    psO[:, hf * 512:(hf + 1) * 512],
                                    VT[:, kt * 128:(kt + 1) * 128],
                                    PT[par3][:, hf * 512:(hf + 1) * 512],
                                    start=(kt == 0), stop=(kt == 15),
                                    skip_group_check=True))
                            for hf in range(2):
                                R.op("t", lambda h, kt=kt, par3=par3, hf=hf: h.matmul(
                                    psR[:, hf * 512:(hf + 1) * 512],
                                    ONES[:],
                                    PT[par3][:, hf * 512:(hf + 1) * 512],
                                    start=(kt == 0), stop=(kt == 15),
                                    skip_group_check=True))
                            v_pv[pk] = R.op("t", None, inc="ct")
                            if kt == 15:
                                R.wait("sc", "ct", v_pv[pk])
                                R.op("sc", lambda h, qh=qh: h.activation(
                                    RSROW[0:1, qh * 1024:(qh + 1) * 1024], psR[0:1, :],
                                    AF.Copy), inc="cc")
                                v_oe = R.op("sc", lambda h, qh=qh: h.activation(
                                    OSB[:, qh * 1024:(qh + 1) * 1024], psO[:],
                                    AF.Copy), inc="cc")
                    v_exp_carry = [v_exp[30], v_exp[31]]
                    v_pv_carry3 = [v_pv[31]] * 3

                with contextlib.ExitStack() as p1c:
                    psbc = lambda n_, s_, d_: p1c.enter_context(nc.psum_tensor(n_ + f"_{head}", s_, d_))
                    psAm = psbc("psAm", [128, 1024], F32)
                    psTT = psbc("psTT", [128, 1024], F32)
                    R.wait("t", "cc", v_oe)
                    cc_ev = v_accev_prev
                    for j in range(16):
                        R.op("t", lambda h, j=j: h.matmul(
                            psAm[:, j:j + 1],
                            RSROW[0:1, j * 128:(j + 1) * 128], IDENT[0:1, 0:1], start=True, stop=True))
                    vt_rs = R.op("t", None, inc="ct")
                    R.wait("v", "ct", vt_rs)
                    v_rec = R.op("v", lambda h: h.reciprocal(
                        RECT[:], psAm[:, 0:16]), inc="cv")
                    for r in range(2):
                        R.wait("t", "cc", cc_ev)
                        for j in range(8):
                            R.op("t", lambda h, r=r, j=j: h.matmul(
                                psTT[:, j * 128:(j + 1) * 128],
                                OSB[:, (r * 8 + j) * 128:(r * 8 + j) * 128 + 128],
                                IDENT[:], start=True, stop=True))
                        vt = R.op("t", None, inc="ct")
                        R.wait("sc", "ct", vt)
                        R.wait("sc", "cv", v_rec)
                        for j in range(8):
                            kt2 = r * 8 + j
                            cc_ev = R.op("sc", lambda h, kt2=kt2, j=j: h.activation(
                                OT[:, kt2 * 128:(kt2 + 1) * 128],
                                psTT[:, j * 128:(j + 1) * 128], AF.Copy,
                                scale=RECT[:, kt2:kt2 + 1]), inc="cc")
                    if head == 0 and "ot0" in taps:
                        R.wait("sy", "cc", cc_ev)
                        R.op("sy", lambda h: h.dma_start(tap_h["ot0"], OT[:]), inc="cs",
                             n=16)
                    for u in range(2):
                        R.wait("t", "cc", cc_ev)
                        R.wait("t", "cv", v_rec)
                        for i, kt in enumerate(range(u, 16, 2)):
                            for hf in range(2):
                                R.op("t", lambda h, kt=kt, i=i, hf=hf: h.matmul(
                                    psAm[:, hf * 512:(hf + 1) * 512],
                                    OT[:, kt * 128:(kt + 1) * 128],
                                    WOTP[:, (kt // 2) * E + hf * 512:
                                         (kt // 2) * E + hf * 512 + 512],
                                    start=(i == 0), stop=(i == 7)))
                        vt = R.op("t", None, inc="ct")
                        R.wait("sc", "ct", vt)
                        v_asb = R.op("sc", lambda h: h.activation(
                            ASB[:, 0:1024], psAm[:], AF.Copy), inc="cc")
                        R.wait("t", "cc", v_asb)
                        for cc_i in range(8):
                            R.op("t", lambda h, cc_i=cc_i: h.matmul(
                                psTT[:, cc_i * 128:(cc_i + 1) * 128],
                                ASB[:, cc_i * 128:(cc_i + 1) * 128], IDENT[:], start=True, stop=True))
                        vt = R.op("t", None, inc="ct")
                        R.wait("sc", "ct", vt)
                        cc_ev = R.op("sc", lambda h, u=u, head=head: h.activation(
                            bass.AP(ACC, (u * 8) * E + head * 128,
                                    [[16 * E, 128], [E, 8], [1, 128]]),
                            bass.AP(psTT, 0, [[1024, 128], [128, 8], [1, 128]]),
                            AF.Copy), inc="cc")
                    v_accev_prev = cc_ev
            if "acc1" in taps:
                R.wait("sy", "cc", R.vals["cc"])
                R.op("sy", lambda h: h.dma_start(tap_h["acc1"], ACC[:]), inc="cs", n=16)
        if not (do_attn and stage >= 3):
            vz = R.op("v", lambda h: h.memset(ACC[:], 0.0), inc="cv")
            R.wait("t", "cv", vz)
        R.barrier()

        pA.close()

        if do_ffn and stage >= 4:
            pass
            R.barrier()

            with contextlib.ExitStack() as p2e:
                psbe = lambda n_, s_, d_: p2e.enter_context(nc.psum_tensor(n_, s_, d_))
                sbe = lambda n_, s_, d_: p2e.enter_context(nc.sbuf_tensor(n_, s_, d_))
                psG1 = psbe("psG1", [128, 1024], F32)
                psG2 = psbe("psG2", [128, 1024], F32)
                psD = [psbe(f"psD{i}", [128, 1024], F32) for i in range(2)]
                HARE = sbe("HARE", [128, 8 * S_LEN], BF16)
                SG = sbe("SG", [128, S_LEN], BF16)
                TFFN = sbe("TFFN", [128, S_LEN], BF16)
                WGA = sbe("WGA", [128, 8 * 512], BF16)
                WUA = sbe("WUA", [128, 8 * 512], BF16)
                WDA = sbe("WDA", [128, 8 * 1024], BF16)
                CB = [sbe(f"CB{i}", [128, S_LEN], BF16) for i in range(2)]

                groups = [("r", [4 * g + i for i in range(4)]) for g in range(4)]
                groups.append(("s", [0, 1]))
                XI = [XNT[:, 0:2048].bitcast(F32), XNT[:, 2048:4096].bitcast(F32)]
                OU = [XNT[:, 4096:6144].bitcast(F32), XNT[:, 6144:8192].bitcast(F32)]
                v_ad3 = [0] * 16
                v_st3 = [0] * 16
                v_wg_used = {}
                v_wd_used = {}
                v_cb_used = {}
                v_prev_add = [0, 0]
                v_silu_prev = 0
                v_h_prev = 0
                for gi, (gkind, gl) in enumerate(groups):
                    shared = gkind == "s"
                    nmc = 4 if shared else 2
                    mid = 512 if shared else 256
                    ndslots = 2 if shared else 4
                    v_h = []
                    v_ld_all = 0
                    for li, e in enumerate(gl):
                        sl_g = 0 if shared else li % 2
                        sl_d = li % ndslots
                        wg_off = sl_g * 8 * 256 if not shared else 0
                        wd_off = sl_d * (4 * E if shared else 2 * E)
                        R.wait("g", "cv", v_wg_used.get(sl_g, 0))
                        src_g = (wg_s_h if shared else wg_r_h)[e].rearrange(
                            "(ec e2) m -> e2 ec m", e2=128)
                        v_lg_ = R.op("g", lambda h, src=src_g, off=wg_off, mid=mid:
                                     h.dma_start(WGA[:, off:off + 8 * mid].rearrange(
                                         "p (ec m) -> p ec m", ec=8), src),
                                     inc="cg", n=16)
                        src_u = (wu_s_h if shared else wu_r_h)[e].rearrange(
                            "(ec e2) m -> e2 ec m", e2=128)
                        v_lu_ = R.op("g", lambda h, src=src_u, off=wg_off, mid=mid:
                                     h.dma_start(WUA[:, off:off + 8 * mid].rearrange(
                                         "p (ec m) -> p ec m", ec=8), src),
                                     inc="cg", n=16)
                        R.wait("g", "cv", v_wd_used.get(sl_d, 0))
                        src_d = (wd_s_h if shared else wd_r_h)[e].rearrange(
                            "(mc m) x -> m mc x", m=128)
                        v_ld_all = R.op("g", lambda h, src=src_d, off=wd_off, nmc=nmc:
                                        h.dma_start(WDA[:, off:off + nmc * E].rearrange(
                                            "p (mc x) -> p mc x", mc=nmc), src),
                                        inc="cg", n=16)
                        v_cb_ = 0
                        if not shared:
                            R.wait("g", "cv", v_cb_used.get(sl_g, 0))
                            v_cb_ = R.op("g", lambda h, e=e, sl=sl_g: h.dma_start(
                                CB[sl][:], combd_h[e:e + 1, :].partition_broadcast(128)),
                                inc="cg", n=16)
                        for mc in range(nmc):
                            for th in range(2):
                                R.wait("t", "cg", v_lg_)
                                R.wait("t", "cc", v_silu_prev)
                                for ec in range(8):
                                    for hf in range(2):
                                        R.op("t", lambda h, ec=ec, hf=hf, mc=mc, th=th,
                                             off=wg_off, mid=mid: h.matmul(
                                            psG1[:, hf * 512:(hf + 1) * 512],
                                            WGA[:, off + ec * mid + mc * 128:
                                                off + ec * mid + mc * 128 + 128],
                                            XNT[:, ec * S_LEN + th * 1024 + hf * 512:
                                                ec * S_LEN + th * 1024 + hf * 512 + 512],
                                            start=(ec == 0), stop=(ec == 7)))
                                vt_g = R.op("t", None, inc="ct")
                                R.wait("sc", "ct", vt_g)
                                R.wait("sc", "cv", v_h_prev)
                                v_silu_prev = R.op("sc", lambda h, th=th: h.activation(
                                    SG[:, th * 1024:(th + 1) * 1024], psG1[:],
                                    AF.Silu), inc="cc")
                                R.wait("t", "cg", v_lu_)
                                R.wait("t", "cv", v_h_prev)
                                for ec in range(8):
                                    for hf in range(2):
                                        R.op("t", lambda h, ec=ec, hf=hf, mc=mc, th=th,
                                             off=wg_off, mid=mid: h.matmul(
                                            psG2[:, hf * 512:(hf + 1) * 512],
                                            WUA[:, off + ec * mid + mc * 128:
                                                off + ec * mid + mc * 128 + 128],
                                            XNT[:, ec * S_LEN + th * 1024 + hf * 512:
                                                ec * S_LEN + th * 1024 + hf * 512 + 512],
                                            start=(ec == 0), stop=(ec == 7)))
                                vt_u = R.op("t", None, inc="ct")
                                R.wait("v", "ct", vt_u)
                                R.wait("v", "cc", v_silu_prev)
                                hslot = li * nmc + mc
                                base = hslot * S_LEN + th * 1024
                                if not shared:
                                    R.wait("v", "cg", v_cb_)
                                    vv = R.op("v", lambda h, th=th: h.tensor_tensor(
                                        TFFN[:, 0:1024], SG[:, th * 1024:(th + 1) * 1024],
                                        psG2[:], ALU.mult), inc="cv")
                                    R.wait("v", "cv", vv)
                                    vv = R.op("v", lambda h, base=base, th=th, sl=sl_g:
                                              h.tensor_tensor(
                                        HARE[:, base:base + 1024], TFFN[:, 0:1024],
                                        CB[sl][:, th * 1024:(th + 1) * 1024],
                                        ALU.mult), inc="cv")
                                else:
                                    vv = R.op("v", lambda h, base=base, th=th:
                                              h.tensor_tensor(
                                        HARE[:, base:base + 1024],
                                        SG[:, th * 1024:(th + 1) * 1024],
                                        psG2[:], ALU.mult), inc="cv")
                                v_h_prev = vv
                                v_h.append(vv)
                        v_wg_used[sl_g] = v_h_prev
                        if not shared:
                            v_cb_used[sl_g] = v_h_prev
                    v_hall = max(v_h)
                    R.wait("t", "cv", v_hall)
                    R.wait("t", "cg", v_ld_all)
                    v_add = [0] * 16
                    for tc in range(16):
                        par = tc % 2
                        prev = v_add[tc - 2] if tc >= 2 else v_prev_add[par]
                        R.wait("t", "cv", prev)
                        first = True
                        nexp = len(gl)
                        for li in range(nexp):
                            sl_d = li % ndslots
                            wd_off2 = sl_d * (4 * E if shared else 2 * E)
                            for mc in range(nmc):
                                hslot = li * nmc + mc
                                last = (li == nexp - 1 and mc == nmc - 1)
                                for hf in range(2):
                                    R.op("t", lambda h, tc=tc, par=par, hslot=hslot,
                                         off=wd_off2, mc=mc, first=first, hf=hf, last=last:
                                         h.matmul(
                                        psD[par][:, hf * 512:(hf + 1) * 512],
                                        HARE[:, hslot * S_LEN + tc * 128:
                                             hslot * S_LEN + tc * 128 + 128],
                                        WDA[:, off + mc * E + hf * 512:
                                            off + mc * E + hf * 512 + 512],
                                        start=first, stop=last,
                                        skip_group_check=True))
                                first = False
                        vt = R.op("t", None, inc="ct")
                        R.wait("v", "ct", vt)
                        v_add[tc] = R.op("v", lambda h, tc=tc, par=par: h.tensor_tensor(
                            ACC[:, tc * E:(tc + 1) * E], psD[par][:],
                            ACC[:, tc * E:(tc + 1) * E], ALU.add), inc="cv")
                        if shared:
                            if tc < 2:
                                R.wait("sy", "cv", v_hall)
                            else:
                                R.wait("sy", "cv", v_ad3[tc - 2])
                                R.wait("sy", "cs", v_st3[tc - 2])
                            v_xl = R.op("sy", lambda h, tc=tc, par=par: h.dma_start(
                                XI[par], x_h[tc * 128:(tc + 1) * 128, :]),
                                inc="cs", n=16)
                            R.wait("v", "cs", v_xl)
                            v_ad3[tc] = R.op("v", lambda h, tc=tc, par=par:
                                             h.tensor_tensor(
                                OU[par], ACC[:, tc * E:(tc + 1) * E], XI[par],
                                ALU.add), inc="cv")
                            R.wait("sc", "cv", v_ad3[tc])
                            v_st3[tc] = R.op("sc", lambda h, tc=tc, par=par:
                                             h.dma_start(
                                out_h[tc * 128:(tc + 1) * 128, :], OU[par]),
                                inc="cs", n=16)
                    v_prev_add = [v_add[14], v_add[15]]
                    for sl in range(ndslots):
                        v_wd_used[sl] = v_add[15]
                    v_add_final = list(v_add)
                    v_hall_last = v_hall
        if (not do_ffn or stage < 4) or "accf" in taps:
            R.barrier()
            v_add_final = [R.vals["cv"]] * 16
            v_hall_last = R.vals["cv"]
        if "accf" in taps:
            R.op("sy", lambda h: h.dma_start(tap_h["accf"], ACC[:]), inc="cs", n=16)

        if not (do_ffn and stage >= 4):
            XI = [XNT[:, 0:2048].bitcast(F32), XNT[:, 2048:4096].bitcast(F32)]
            OU = [XNT[:, 4096:6144].bitcast(F32), XNT[:, 6144:8192].bitcast(F32)]
            v_st = [0] * 16
            v_ad = [0] * 16
            for tc in range(16):
                par = tc % 2
                if tc < 2:
                    R.wait("sy", "cv", v_hall_last)
                if tc >= 2:
                    R.wait("sy", "cv", v_ad[tc - 2])
                    R.wait("sy", "cs", v_st[tc - 2])
                v_xl = R.op("sy", lambda h, tc=tc, par=par: h.dma_start(
                    XI[par], x_h[tc * 128:(tc + 1) * 128, :]), inc="cs", n=16)
                R.wait("v", "cs", v_xl)
                R.wait("v", "cv", v_add_final[tc])
                v_ad[tc] = R.op("v", lambda h, tc=tc, par=par: h.tensor_tensor(
                    OU[par], ACC[:, tc * E:(tc + 1) * E], XI[par], ALU.add),
                    inc="cv")
                R.wait("sc", "cv", v_ad[tc])
                v_st[tc] = R.op("sc", lambda h, tc=tc, par=par: h.dma_start(
                    out_h[tc * 128:(tc + 1) * 128, :], OU[par]), inc="cs", n=16)
            R.wait("sy", "cs", R.vals["cs"])
            R.wait("g", "cs", R.vals["cs"])
        R.wait("sy", "cs", R.vals["cs"])
        R.wait("g", "cs", R.vals["cs"])

        with nc.Block() as block:
            R.replay(block)
    return nc


def make_in_map(inputs, core_id):
    hc = host_constants()
    m = {
        "x": np.ascontiguousarray(np.asarray(inputs["L0_attn_emb"])[core_id]),
        "w1": np.asarray(inputs["L0_attn_W1"]),
        "w2": np.asarray(inputs["L0_attn_W2"]),
        "w3": np.asarray(inputs["L0_attn_W3"]),
        "wo": np.asarray(inputs["L0_attn_Wo"]),
        "rw": np.asarray(inputs["L0_ffn_router_w"]),
        "wg_r": np.asarray(inputs["L0_ffn_wg_r"]),
        "wu_r": np.asarray(inputs["L0_ffn_wu_r"]),
        "wd_r": np.asarray(inputs["L0_ffn_wd_r"]),
        "wg_s": np.asarray(inputs["L0_ffn_wg_s"]),
        "wu_s": np.asarray(inputs["L0_ffn_wu_s"]),
        "wd_s": np.asarray(inputs["L0_ffn_wd_s"]),
        "cost": hc["cost"], "sint": hc["sint"], "ident": hc["ident"],
        "identf": hc["identf"], "rmat": hc["rmat"], "consts": hc["consts"],
    }
    return m


def kernel(**inputs):
    nc = build()
    in_maps = [make_in_map(inputs, c) for c in range(8)]
    res = run_bass_kernel_spmd(nc, in_maps, core_ids=list(range(8)))
    import numpy as _np
    return _np.stack([_np.asarray(r["out"]) for r in res.results], axis=0)


# version 11
# speedup vs baseline: 1.4646x; 1.0058x over previous
import contextlib
import numpy as np

import concourse.bass as bass
import concourse.mybir as mybir
from concourse.bass_utils import run_bass_kernel_spmd

F32 = mybir.dt.float32
BF16 = mybir.dt.bfloat16
AF = mybir.ActivationFunctionType
ALU = mybir.AluOpType
AX = mybir.AxisListType

S_LEN = 2048
E = 1024
NH = 8
TH = 256

INV_SQRT8 = 0.35355339059327373
SOFTMAX_SHIFT = -30.0
CV = [1.0 / 1024.0, 1e-5, -1e30, -1.0, INV_SQRT8, SOFTMAX_SHIFT]

ENG_NAMES = {"t": "tensor", "sc": "scalar", "v": "vector", "g": "gpsimd", "sy": "sync"}


class Rec:
    def __init__(self, nc, stack):
        self.nc = nc
        self.q = {e: [] for e in ENG_NAMES}
        self.hwm = {}
        self.sems = {}
        self.vals = {}
        for name in ("cg", "cs", "ct", "cc", "cv", "cgp",
                     "et", "ec", "ev2", "eg"):
            self.sems[name] = stack.enter_context(nc.semaphore(name))
            self.vals[name] = 0

    EVT = {"sc": "ec", "v": "ev2", "g": "eg"}

    def op(self, eng, fn, inc=None, n=1):
        if inc is None:
            if eng == "t":
                self.q[eng].append({"fn": fn, "inc": None})
            else:
                self.q[eng].append(fn)
            return None
        sem = self.sems[inc]
        self.vals[inc] += n
        v = self.vals[inc]
        if eng == "t":
            if fn is None:
                last = self.q[eng][-1]
                assert isinstance(last, dict) and last["inc"] is None, \
                    "no tensor instruction to attach inc to"
                last["inc"] = (sem, n)
            else:
                self.q[eng].append({"fn": fn, "inc": (sem, n)})
            return v
        if inc in ("cs", "cg"):
            def wrapped(h, fn=fn, sem=sem, n=n):
                fn(h).then_inc(sem, n)

            self.q[eng].append(wrapped)
        else:
            en = self.EVT[eng]
            esem = self.sems[en]
            self.vals[en] += 1
            ev = self.vals[en]

            def wrapped(h, fn=fn, sem=sem, n=n, esem=esem, ev=ev):
                fn(h).then_inc(esem, 1)
                h.wait_ge(esem, ev)
                h.nop().then_inc(sem, n)

            self.q[eng].append(wrapped)
        return v

    def wait(self, eng, cname, v):
        if not v or v <= 0:
            return
        hw = self.hwm.setdefault((eng, cname), 0)
        if v <= hw:
            return
        self.hwm[(eng, cname)] = v
        sem = self.sems[cname]
        self.q[eng].append(lambda h, sem=sem, v=v: h.wait_ge(sem, v))

    def barrier(self):
        engs = ("t", "sc", "v", "g", "sy")
        for e in ("sc", "v", "g"):
            self.op(e, lambda h: h.nop(),
                    inc={"sc": "cc", "v": "cv", "g": "cgp"}[e])
        if self.q["t"] and isinstance(self.q["t"][-1], dict) \
                and self.q["t"][-1]["inc"] is None:
            self.op("t", None, inc="ct")
        snap = dict(self.vals)
        for e in engs:
            for cn in ("ct", "cc", "cv", "cgp", "cg", "cs"):
                if e == "t" and cn == "ct":
                    continue
                self.wait(e, cn, snap[cn])

    def replay(self, block):
        for eng, name in ENG_NAMES.items():
            ops = self.q[eng]

            def body(h, ops=ops):
                for f in ops:
                    if isinstance(f, dict):
                        instr = f["fn"](h)
                        if f["inc"] is not None:
                            instr.then_inc(*f["inc"])
                    else:
                        f(h)

            getattr(block, name)(body)


def host_constants():
    import ml_dtypes
    DH = 128
    d = {}
    inv_freq = 1.0 / 10000.0 ** (np.arange(0, DH, 2, dtype=np.float64) / DH)
    t = np.arange(S_LEN, dtype=np.float64)
    freqs = np.repeat(t[:, None] * inv_freq[None, :], 2, axis=1)
    COS = np.cos(freqs)
    SIN2 = np.sin(freqs)
    SIN2[:, 0::2] *= -1.0
    tau_to_sp = np.empty(S_LEN, np.int64)
    for c in range(8):
        for sl in range(TH):
            tau_to_sp[c * TH + sl] = 8 * sl + c
    d["cost"] = np.ascontiguousarray(COS[tau_to_sp, :].T).astype(ml_dtypes.bfloat16)
    d["sint"] = np.ascontiguousarray(SIN2[tau_to_sp, :].T).astype(ml_dtypes.bfloat16)
    d["ident"] = np.eye(128, dtype=ml_dtypes.bfloat16)
    d["identf"] = np.eye(128, dtype=np.float32)
    r = np.zeros((128, 128), np.float32)
    for i in range(128):
        r[i ^ 1, i] = 1.0
    d["rmat"] = r.astype(ml_dtypes.bfloat16)
    consts = np.zeros((128, 8), np.float32)
    for i, v in enumerate(CV):
        consts[:, i] = v
    d["consts"] = consts
    return d


def build(n_heads=NH, do_attn=True, do_ffn=True, taps=(), stage=9, p0_tr=True):
    nc = bass.Bass("TRN2", target_bir_lowering=False, detect_race_conditions=False)

    x_h = nc.dram_tensor("x", [S_LEN, E], F32, kind="ExternalInput").ap()
    w_h = {nm: nc.dram_tensor(nm, [E, E], F32, kind="ExternalInput").ap()
           for nm in ("w1", "w2", "w3", "wo")}
    rw_h = nc.dram_tensor("rw", [E, 16], F32, kind="ExternalInput").ap()
    wg_r_h = nc.dram_tensor("wg_r", [16, E, 256], F32, kind="ExternalInput").ap()
    wu_r_h = nc.dram_tensor("wu_r", [16, E, 256], F32, kind="ExternalInput").ap()
    wd_r_h = nc.dram_tensor("wd_r", [16, 256, E], F32, kind="ExternalInput").ap()
    wg_s_h = nc.dram_tensor("wg_s", [2, E, 512], F32, kind="ExternalInput").ap()
    wu_s_h = nc.dram_tensor("wu_s", [2, E, 512], F32, kind="ExternalInput").ap()
    wd_s_h = nc.dram_tensor("wd_s", [2, 512, E], F32, kind="ExternalInput").ap()
    cost_h = nc.dram_tensor("cost", [128, S_LEN], BF16, kind="ExternalInput").ap()
    sint_h = nc.dram_tensor("sint", [128, S_LEN], BF16, kind="ExternalInput").ap()
    ident_h = nc.dram_tensor("ident", [128, 128], BF16, kind="ExternalInput").ap()
    identf_h = nc.dram_tensor("identf", [128, 128], F32, kind="ExternalInput").ap()
    rmat_h = nc.dram_tensor("rmat", [128, 128], BF16, kind="ExternalInput").ap()
    consts_h = nc.dram_tensor("consts", [128, 8], F32, kind="ExternalInput").ap()
    out_h = nc.dram_tensor("out", [S_LEN, E], F32, kind="ExternalOutput").ap()
    combd_h = nc.dram_tensor("combd", [16, S_LEN], BF16, kind="Internal").ap()

    tap_h = {}

    def mktap(name, shape, dtype):
        if name in taps:
            tap_h[name] = nc.dram_tensor("tap_" + name, shape, dtype,
                                         kind="ExternalOutput").ap()

    mktap("xnt", [128, 8 * S_LEN], BF16)
    mktap("qr0", [128, S_LEN], BF16)
    mktap("kr0", [128, S_LEN], BF16)
    mktap("vt0", [128, S_LEN], BF16)
    mktap("ot0", [128, S_LEN], BF16)
    mktap("acc1", [128, 16 * E], F32)
    mktap("comb", [128, 256], F32)
    mktap("accf", [128, 16 * E], F32)

    top = contextlib.ExitStack()
    with top:
        R = Rec(nc, top)
        sb = lambda name, shape, dtype: top.enter_context(nc.sbuf_tensor(name, shape, dtype))

        ACC = sb("ACC", [128, 16 * E], F32)
        XNT = sb("XNT", [128, 8 * S_LEN], BF16)
        CONSTS = sb("CONSTS", [128, 8], F32)
        IDENT = sb("IDENT", [128, 128], BF16)
        IDENTF = sb("IDENTF", [128, 128], F32)
        RMAT = sb("RMAT", [128, 128], BF16)
        ONES = sb("ONESB", [128, 1], BF16)
        COMBF = sb("COMBF", [16, S_LEN], BF16)
        for i, v in enumerate(CV):
            nc.const_aps.register(F32, v, CONSTS[:, i:i + 1])

        v_c = R.op("sy", lambda h: h.dma_start(CONSTS[:], consts_h[:]), inc="cs", n=16)
        v_c = R.op("sy", lambda h: h.dma_start(IDENT[:], ident_h[:]), inc="cs", n=16)
        v_c = R.op("sy", lambda h: h.dma_start(IDENTF[:], identf_h[:]), inc="cs", n=16)
        v_c = R.op("sy", lambda h: h.dma_start(RMAT[:], rmat_h[:]), inc="cs", n=16)
        v_ones = R.op("g", lambda h: h.memset(ONES[:], 1.0), inc="cgp")
        for e in ("t", "sc", "v", "g", "sy"):
            R.wait(e, "cs", v_c)
            R.wait(e, "cgp", v_ones)

        p0 = contextlib.ExitStack()

        def phase_p0_ln():
            psb0 = lambda n_, s_, d_: p0.enter_context(nc.psum_tensor(n_, s_, d_))
            sb0 = lambda n_, s_, d_: p0.enter_context(nc.sbuf_tensor(n_, s_, d_))
            XIN = [sb0(f"XIN{i}", [128, E], F32) for i in range(2)]
            XNM = [sb0(f"XNM{i}", [128, E], BF16) for i in range(2)]
            JUNK = [sb0(f"JUNK{i}", [128, E], BF16) for i in range(2)]
            MB = sb0("MB", [128, 16], F32)
            psT = [psb0(f"psT0{i}", [128, 1024], F32) for i in range(2)]

            v_ln = [0] * 16
            v_ev = [0] * 16
            v_tr = [0] * 16
            if w_units:
                w_alloc()
                w_units[0]()
            for tc in range(16):
                par = tc % 2
                if tc >= 2:
                    R.wait("sy", "cv", v_ln[tc - 2])
                v_x = R.op("sy", lambda h, tc=tc, par=par: h.dma_start(
                    XIN[par][:], x_h[tc * 128:(tc + 1) * 128, :]), inc="cs", n=16)
                mb = par * 8
                R.wait("v", "cs", v_x)
                vv = R.op("v", lambda h, par=par, mb=mb: h.reduce_sum(
                    MB[:, mb + 6:mb + 7], XIN[par][:], AX.X), inc="cv")
                R.wait("v", "cv", vv)
                vv = R.op("v", lambda h, mb=mb: h.tensor_scalar(
                    MB[:, mb + 0:mb + 1], MB[:, mb + 6:mb + 7], CV[0], None,
                    ALU.mult), inc="cv")
                R.wait("sc", "cs", v_x)
                if tc >= 2:
                    R.wait("sc", "cv", v_ln[tc - 2])
                vs = R.op("sc", lambda h, par=par, mb=mb: h.activation(
                    JUNK[par][:], XIN[par][:], AF.Square,
                    accum_out=MB[:, mb + 1:mb + 2]), inc="cc")
                R.wait("v", "cv", vv)
                vv = R.op("v", lambda h, mb=mb: h.tensor_tensor(
                    MB[:, mb + 2:mb + 3], MB[:, mb + 0:mb + 1],
                    MB[:, mb + 0:mb + 1], ALU.mult), inc="cv")
                R.wait("v", "cc", vs)
                R.wait("v", "cv", vv)
                vv = R.op("v", lambda h, mb=mb: h.scalar_tensor_tensor(
                    MB[:, mb + 3:mb + 4], MB[:, mb + 1:mb + 2], CV[0],
                    MB[:, mb + 2:mb + 3], ALU.mult, ALU.subtract), inc="cv")
                R.wait("sc", "cv", vv)
                vs = R.op("sc", lambda h, mb=mb: h.activation(
                    MB[:, mb + 5:mb + 6], MB[:, mb + 3:mb + 4], AF.Sqrt,
                    bias=CV[1]), inc="cc")
                R.wait("v", "cc", vs)
                vv = R.op("v", lambda h, mb=mb: h.reciprocal(
                    MB[:, mb + 4:mb + 5], MB[:, mb + 5:mb + 6]), inc="cv")
                R.wait("v", "cv", vv)
                if tc >= 2:
                    R.wait("v", "ct", v_tr[tc - 2])
                v_ln[tc] = R.op("v", lambda h, tc=tc, par=par, mb=mb: h.tensor_scalar(
                    XNM[par][:], XIN[par][:], MB[:, mb + 0:mb + 1],
                    MB[:, mb + 4:mb + 5], ALU.subtract, ALU.mult), inc="cv")
                R.wait("t", "cv", v_ln[tc])
                if tc >= 2:
                    R.wait("t", "cc", v_ev[tc - 2])
                for ec in range(8):
                    R.op("t", lambda h, ec=ec, tc=tc, par=par: h.matmul(
                        psT[par][:, ec * 128:(ec + 1) * 128],
                        XNM[par][:, ec * 128:(ec + 1) * 128],
                        IDENT[:], start=True, stop=True))
                v_tr[tc] = R.op("t", None, inc="ct")
                R.wait("sc", "ct", v_tr[tc])
                v_ev[tc] = R.op("sc", lambda h, tc=tc, par=par: h.activation(
                    bass.AP(XNT, tc * 128, [[8 * S_LEN, 128], [S_LEN, 8], [1, 128]]),
                    bass.AP(psT[par], 0, [[1024, 128], [128, 8], [1, 128]]),
                    AF.Copy), inc="cc")
                for wu in w_units_for_chunk(tc):
                    wu()
            if "xnt" in taps:
                R.wait("sy", "cc", v_ev[15])
                R.op("sy", lambda h: h.dma_start(tap_h["xnt"], XNT[:]), inc="cs", n=16)
            return v_ev

        pA = contextlib.ExitStack()
        sbA = lambda name, shape, dtype: pA.enter_context(nc.sbuf_tensor(name, shape, dtype))
        WOTP = sbA("WOTP", [128, 8 * E], BF16)
        W123T = {nm: sbA(nm + "T", [128, 8 * E], BF16) for nm in ("w1", "w2", "w3")}
        pw = contextlib.ExitStack()
        wst = {"cc_ev": 0, "v_tdone": [0] * 4, "v_wl": 0}
        wbuf = {}

        def w_alloc():
            wbuf["WNAT"] = pw.enter_context(nc.sbuf_tensor("WNAT0", [128, 8 * E], BF16))
            wbuf["psTW"] = pw.enter_context(nc.psum_tensor("psTW", [128, 1024], F32))

        def w_dma_unit(i, nm):
            def go(i=i, nm=nm):
                if i >= 1:
                    R.wait("g", "ct", wst["v_tdone"][i - 1])
                wst["v_wl"] = R.op("g", lambda h, nm=nm: h.dma_start(
                    wbuf["WNAT"][:].rearrange("p (pc e) -> p pc e", pc=8),
                    w_h[nm].rearrange("(pc p) e -> p pc e", p=128)),
                    inc="cg", n=16)
            return go

        def w_blk_unit(i, nm, blk):
            def go(i=i, nm=nm, blk=blk):
                R.wait("t", "cg", wst["v_wl"])
                R.wait("t", "cc", wst["cc_ev"])
                for j in range(8):
                    if nm != "wo":
                        R.op("t", lambda h, blk=blk, j=j: h.matmul(
                            wbuf["psTW"][:, j * 128:(j + 1) * 128],
                            wbuf["WNAT"][:, j * E + blk * 128: j * E + blk * 128 + 128],
                            IDENT[:], start=True, stop=True))
                    else:
                        R.op("t", lambda h, blk=blk, j=j: h.matmul(
                            wbuf["psTW"][:, j * 128:(j + 1) * 128],
                            wbuf["WNAT"][:, j * E + blk: j * E + blk + 1017:8],
                            IDENT[:], start=True, stop=True))
                vt = R.op("t", None, inc="ct")
                R.wait("sc", "ct", vt)
                if nm != "wo":
                    wst["cc_ev"] = R.op("sc", lambda h, nm=nm, blk=blk: h.activation(
                        W123T[nm][:, blk * E:(blk + 1) * E], wbuf["psTW"][:], AF.Copy),
                        inc="cc")
                else:
                    wst["cc_ev"] = R.op("sc", lambda h, blk=blk: h.activation(
                        WOTP[:, blk * E:(blk + 1) * E], wbuf["psTW"][:], AF.Copy),
                        inc="cc")
                if blk == 7:
                    wst["v_tdone"][i] = R.vals["ct"]
            return go

        w_units = []
        if stage >= 2:
            for i, nm in enumerate(["w1", "w2", "w3", "wo"]):
                w_units.append(w_dma_unit(i, nm))
                for blk in range(8):
                    w_units.append(w_blk_unit(i, nm, blk))
        w_sched = {}
        for ui in range(1, len(w_units)):
            chunk = 2 + ((ui - 1) * 14) // max(len(w_units) - 1, 1)
            w_sched.setdefault(min(chunk, 15), []).append(w_units[ui])

        def w_units_for_chunk(tc):
            return w_sched.get(tc, [])

        def phase_w():
            R.wait("t", "cc", wst["cc_ev"])
        def phase_router():
            R.wait("g", "ct", R.vals["ct"])
            with contextlib.ExitStack() as p2r:
                psbr = lambda n_, s_, d_: p2r.enter_context(nc.psum_tensor(n_, s_, d_))
                sbr = lambda n_, s_, d_: p2r.enter_context(nc.sbuf_tensor(n_, s_, d_))
                RWB = sbr("RWB", [128, 8 * 16], BF16)
                LGF = sbr("LGF", [16, S_LEN], F32)
                LGT = sbr("LGT", [128, 256], F32)
                COMBT = sbr("COMBT", [128, 256], F32)
                RS = sbr("RS", [128, 16], F32)
                L2 = sbr("L2", [128, 32], F32)
                MS1 = sbr("MS1", [128, 32], F32)
                MS2 = sbr("MS2", [128, 32], F32)
                psL = psbr("psL", [16, 1024], F32)
                psLT = psbr("psLT", [128, 256], F32)
                v_rw = R.op("g", lambda h: h.dma_start(
                    RWB[:].rearrange("e (ec x) -> e ec x", ec=8),
                    rw_h.rearrange("(ec e) x -> e ec x", e=128)), inc="cg", n=16)
                R.wait("t", "cg", v_rw)
                v_lg = 0
                for hf2 in range(2):
                    R.wait("t", "cc", v_lg)
                    for ec in range(8):
                        for hf in range(2):
                            R.op("t", lambda h, ec=ec, hf=hf, hf2=hf2: h.matmul(
                                psL[:, hf * 512:(hf + 1) * 512],
                                RWB[:, ec * 16:(ec + 1) * 16],
                                XNT[:, ec * S_LEN + hf2 * 1024 + hf * 512:
                                    ec * S_LEN + hf2 * 1024 + hf * 512 + 512],
                                start=(ec == 0), stop=(ec == 7)))
                    vt = R.op("t", None, inc="ct")
                    R.wait("sc", "ct", vt)
                    v_lg = R.op("sc", lambda h, hf2=hf2: h.activation(
                        LGF[:, hf2 * 1024:(hf2 + 1) * 1024], psL[:], AF.Copy),
                        inc="cc")
                R.wait("t", "cc", v_lg)
                for tc in range(16):
                    R.op("t", lambda h, tc=tc: h.matmul(
                        psLT[:, tc * 16:(tc + 1) * 16],
                        LGF[:, tc * 128:(tc + 1) * 128], IDENTF[0:16, 0:16], start=True, stop=True))
                vt = R.op("t", None, inc="ct")
                R.wait("sc", "ct", vt)
                v_lgt = R.op("sc", lambda h: h.activation(LGT[:], psLT[:], AF.Copy),
                             inc="cc")
                R.wait("v", "cc", v_lgt)
                for tc in range(16):
                    rs = (tc % 2) * 8
                    mo = (tc % 2) * 16
                    vv = R.op("v", lambda h, tc=tc, rs=rs: h.reduce_max(
                        RS[:, rs + 0:rs + 1], LGT[:, tc * 16:(tc + 1) * 16], AX.X),
                        inc="cv")
                    R.wait("v", "cv", vv)
                    vv = R.op("v", lambda h, tc=tc, rs=rs, mo=mo: h.tensor_scalar(
                        MS1[:, mo:mo + 16], LGT[:, tc * 16:(tc + 1) * 16],
                        RS[:, rs + 0:rs + 1], None, ALU.is_equal), inc="cv")
                    R.wait("v", "cv", vv)
                    vv = R.op("v", lambda h, tc=tc, mo=mo: h.scalar_tensor_tensor(
                        L2[:, mo:mo + 16], MS1[:, mo:mo + 16], CV[2],
                        LGT[:, tc * 16:(tc + 1) * 16], ALU.mult, ALU.add), inc="cv")
                    R.wait("v", "cv", vv)
                    vv = R.op("v", lambda h, rs=rs, mo=mo: h.reduce_max(
                        RS[:, rs + 1:rs + 2], L2[:, mo:mo + 16], AX.X), inc="cv")
                    R.wait("v", "cv", vv)
                    vv = R.op("v", lambda h, rs=rs, mo=mo: h.tensor_scalar(
                        MS2[:, mo:mo + 16], L2[:, mo:mo + 16], RS[:, rs + 1:rs + 2],
                        None, ALU.is_equal), inc="cv")
                    R.wait("v", "cv", vv)
                    vv = R.op("v", lambda h, rs=rs: h.tensor_tensor(
                        RS[:, rs + 2:rs + 3], RS[:, rs + 1:rs + 2],
                        RS[:, rs + 0:rs + 1], ALU.subtract), inc="cv")
                    R.wait("sc", "cv", vv)
                    vs = R.op("sc", lambda h, rs=rs: h.activation(
                        RS[:, rs + 4:rs + 5], RS[:, rs + 2:rs + 3], AF.Sigmoid),
                        inc="cc")
                    R.wait("sc", "cc", vs)
                    vs = R.op("sc", lambda h, rs=rs: h.activation(
                        RS[:, rs + 3:rs + 4], RS[:, rs + 2:rs + 3], AF.Sigmoid,
                        scale=CV[3]), inc="cc")
                    R.wait("v", "cc", vs)
                    vv = R.op("v", lambda h, tc=tc, rs=rs, mo=mo: h.tensor_scalar(
                        COMBT[:, tc * 16:(tc + 1) * 16], MS1[:, mo:mo + 16],
                        RS[:, rs + 3:rs + 4], None, ALU.mult), inc="cv")
                    R.wait("v", "cv", vv)
                    R.op("v", lambda h, tc=tc, rs=rs, mo=mo: h.scalar_tensor_tensor(
                        COMBT[:, tc * 16:(tc + 1) * 16], MS2[:, mo:mo + 16],
                        RS[:, rs + 4:rs + 5], COMBT[:, tc * 16:(tc + 1) * 16],
                        ALU.mult, ALU.add), inc="cv")
                v_combt = R.vals["cv"]
                if "comb" in taps:
                    R.wait("sy", "cv", v_combt)
                    R.op("sy", lambda h: h.dma_start(tap_h["comb"], COMBT[:]),
                         inc="cs", n=16)
                R.wait("t", "cv", v_combt)
                v_combf = 0
                for r2 in range(2):
                    R.wait("t", "cc", v_combf)
                    for j in range(8):
                        tc = r2 * 8 + j
                        R.op("t", lambda h, tc=tc, j=j: h.matmul(
                            psL[:, j * 128:(j + 1) * 128],
                            COMBT[:, tc * 16:(tc + 1) * 16], IDENTF[:],
                            start=True, stop=True))
                    vt = R.op("t", None, inc="ct")
                    R.wait("sc", "ct", vt)
                    v_combf = R.op("sc", lambda h, r2=r2: h.activation(
                        COMBF[:, r2 * 1024:(r2 + 1) * 1024], psL[:], AF.Copy),
                        inc="cc")
                R.wait("g", "cc", v_combf)
                v_combd = R.op("g", lambda h: h.dma_start(combd_h, COMBF[:]),
                               inc="cg", n=16)
            return v_combd
        if stage >= 1:
            _v_ev_saved = phase_p0_ln()
        if stage >= 2:
            phase_w()
        pw.close()
        _v_combd = phase_router() if (do_ffn and stage >= 4) else 0
        if stage < 1:
            vz = R.op("v", lambda h: h.memset(XNT[:], 0.0), inc="cv")
            R.wait("t", "cv", vz)
        p0.close()
        R.barrier()

        if do_attn and stage >= 3:
            COST = sbA("COST", [128, S_LEN], BF16)
            SINT = sbA("SINT", [128, S_LEN], BF16)
            R.op("sy", lambda h: h.dma_start(COST[:], cost_h[:]), inc="cs", n=16)
            v_tab = R.op("sy", lambda h: h.dma_start(SINT[:], sint_h[:]), inc="cs", n=16)

            QHB = sbA("QHB", [128, S_LEN], BF16)
            KHB = sbA("KHB", [128, S_LEN], BF16)
            QR = sbA("QR", [128, S_LEN], BF16)
            KR = sbA("KR", [128, S_LEN], BF16)
            VT = sbA("VT", [128, S_LEN], BF16)
            T1 = sbA("T1", [128, 512], BF16)
            T2 = sbA("T2", [128, 512], BF16)
            PT = [sbA(f"PT{i}", [128, 1024], BF16) for i in range(3)]
            OSB = KHB
            OT = QHB
            ASB = OSB
            RSROW = sbA("RSROW", [1, S_LEN], BF16)
            RECT = sbA("RECT", [128, 16], F32)

            v_rope_prev = 0
            v_rope_hist = [0, 0]
            v_accev_prev = 0
            v_exp_carry = [0, 0]
            v_pv_carry3 = [0, 0, 0]
            v_oe = 0
            for head in range(n_heads):
                with contextlib.ExitStack() as p1a:
                    psba = lambda n_, s_, d_: p1a.enter_context(nc.psum_tensor(n_ + f"_{head}", s_, d_))
                    psQK = psba("psQK", [128, S_LEN], F32)
                    psV = psba("psV", [128, 1024], F32)
                    psSH = psba("psSH", [128, 1024], F32)

                    def proj_qk(wt, dst, head=head):
                        for pc in range(8):
                            for ec in range(8):
                                R.op("t", lambda h, pc=pc, ec=ec, wt=wt, head=head: h.matmul(
                                    psQK[:, pc * TH:(pc + 1) * TH],
                                    W123T[wt][:, ec * E + pc * 128: ec * E + pc * 128 + 128],
                                    XNT[:, ec * S_LEN + head * TH: ec * S_LEN + head * TH + TH],
                                    start=(ec == 0), stop=(ec == 7)))
                        vt = R.op("t", None, inc="ct")
                        R.wait("sc", "ct", vt)
                        return R.op("sc", lambda h, dst=dst: h.activation(
                            dst[:], psQK[:], AF.Copy), inc="cc")

                    R.wait("sc", "cv", v_rope_prev)
                    v_q = proj_qk("w1", QHB)
                    R.wait("t", "cc", v_q)
                    v_k = proj_qk("w2", KHB)

                    v_vev = 0
                    for blk in range(2):
                        R.wait("t", "cc", v_vev)
                        for ec in range(8):
                            for hf in range(2):
                                R.op("t", lambda h, ec=ec, hf=hf, blk=blk, head=head: h.matmul(
                                    psV[:, hf * 512:(hf + 1) * 512],
                                    XNT[:, ec * S_LEN + head * TH + blk * 128:
                                        ec * S_LEN + head * TH + blk * 128 + 128],
                                    W123T["w3"][:, ec * E + hf * 512: ec * E + hf * 512 + 512],
                                    start=(ec == 0), stop=(ec == 7)))
                        vt = R.op("t", None, inc="ct")
                        R.wait("sc", "ct", vt)
                        v_vev = R.op("sc", lambda h, blk=blk: h.activation(
                            bass.AP(VT, blk * 128, [[S_LEN, 128], [TH, 8], [1, 128]]),
                            bass.AP(psV, 0, [[1024, 128], [128, 8], [1, 128]]),
                            AF.Copy), inc="cc")

                    def rope(src, dst, v_src):
                        vq = 0
                        for j in range(4):
                            par = j % 2
                            R.wait("t", "cc", v_src)
                            R.wait("t", "cv", v_rope_hist[par])
                            vt = R.op("t", lambda h, j=j, par=par, src=src: h.matmul(
                                psSH[:, par * 512:(par + 1) * 512], RMAT[:],
                                src[:, j * 512:(j + 1) * 512], start=True, stop=True),
                                inc="ct")
                            R.wait("v", "cs", v_tab)
                            R.wait("v", "cc", v_src)
                            R.op("v", lambda h, j=j, src=src: h.tensor_tensor(
                                T1[:], src[:, j * 512:(j + 1) * 512],
                                COST[:, j * 512:(j + 1) * 512], ALU.mult), inc="cv")
                            R.wait("v", "ct", vt)
                            vv = R.op("v", lambda h, j=j, par=par: h.tensor_tensor(
                                T2[:], psSH[:, par * 512:(par + 1) * 512],
                                SINT[:, j * 512:(j + 1) * 512], ALU.mult), inc="cv")
                            R.wait("v", "cv", vv)
                            vq = R.op("v", lambda h, j=j, dst=dst: h.tensor_tensor(
                                dst[:, j * 512:(j + 1) * 512], T1[:], T2[:], ALU.add),
                                inc="cv")
                            v_rope_hist[par] = vq
                        return vq

                    v_qr = rope(QHB, QR, v_q)
                    v_kr = rope(KHB, KR, v_k)
                    v_rope_prev = v_kr

                if head == 0 and "qr0" in taps:
                    R.wait("sy", "cv", v_qr)
                    R.op("sy", lambda h: h.dma_start(tap_h["qr0"], QR[:]), inc="cs", n=16)
                    R.wait("sy", "cv", v_kr)
                    R.op("sy", lambda h: h.dma_start(tap_h["kr0"], KR[:]), inc="cs", n=16)
                if head == 0 and "vt0" in taps:
                    R.wait("sy", "cc", R.vals["cc"])
                    R.op("sy", lambda h: h.dma_start(tap_h["vt0"], VT[:]), inc="cs", n=16)

                with contextlib.ExitStack() as p1b:
                    psbb = lambda n_, s_, d_: p1b.enter_context(nc.psum_tensor(n_ + f"_{head}", s_, d_))
                    psS = [psbb(f"psS{i}", [128, 1024], F32) for i in range(2)]
                    psO = psbb("psO", [128, 1024], F32)
                    psR = psbb("psR", [1, 1024], F32)
                    R.wait("t", "cv", v_qr)
                    R.wait("t", "cv", v_kr)
                    v_exp = [0] * 32
                    v_pv = [0] * 32
                    v_s = [0] * 32
                    for kk in range(34):
                        if kk < 32:
                            qh, kt = kk // 16, kk % 16
                            par = kk % 2
                            R.wait("t", "cc",
                                   v_exp[kk - 2] if kk >= 2 else v_exp_carry[kk])
                            for hf in range(2):
                                R.op("t", lambda h, kt=kt, par=par, qh=qh, hf=hf: h.matmul(
                                    psS[par][:, hf * 512:(hf + 1) * 512],
                                    KR[:, kt * 128:(kt + 1) * 128],
                                    QR[:, qh * 1024 + hf * 512: qh * 1024 + hf * 512 + 512],
                                    start=True, stop=True))
                            v_s[kk] = R.op("t", None, inc="ct")
                            R.wait("sc", "ct", v_s[kk])
                            R.wait("sc", "ct",
                                   v_pv[kk - 3] if kk >= 3 else v_pv_carry3[kk])
                            v_exp[kk] = R.op("sc", lambda h, par3=kk % 3, par=par: h.activation(
                                PT[par3][:], psS[par][:], AF.Exp,
                                bias=SOFTMAX_SHIFT, scale=INV_SQRT8), inc="cc")
                        if kk >= 2:
                            pk = kk - 2
                            qh, kt = pk // 16, pk % 16
                            par = pk % 2
                            R.wait("t", "cc", v_exp[pk])
                            if kt == 0:
                                R.wait("t", "cc", v_oe)
                            par3 = pk % 3
                            for hf in range(2):
                                R.op("t", lambda h, kt=kt, par3=par3, hf=hf: h.matmul(
                                    psO[:, hf * 512:(hf + 1) * 512],
                                    VT[:, kt * 128:(kt + 1) * 128],
                                    PT[par3][:, hf * 512:(hf + 1) * 512],
                                    start=(kt == 0), stop=(kt == 15),
                                    skip_group_check=True))
                            for hf in range(2):
                                R.op("t", lambda h, kt=kt, par3=par3, hf=hf: h.matmul(
                                    psR[:, hf * 512:(hf + 1) * 512],
                                    ONES[:],
                                    PT[par3][:, hf * 512:(hf + 1) * 512],
                                    start=(kt == 0), stop=(kt == 15),
                                    skip_group_check=True))
                            v_pv[pk] = R.op("t", None, inc="ct")
                            if kt == 15:
                                R.wait("sc", "ct", v_pv[pk])
                                R.op("sc", lambda h, qh=qh: h.activation(
                                    RSROW[0:1, qh * 1024:(qh + 1) * 1024], psR[0:1, :],
                                    AF.Copy), inc="cc")
                                v_oe = R.op("sc", lambda h, qh=qh: h.activation(
                                    OSB[:, qh * 1024:(qh + 1) * 1024], psO[:],
                                    AF.Copy), inc="cc")
                    v_exp_carry = [v_exp[30], v_exp[31]]
                    v_pv_carry3 = [v_pv[31]] * 3

                with contextlib.ExitStack() as p1c:
                    psbc = lambda n_, s_, d_: p1c.enter_context(nc.psum_tensor(n_ + f"_{head}", s_, d_))
                    psAm = psbc("psAm", [128, 1024], F32)
                    psTT = psbc("psTT", [128, 1024], F32)
                    R.wait("t", "cc", v_oe)
                    cc_ev = v_accev_prev
                    for j in range(16):
                        R.op("t", lambda h, j=j: h.matmul(
                            psAm[:, j:j + 1],
                            RSROW[0:1, j * 128:(j + 1) * 128], IDENT[0:1, 0:1], start=True, stop=True))
                    vt_rs = R.op("t", None, inc="ct")
                    R.wait("v", "ct", vt_rs)
                    v_rec = R.op("v", lambda h: h.reciprocal(
                        RECT[:], psAm[:, 0:16]), inc="cv")
                    for r in range(2):
                        R.wait("t", "cc", cc_ev)
                        for j in range(8):
                            R.op("t", lambda h, r=r, j=j: h.matmul(
                                psTT[:, j * 128:(j + 1) * 128],
                                OSB[:, (r * 8 + j) * 128:(r * 8 + j) * 128 + 128],
                                IDENT[:], start=True, stop=True))
                        vt = R.op("t", None, inc="ct")
                        R.wait("sc", "ct", vt)
                        R.wait("sc", "cv", v_rec)
                        for j in range(8):
                            kt2 = r * 8 + j
                            cc_ev = R.op("sc", lambda h, kt2=kt2, j=j: h.activation(
                                OT[:, kt2 * 128:(kt2 + 1) * 128],
                                psTT[:, j * 128:(j + 1) * 128], AF.Copy,
                                scale=RECT[:, kt2:kt2 + 1]), inc="cc")
                    if head == 0 and "ot0" in taps:
                        R.wait("sy", "cc", cc_ev)
                        R.op("sy", lambda h: h.dma_start(tap_h["ot0"], OT[:]), inc="cs",
                             n=16)
                    for u in range(2):
                        R.wait("t", "cc", cc_ev)
                        R.wait("t", "cv", v_rec)
                        for i, kt in enumerate(range(u, 16, 2)):
                            for hf in range(2):
                                R.op("t", lambda h, kt=kt, i=i, hf=hf: h.matmul(
                                    psAm[:, hf * 512:(hf + 1) * 512],
                                    OT[:, kt * 128:(kt + 1) * 128],
                                    WOTP[:, (kt // 2) * E + hf * 512:
                                         (kt // 2) * E + hf * 512 + 512],
                                    start=(i == 0), stop=(i == 7)))
                        vt = R.op("t", None, inc="ct")
                        R.wait("sc", "ct", vt)
                        v_asb = R.op("sc", lambda h: h.activation(
                            ASB[:, 0:1024], psAm[:], AF.Copy), inc="cc")
                        R.wait("t", "cc", v_asb)
                        for cc_i in range(8):
                            R.op("t", lambda h, cc_i=cc_i: h.matmul(
                                psTT[:, cc_i * 128:(cc_i + 1) * 128],
                                ASB[:, cc_i * 128:(cc_i + 1) * 128], IDENT[:], start=True, stop=True))
                        vt = R.op("t", None, inc="ct")
                        R.wait("sc", "ct", vt)
                        cc_ev = R.op("sc", lambda h, u=u, head=head: h.activation(
                            bass.AP(ACC, (u * 8) * E + head * 128,
                                    [[16 * E, 128], [E, 8], [1, 128]]),
                            bass.AP(psTT, 0, [[1024, 128], [128, 8], [1, 128]]),
                            AF.Copy), inc="cc")
                    v_accev_prev = cc_ev
            if "acc1" in taps:
                R.wait("sy", "cc", R.vals["cc"])
                R.op("sy", lambda h: h.dma_start(tap_h["acc1"], ACC[:]), inc="cs", n=16)
        if not (do_attn and stage >= 3):
            vz = R.op("v", lambda h: h.memset(ACC[:], 0.0), inc="cv")
            R.wait("t", "cv", vz)
        R.barrier()

        pA.close()

        if do_ffn and stage >= 4:
            pass
            R.barrier()

            with contextlib.ExitStack() as p2e:
                psbe = lambda n_, s_, d_: p2e.enter_context(nc.psum_tensor(n_, s_, d_))
                sbe = lambda n_, s_, d_: p2e.enter_context(nc.sbuf_tensor(n_, s_, d_))
                psG1 = psbe("psG1", [128, 1024], F32)
                psG2 = psbe("psG2", [128, 1024], F32)
                psD = [psbe(f"psD{i}", [128, 1024], F32) for i in range(2)]
                HARE = sbe("HARE", [128, 8 * S_LEN], BF16)
                SG = sbe("SG", [128, S_LEN], BF16)
                TFFN = sbe("TFFN", [128, S_LEN], BF16)
                WGA = sbe("WGA", [128, 8 * 512], BF16)
                WUA = sbe("WUA", [128, 8 * 512], BF16)
                WDA = sbe("WDA", [128, 8 * 1024], BF16)
                CB = [sbe(f"CB{i}", [128, S_LEN], BF16) for i in range(2)]

                groups = [("r", [4 * g + i for i in range(4)]) for g in range(4)]
                groups.append(("s", [0, 1]))
                XI = [XNT[:, 0:2048].bitcast(F32), XNT[:, 2048:4096].bitcast(F32)]
                OU = [XNT[:, 4096:6144].bitcast(F32), XNT[:, 6144:8192].bitcast(F32)]
                v_ad3 = [0] * 16
                v_st3 = [0] * 16
                v_wg_used = {}
                v_wd_used = {}
                v_cb_used = {}
                v_prev_add = [0, 0]
                v_silu_prev = 0
                v_h_prev = 0
                for gi, (gkind, gl) in enumerate(groups):
                    shared = gkind == "s"
                    nmc = 4 if shared else 2
                    mid = 512 if shared else 256
                    ndslots = 2 if shared else 4
                    v_h = []
                    v_ld_all = 0
                    for li, e in enumerate(gl):
                        sl_g = 0 if shared else li % 2
                        sl_d = li % ndslots
                        wg_off = sl_g * 8 * 256 if not shared else 0
                        wd_off = sl_d * (4 * E if shared else 2 * E)
                        R.wait("g", "cv", v_wg_used.get(sl_g, 0))
                        src_g = (wg_s_h if shared else wg_r_h)[e].rearrange(
                            "(ec e2) m -> e2 ec m", e2=128)
                        v_lg_ = R.op("g", lambda h, src=src_g, off=wg_off, mid=mid:
                                     h.dma_start(WGA[:, off:off + 8 * mid].rearrange(
                                         "p (ec m) -> p ec m", ec=8), src),
                                     inc="cg", n=16)
                        src_u = (wu_s_h if shared else wu_r_h)[e].rearrange(
                            "(ec e2) m -> e2 ec m", e2=128)
                        v_lu_ = R.op("g", lambda h, src=src_u, off=wg_off, mid=mid:
                                     h.dma_start(WUA[:, off:off + 8 * mid].rearrange(
                                         "p (ec m) -> p ec m", ec=8), src),
                                     inc="cg", n=16)
                        R.wait("g", "cv", v_wd_used.get(sl_d, 0))
                        src_d = (wd_s_h if shared else wd_r_h)[e].rearrange(
                            "(mc m) x -> m mc x", m=128)
                        v_ld_all = R.op("g", lambda h, src=src_d, off=wd_off, nmc=nmc:
                                        h.dma_start(WDA[:, off:off + nmc * E].rearrange(
                                            "p (mc x) -> p mc x", mc=nmc), src),
                                        inc="cg", n=16)
                        v_cb_ = 0
                        if not shared:
                            R.wait("g", "cv", v_cb_used.get(sl_g, 0))
                            v_cb_ = R.op("g", lambda h, e=e, sl=sl_g: h.dma_start(
                                CB[sl][:], combd_h[e:e + 1, :].partition_broadcast(128)),
                                inc="cg", n=16)
                        for mc in range(nmc):
                            for th in range(2):
                                R.wait("t", "cg", v_lg_)
                                R.wait("t", "cc", v_silu_prev)
                                for ec in range(8):
                                    for hf in range(2):
                                        R.op("t", lambda h, ec=ec, hf=hf, mc=mc, th=th,
                                             off=wg_off, mid=mid: h.matmul(
                                            psG1[:, hf * 512:(hf + 1) * 512],
                                            WGA[:, off + ec * mid + mc * 128:
                                                off + ec * mid + mc * 128 + 128],
                                            XNT[:, ec * S_LEN + th * 1024 + hf * 512:
                                                ec * S_LEN + th * 1024 + hf * 512 + 512],
                                            start=(ec == 0), stop=(ec == 7)))
                                vt_g = R.op("t", None, inc="ct")
                                R.wait("sc", "ct", vt_g)
                                R.wait("sc", "cv", v_h_prev)
                                v_silu_prev = R.op("sc", lambda h, th=th: h.activation(
                                    SG[:, th * 1024:(th + 1) * 1024], psG1[:],
                                    AF.Silu), inc="cc")
                                R.wait("t", "cg", v_lu_)
                                R.wait("t", "cv", v_h_prev)
                                for ec in range(8):
                                    for hf in range(2):
                                        R.op("t", lambda h, ec=ec, hf=hf, mc=mc, th=th,
                                             off=wg_off, mid=mid: h.matmul(
                                            psG2[:, hf * 512:(hf + 1) * 512],
                                            WUA[:, off + ec * mid + mc * 128:
                                                off + ec * mid + mc * 128 + 128],
                                            XNT[:, ec * S_LEN + th * 1024 + hf * 512:
                                                ec * S_LEN + th * 1024 + hf * 512 + 512],
                                            start=(ec == 0), stop=(ec == 7)))
                                vt_u = R.op("t", None, inc="ct")
                                R.wait("v", "ct", vt_u)
                                R.wait("v", "cc", v_silu_prev)
                                hslot = li * nmc + mc
                                base = hslot * S_LEN + th * 1024
                                if not shared:
                                    R.wait("v", "cg", v_cb_)
                                    vv = R.op("v", lambda h, th=th: h.tensor_tensor(
                                        TFFN[:, 0:1024], SG[:, th * 1024:(th + 1) * 1024],
                                        psG2[:], ALU.mult), inc="cv")
                                    R.wait("v", "cv", vv)
                                    vv = R.op("v", lambda h, base=base, th=th, sl=sl_g:
                                              h.tensor_tensor(
                                        HARE[:, base:base + 1024], TFFN[:, 0:1024],
                                        CB[sl][:, th * 1024:(th + 1) * 1024],
                                        ALU.mult), inc="cv")
                                else:
                                    vv = R.op("v", lambda h, base=base, th=th:
                                              h.tensor_tensor(
                                        HARE[:, base:base + 1024],
                                        SG[:, th * 1024:(th + 1) * 1024],
                                        psG2[:], ALU.mult), inc="cv")
                                v_h_prev = vv
                                v_h.append(vv)
                        v_wg_used[sl_g] = v_h_prev
                        if not shared:
                            v_cb_used[sl_g] = v_h_prev
                    v_hall = max(v_h)
                    R.wait("t", "cv", v_hall)
                    R.wait("t", "cg", v_ld_all)
                    v_add = [0] * 16
                    for tc in range(16):
                        par = tc % 2
                        prev = v_add[tc - 2] if tc >= 2 else v_prev_add[par]
                        R.wait("t", "cv", prev)
                        first = True
                        nexp = len(gl)
                        for li in range(nexp):
                            sl_d = li % ndslots
                            wd_off2 = sl_d * (4 * E if shared else 2 * E)
                            for mc in range(nmc):
                                hslot = li * nmc + mc
                                last = (li == nexp - 1 and mc == nmc - 1)
                                for hf in range(2):
                                    R.op("t", lambda h, tc=tc, par=par, hslot=hslot,
                                         off=wd_off2, mc=mc, first=first, hf=hf, last=last:
                                         h.matmul(
                                        psD[par][:, hf * 512:(hf + 1) * 512],
                                        HARE[:, hslot * S_LEN + tc * 128:
                                             hslot * S_LEN + tc * 128 + 128],
                                        WDA[:, off + mc * E + hf * 512:
                                            off + mc * E + hf * 512 + 512],
                                        start=first, stop=last,
                                        skip_group_check=True))
                                first = False
                        vt = R.op("t", None, inc="ct")
                        R.wait("v", "ct", vt)
                        v_add[tc] = R.op("v", lambda h, tc=tc, par=par: h.tensor_tensor(
                            ACC[:, tc * E:(tc + 1) * E], psD[par][:],
                            ACC[:, tc * E:(tc + 1) * E], ALU.add), inc="cv")
                        if shared:
                            if tc < 2:
                                R.wait("sy", "cv", v_hall)
                            else:
                                R.wait("sy", "cv", v_ad3[tc - 2])
                                R.wait("sy", "cs", v_st3[tc - 2])
                            v_xl = R.op("sy", lambda h, tc=tc, par=par: h.dma_start(
                                XI[par], x_h[tc * 128:(tc + 1) * 128, :]),
                                inc="cs", n=16)
                            R.wait("v", "cs", v_xl)
                            v_ad3[tc] = R.op("v", lambda h, tc=tc, par=par:
                                             h.tensor_tensor(
                                OU[par], ACC[:, tc * E:(tc + 1) * E], XI[par],
                                ALU.add), inc="cv")
                            R.wait("sc", "cv", v_ad3[tc])
                            v_st3[tc] = R.op("sc", lambda h, tc=tc, par=par:
                                             h.dma_start(
                                out_h[tc * 128:(tc + 1) * 128, :], OU[par]),
                                inc="cs", n=16)
                    v_prev_add = [v_add[14], v_add[15]]
                    for sl in range(ndslots):
                        v_wd_used[sl] = v_add[15]
                    v_add_final = list(v_add)
                    v_hall_last = v_hall
        if (not do_ffn or stage < 4) or "accf" in taps:
            R.barrier()
            v_add_final = [R.vals["cv"]] * 16
            v_hall_last = R.vals["cv"]
        if "accf" in taps:
            R.op("sy", lambda h: h.dma_start(tap_h["accf"], ACC[:]), inc="cs", n=16)

        if not (do_ffn and stage >= 4):
            XI = [XNT[:, 0:2048].bitcast(F32), XNT[:, 2048:4096].bitcast(F32)]
            OU = [XNT[:, 4096:6144].bitcast(F32), XNT[:, 6144:8192].bitcast(F32)]
            v_st = [0] * 16
            v_ad = [0] * 16
            for tc in range(16):
                par = tc % 2
                if tc < 2:
                    R.wait("sy", "cv", v_hall_last)
                if tc >= 2:
                    R.wait("sy", "cv", v_ad[tc - 2])
                    R.wait("sy", "cs", v_st[tc - 2])
                v_xl = R.op("sy", lambda h, tc=tc, par=par: h.dma_start(
                    XI[par], x_h[tc * 128:(tc + 1) * 128, :]), inc="cs", n=16)
                R.wait("v", "cs", v_xl)
                R.wait("v", "cv", v_add_final[tc])
                v_ad[tc] = R.op("v", lambda h, tc=tc, par=par: h.tensor_tensor(
                    OU[par], ACC[:, tc * E:(tc + 1) * E], XI[par], ALU.add),
                    inc="cv")
                R.wait("sc", "cv", v_ad[tc])
                v_st[tc] = R.op("sc", lambda h, tc=tc, par=par: h.dma_start(
                    out_h[tc * 128:(tc + 1) * 128, :], OU[par]), inc="cs", n=16)
            R.wait("sy", "cs", R.vals["cs"])
            R.wait("g", "cs", R.vals["cs"])
        R.wait("sy", "cs", R.vals["cs"])
        R.wait("g", "cs", R.vals["cs"])

        with nc.Block() as block:
            R.replay(block)
    return nc


def make_in_map(inputs, core_id):
    hc = host_constants()
    m = {
        "x": np.ascontiguousarray(np.asarray(inputs["L0_attn_emb"])[core_id]),
        "w1": np.asarray(inputs["L0_attn_W1"]),
        "w2": np.asarray(inputs["L0_attn_W2"]),
        "w3": np.asarray(inputs["L0_attn_W3"]),
        "wo": np.asarray(inputs["L0_attn_Wo"]),
        "rw": np.asarray(inputs["L0_ffn_router_w"]),
        "wg_r": np.asarray(inputs["L0_ffn_wg_r"]),
        "wu_r": np.asarray(inputs["L0_ffn_wu_r"]),
        "wd_r": np.asarray(inputs["L0_ffn_wd_r"]),
        "wg_s": np.asarray(inputs["L0_ffn_wg_s"]),
        "wu_s": np.asarray(inputs["L0_ffn_wu_s"]),
        "wd_s": np.asarray(inputs["L0_ffn_wd_s"]),
        "cost": hc["cost"], "sint": hc["sint"], "ident": hc["ident"],
        "identf": hc["identf"], "rmat": hc["rmat"], "consts": hc["consts"],
    }
    return m


def kernel(**inputs):
    nc = build()
    in_maps = [make_in_map(inputs, c) for c in range(8)]
    res = run_bass_kernel_spmd(nc, in_maps, core_ids=list(range(8)))
    import numpy as _np
    return _np.stack([_np.asarray(r["out"]) for r in res.results], axis=0)


# version 13
# speedup vs baseline: 1.4751x; 1.0072x over previous
import contextlib
import numpy as np

import concourse.bass as bass
import concourse.mybir as mybir
from concourse.bass_utils import run_bass_kernel_spmd

F32 = mybir.dt.float32
BF16 = mybir.dt.bfloat16
AF = mybir.ActivationFunctionType
ALU = mybir.AluOpType
AX = mybir.AxisListType

S_LEN = 2048
E = 1024
NH = 8
TH = 256

INV_SQRT8 = 0.35355339059327373
SOFTMAX_SHIFT = -30.0
CV = [1.0 / 1024.0, 1e-5, -1e30, -1.0, INV_SQRT8, SOFTMAX_SHIFT]

ENG_NAMES = {"t": "tensor", "sc": "scalar", "v": "vector", "g": "gpsimd", "sy": "sync"}


class Rec:
    def __init__(self, nc, stack):
        self.nc = nc
        self.q = {e: [] for e in ENG_NAMES}
        self.hwm = {}
        self.sems = {}
        self.vals = {}
        for name in ("cg", "cs", "cst", "ct", "cc", "cv", "cgp",
                     "et", "ec", "ev2", "eg"):
            self.sems[name] = stack.enter_context(nc.semaphore(name))
            self.vals[name] = 0

    EVT = {"sc": "ec", "v": "ev2", "g": "eg"}

    def op(self, eng, fn, inc=None, n=1):
        if inc is None:
            if eng == "t":
                self.q[eng].append({"fn": fn, "inc": None})
            else:
                self.q[eng].append(fn)
            return None
        sem = self.sems[inc]
        self.vals[inc] += n
        v = self.vals[inc]
        if eng == "t":
            if fn is None:
                last = self.q[eng][-1]
                assert isinstance(last, dict) and last["inc"] is None, \
                    "no tensor instruction to attach inc to"
                last["inc"] = (sem, n)
            else:
                self.q[eng].append({"fn": fn, "inc": (sem, n)})
            return v
        if inc in ("cs", "cg", "cst"):
            def wrapped(h, fn=fn, sem=sem, n=n):
                fn(h).then_inc(sem, n)

            self.q[eng].append(wrapped)
        else:
            en = self.EVT[eng]
            esem = self.sems[en]
            self.vals[en] += 1
            ev = self.vals[en]

            def wrapped(h, fn=fn, sem=sem, n=n, esem=esem, ev=ev):
                fn(h).then_inc(esem, 1)
                h.wait_ge(esem, ev)
                h.nop().then_inc(sem, n)

            self.q[eng].append(wrapped)
        return v

    def wait(self, eng, cname, v):
        if not v or v <= 0:
            return
        hw = self.hwm.setdefault((eng, cname), 0)
        if v <= hw:
            return
        self.hwm[(eng, cname)] = v
        sem = self.sems[cname]
        self.q[eng].append(lambda h, sem=sem, v=v: h.wait_ge(sem, v))

    def barrier(self):
        engs = ("t", "sc", "v", "g", "sy")
        for e in ("sc", "v", "g"):
            self.op(e, lambda h: h.nop(),
                    inc={"sc": "cc", "v": "cv", "g": "cgp"}[e])
        if self.q["t"] and isinstance(self.q["t"][-1], dict) \
                and self.q["t"][-1]["inc"] is None:
            self.op("t", None, inc="ct")
        snap = dict(self.vals)
        for e in engs:
            for cn in ("ct", "cc", "cv", "cgp", "cg", "cs"):
                if e == "t" and cn == "ct":
                    continue
                self.wait(e, cn, snap[cn])

    def replay(self, block):
        for eng, name in ENG_NAMES.items():
            ops = self.q[eng]

            def body(h, ops=ops):
                for f in ops:
                    if isinstance(f, dict):
                        instr = f["fn"](h)
                        if f["inc"] is not None:
                            instr.then_inc(*f["inc"])
                    else:
                        f(h)

            getattr(block, name)(body)


def host_constants():
    import ml_dtypes
    DH = 128
    d = {}
    inv_freq = 1.0 / 10000.0 ** (np.arange(0, DH, 2, dtype=np.float64) / DH)
    t = np.arange(S_LEN, dtype=np.float64)
    freqs = np.repeat(t[:, None] * inv_freq[None, :], 2, axis=1)
    COS = np.cos(freqs)
    SIN2 = np.sin(freqs)
    SIN2[:, 0::2] *= -1.0
    tau_to_sp = np.empty(S_LEN, np.int64)
    for c in range(8):
        for sl in range(TH):
            tau_to_sp[c * TH + sl] = 8 * sl + c
    d["cost"] = np.ascontiguousarray(COS[tau_to_sp, :].T).astype(ml_dtypes.bfloat16)
    d["sint"] = np.ascontiguousarray(SIN2[tau_to_sp, :].T).astype(ml_dtypes.bfloat16)
    d["ident"] = np.eye(128, dtype=ml_dtypes.bfloat16)
    d["identf"] = np.eye(128, dtype=np.float32)
    r = np.zeros((128, 128), np.float32)
    for i in range(128):
        r[i ^ 1, i] = 1.0
    d["rmat"] = r.astype(ml_dtypes.bfloat16)
    consts = np.zeros((128, 8), np.float32)
    for i, v in enumerate(CV):
        consts[:, i] = v
    d["consts"] = consts
    return d


def build(n_heads=NH, do_attn=True, do_ffn=True, taps=(), stage=9, p0_tr=True):
    nc = bass.Bass("TRN2", target_bir_lowering=False, detect_race_conditions=False)

    x_h = nc.dram_tensor("x", [S_LEN, E], F32, kind="ExternalInput").ap()
    w_h = {nm: nc.dram_tensor(nm, [E, E], F32, kind="ExternalInput").ap()
           for nm in ("w1", "w2", "w3", "wo")}
    rw_h = nc.dram_tensor("rw", [E, 16], F32, kind="ExternalInput").ap()
    wg_r_h = nc.dram_tensor("wg_r", [16, E, 256], F32, kind="ExternalInput").ap()
    wu_r_h = nc.dram_tensor("wu_r", [16, E, 256], F32, kind="ExternalInput").ap()
    wd_r_h = nc.dram_tensor("wd_r", [16, 256, E], F32, kind="ExternalInput").ap()
    wg_s_h = nc.dram_tensor("wg_s", [2, E, 512], F32, kind="ExternalInput").ap()
    wu_s_h = nc.dram_tensor("wu_s", [2, E, 512], F32, kind="ExternalInput").ap()
    wd_s_h = nc.dram_tensor("wd_s", [2, 512, E], F32, kind="ExternalInput").ap()
    cost_h = nc.dram_tensor("cost", [128, S_LEN], BF16, kind="ExternalInput").ap()
    sint_h = nc.dram_tensor("sint", [128, S_LEN], BF16, kind="ExternalInput").ap()
    ident_h = nc.dram_tensor("ident", [128, 128], BF16, kind="ExternalInput").ap()
    identf_h = nc.dram_tensor("identf", [128, 128], F32, kind="ExternalInput").ap()
    rmat_h = nc.dram_tensor("rmat", [128, 128], BF16, kind="ExternalInput").ap()
    consts_h = nc.dram_tensor("consts", [128, 8], F32, kind="ExternalInput").ap()
    out_h = nc.dram_tensor("out", [S_LEN, E], F32, kind="ExternalOutput").ap()
    combd_h = nc.dram_tensor("combd", [16, S_LEN], BF16, kind="Internal").ap()

    tap_h = {}

    def mktap(name, shape, dtype):
        if name in taps:
            tap_h[name] = nc.dram_tensor("tap_" + name, shape, dtype,
                                         kind="ExternalOutput").ap()

    mktap("xnt", [128, 8 * S_LEN], BF16)
    mktap("qr0", [128, S_LEN], BF16)
    mktap("kr0", [128, S_LEN], BF16)
    mktap("vt0", [128, S_LEN], BF16)
    mktap("ot0", [128, S_LEN], BF16)
    mktap("acc1", [128, 16 * E], F32)
    mktap("comb", [128, 256], F32)
    mktap("accf", [128, 16 * E], F32)

    top = contextlib.ExitStack()
    with top:
        R = Rec(nc, top)
        sb = lambda name, shape, dtype: top.enter_context(nc.sbuf_tensor(name, shape, dtype))

        ACC = sb("ACC", [128, 16 * E], F32)
        XNT = sb("XNT", [128, 8 * S_LEN], BF16)
        CONSTS = sb("CONSTS", [128, 8], F32)
        IDENT = sb("IDENT", [128, 128], BF16)
        IDENTF = sb("IDENTF", [128, 128], F32)
        RMAT = sb("RMAT", [128, 128], BF16)
        ONES = sb("ONESB", [128, 1], BF16)
        COMBF = sb("COMBF", [16, S_LEN], BF16)
        for i, v in enumerate(CV):
            nc.const_aps.register(F32, v, CONSTS[:, i:i + 1])

        v_c = R.op("sy", lambda h: h.dma_start(CONSTS[:], consts_h[:]), inc="cs", n=16)
        v_c = R.op("sy", lambda h: h.dma_start(IDENT[:], ident_h[:]), inc="cs", n=16)
        v_c = R.op("sy", lambda h: h.dma_start(IDENTF[:], identf_h[:]), inc="cs", n=16)
        v_c = R.op("sy", lambda h: h.dma_start(RMAT[:], rmat_h[:]), inc="cs", n=16)
        v_ones = R.op("g", lambda h: h.memset(ONES[:], 1.0), inc="cgp")
        for e in ("t", "sc", "v", "g", "sy"):
            R.wait(e, "cs", v_c)
            R.wait(e, "cgp", v_ones)

        p0 = contextlib.ExitStack()

        def phase_p0_ln():
            psb0 = lambda n_, s_, d_: p0.enter_context(nc.psum_tensor(n_, s_, d_))
            sb0 = lambda n_, s_, d_: p0.enter_context(nc.sbuf_tensor(n_, s_, d_))
            XIN = [sb0(f"XIN{i}", [128, E], F32) for i in range(3)]
            XNM = [sb0(f"XNM{i}", [128, E], BF16) for i in range(2)]
            JUNK = [sb0(f"JUNK{i}", [128, E], BF16) for i in range(2)]
            MB = sb0("MB", [128, 16], F32)
            psT = [psb0(f"psT0{i}", [128, 1024], F32) for i in range(2)]

            v_ln = [0] * 16
            v_ev = [0] * 16
            v_tr = [0] * 16
            if w_units:
                w_alloc()
                w_units[0]()
            for tc in range(16):
                par = tc % 2
                pin = tc % 3
                if tc >= 3:
                    R.wait("sy", "cv", v_ln[tc - 3])
                v_x = R.op("sy", lambda h, tc=tc, pin=pin: h.dma_start(
                    XIN[pin][:], x_h[tc * 128:(tc + 1) * 128, :]), inc="cs", n=16)
                mb = par * 8
                R.wait("v", "cs", v_x)
                vv = R.op("v", lambda h, pin=pin, mb=mb: h.reduce_sum(
                    MB[:, mb + 6:mb + 7], XIN[pin][:], AX.X), inc="cv")
                R.wait("v", "cv", vv)
                vv = R.op("v", lambda h, mb=mb: h.tensor_scalar(
                    MB[:, mb + 0:mb + 1], MB[:, mb + 6:mb + 7], CV[0], None,
                    ALU.mult), inc="cv")
                R.wait("sc", "cs", v_x)
                if tc >= 2:
                    R.wait("sc", "cv", v_ln[tc - 2])
                vs = R.op("sc", lambda h, par=par, pin=pin, mb=mb: h.activation(
                    JUNK[par][:], XIN[pin][:], AF.Square,
                    accum_out=MB[:, mb + 1:mb + 2]), inc="cc")
                R.wait("v", "cv", vv)
                vv = R.op("v", lambda h, mb=mb: h.tensor_tensor(
                    MB[:, mb + 2:mb + 3], MB[:, mb + 0:mb + 1],
                    MB[:, mb + 0:mb + 1], ALU.mult), inc="cv")
                R.wait("v", "cc", vs)
                R.wait("v", "cv", vv)
                vv = R.op("v", lambda h, mb=mb: h.scalar_tensor_tensor(
                    MB[:, mb + 3:mb + 4], MB[:, mb + 1:mb + 2], CV[0],
                    MB[:, mb + 2:mb + 3], ALU.mult, ALU.subtract), inc="cv")
                R.wait("sc", "cv", vv)
                vs = R.op("sc", lambda h, mb=mb: h.activation(
                    MB[:, mb + 5:mb + 6], MB[:, mb + 3:mb + 4], AF.Sqrt,
                    bias=CV[1]), inc="cc")
                R.wait("v", "cc", vs)
                vv = R.op("v", lambda h, mb=mb: h.reciprocal(
                    MB[:, mb + 4:mb + 5], MB[:, mb + 5:mb + 6]), inc="cv")
                R.wait("v", "cv", vv)
                if tc >= 2:
                    R.wait("v", "ct", v_tr[tc - 2])
                v_ln[tc] = R.op("v", lambda h, par=par, pin=pin, mb=mb: h.tensor_scalar(
                    XNM[par][:], XIN[pin][:], MB[:, mb + 0:mb + 1],
                    MB[:, mb + 4:mb + 5], ALU.subtract, ALU.mult), inc="cv")
                R.wait("t", "cv", v_ln[tc])
                if tc >= 2:
                    R.wait("t", "cc", v_ev[tc - 2])
                for ec in range(8):
                    R.op("t", lambda h, ec=ec, tc=tc, par=par: h.matmul(
                        psT[par][:, ec * 128:(ec + 1) * 128],
                        XNM[par][:, ec * 128:(ec + 1) * 128],
                        IDENT[:], start=True, stop=True))
                v_tr[tc] = R.op("t", None, inc="ct")
                R.wait("sc", "ct", v_tr[tc])
                v_ev[tc] = R.op("sc", lambda h, tc=tc, par=par: h.activation(
                    bass.AP(XNT, tc * 128, [[8 * S_LEN, 128], [S_LEN, 8], [1, 128]]),
                    bass.AP(psT[par], 0, [[1024, 128], [128, 8], [1, 128]]),
                    AF.Copy), inc="cc")
                for wu in w_units_for_chunk(tc):
                    wu()
            if "xnt" in taps:
                R.wait("sy", "cc", v_ev[15])
                R.op("sy", lambda h: h.dma_start(tap_h["xnt"], XNT[:]), inc="cs", n=16)
            return v_ev

        pA = contextlib.ExitStack()
        sbA = lambda name, shape, dtype: pA.enter_context(nc.sbuf_tensor(name, shape, dtype))
        WOTP = sbA("WOTP", [128, 8 * E], BF16)
        W123T = {nm: sbA(nm + "T", [128, 8 * E], BF16) for nm in ("w1", "w2", "w3")}
        pw = contextlib.ExitStack()
        wst = {"cc_ev": 0, "v_tdone": [0] * 4, "v_wl": 0}
        wbuf = {}

        def w_alloc():
            wbuf["WNAT"] = pw.enter_context(nc.sbuf_tensor("WNAT0", [128, 8 * E], BF16))
            wbuf["psTW"] = pw.enter_context(nc.psum_tensor("psTW", [128, 1024], F32))

        def w_dma_unit(i, nm):
            def go(i=i, nm=nm):
                if i >= 1:
                    R.wait("g", "ct", wst["v_tdone"][i - 1])
                wst["v_wl"] = R.op("g", lambda h, nm=nm: h.dma_start(
                    wbuf["WNAT"][:].rearrange("p (pc e) -> p pc e", pc=8),
                    w_h[nm].rearrange("(pc p) e -> p pc e", p=128)),
                    inc="cg", n=16)
            return go

        def w_blk_unit(i, nm, blk):
            def go(i=i, nm=nm, blk=blk):
                R.wait("t", "cg", wst["v_wl"])
                R.wait("t", "cc", wst["cc_ev"])
                for j in range(8):
                    if nm != "wo":
                        R.op("t", lambda h, blk=blk, j=j: h.matmul(
                            wbuf["psTW"][:, j * 128:(j + 1) * 128],
                            wbuf["WNAT"][:, j * E + blk * 128: j * E + blk * 128 + 128],
                            IDENT[:], start=True, stop=True))
                    else:
                        R.op("t", lambda h, blk=blk, j=j: h.matmul(
                            wbuf["psTW"][:, j * 128:(j + 1) * 128],
                            wbuf["WNAT"][:, j * E + blk: j * E + blk + 1017:8],
                            IDENT[:], start=True, stop=True))
                vt = R.op("t", None, inc="ct")
                R.wait("sc", "ct", vt)
                if nm != "wo":
                    wst["cc_ev"] = R.op("sc", lambda h, nm=nm, blk=blk: h.activation(
                        W123T[nm][:, blk * E:(blk + 1) * E], wbuf["psTW"][:], AF.Copy),
                        inc="cc")
                else:
                    wst["cc_ev"] = R.op("sc", lambda h, blk=blk: h.activation(
                        WOTP[:, blk * E:(blk + 1) * E], wbuf["psTW"][:], AF.Copy),
                        inc="cc")
                if blk == 7:
                    wst["v_tdone"][i] = R.vals["ct"]
            return go

        w_units = []
        if stage >= 2:
            for i, nm in enumerate(["w1", "w2", "w3", "wo"]):
                w_units.append(w_dma_unit(i, nm))
                for blk in range(8):
                    w_units.append(w_blk_unit(i, nm, blk))
        w_sched = {}
        for ui in range(1, len(w_units)):
            chunk = 2 + ((ui - 1) * 14) // max(len(w_units) - 1, 1)
            w_sched.setdefault(min(chunk, 15), []).append(w_units[ui])

        def w_units_for_chunk(tc):
            return w_sched.get(tc, [])

        def phase_w():
            R.wait("t", "cc", wst["cc_ev"])
        def phase_router():
            R.wait("g", "ct", R.vals["ct"])
            with contextlib.ExitStack() as p2r:
                psbr = lambda n_, s_, d_: p2r.enter_context(nc.psum_tensor(n_, s_, d_))
                sbr = lambda n_, s_, d_: p2r.enter_context(nc.sbuf_tensor(n_, s_, d_))
                RWB = sbr("RWB", [128, 8 * 16], BF16)
                LGF = sbr("LGF", [16, S_LEN], F32)
                LGT = sbr("LGT", [128, 256], F32)
                COMBT = sbr("COMBT", [128, 256], F32)
                RS = sbr("RS", [128, 16], F32)
                L2 = sbr("L2", [128, 32], F32)
                MS1 = sbr("MS1", [128, 32], F32)
                MS2 = sbr("MS2", [128, 32], F32)
                psL = psbr("psL", [16, 1024], F32)
                psLT = psbr("psLT", [128, 256], F32)
                v_rw = R.op("g", lambda h: h.dma_start(
                    RWB[:].rearrange("e (ec x) -> e ec x", ec=8),
                    rw_h.rearrange("(ec e) x -> e ec x", e=128)), inc="cg", n=16)
                R.wait("t", "cg", v_rw)
                v_lg = 0
                for hf2 in range(2):
                    R.wait("t", "cc", v_lg)
                    for ec in range(8):
                        for hf in range(2):
                            R.op("t", lambda h, ec=ec, hf=hf, hf2=hf2: h.matmul(
                                psL[:, hf * 512:(hf + 1) * 512],
                                RWB[:, ec * 16:(ec + 1) * 16],
                                XNT[:, ec * S_LEN + hf2 * 1024 + hf * 512:
                                    ec * S_LEN + hf2 * 1024 + hf * 512 + 512],
                                start=(ec == 0), stop=(ec == 7)))
                    vt = R.op("t", None, inc="ct")
                    R.wait("sc", "ct", vt)
                    v_lg = R.op("sc", lambda h, hf2=hf2: h.activation(
                        LGF[:, hf2 * 1024:(hf2 + 1) * 1024], psL[:], AF.Copy),
                        inc="cc")
                R.wait("t", "cc", v_lg)
                for tc in range(16):
                    R.op("t", lambda h, tc=tc: h.matmul(
                        psLT[:, tc * 16:(tc + 1) * 16],
                        LGF[:, tc * 128:(tc + 1) * 128], IDENTF[0:16, 0:16], start=True, stop=True))
                vt = R.op("t", None, inc="ct")
                R.wait("sc", "ct", vt)
                v_lgt = R.op("sc", lambda h: h.activation(LGT[:], psLT[:], AF.Copy),
                             inc="cc")
                R.wait("v", "cc", v_lgt)
                for tc in range(16):
                    rs = (tc % 2) * 8
                    mo = (tc % 2) * 16
                    vv = R.op("v", lambda h, tc=tc, rs=rs: h.reduce_max(
                        RS[:, rs + 0:rs + 1], LGT[:, tc * 16:(tc + 1) * 16], AX.X),
                        inc="cv")
                    R.wait("v", "cv", vv)
                    vv = R.op("v", lambda h, tc=tc, rs=rs, mo=mo: h.tensor_scalar(
                        MS1[:, mo:mo + 16], LGT[:, tc * 16:(tc + 1) * 16],
                        RS[:, rs + 0:rs + 1], None, ALU.is_equal), inc="cv")
                    R.wait("v", "cv", vv)
                    vv = R.op("v", lambda h, tc=tc, mo=mo: h.scalar_tensor_tensor(
                        L2[:, mo:mo + 16], MS1[:, mo:mo + 16], CV[2],
                        LGT[:, tc * 16:(tc + 1) * 16], ALU.mult, ALU.add), inc="cv")
                    R.wait("v", "cv", vv)
                    vv = R.op("v", lambda h, rs=rs, mo=mo: h.reduce_max(
                        RS[:, rs + 1:rs + 2], L2[:, mo:mo + 16], AX.X), inc="cv")
                    R.wait("v", "cv", vv)
                    vv = R.op("v", lambda h, rs=rs, mo=mo: h.tensor_scalar(
                        MS2[:, mo:mo + 16], L2[:, mo:mo + 16], RS[:, rs + 1:rs + 2],
                        None, ALU.is_equal), inc="cv")
                    R.wait("v", "cv", vv)
                    vv = R.op("v", lambda h, rs=rs: h.tensor_tensor(
                        RS[:, rs + 2:rs + 3], RS[:, rs + 1:rs + 2],
                        RS[:, rs + 0:rs + 1], ALU.subtract), inc="cv")
                    R.wait("sc", "cv", vv)
                    vs = R.op("sc", lambda h, rs=rs: h.activation(
                        RS[:, rs + 4:rs + 5], RS[:, rs + 2:rs + 3], AF.Sigmoid),
                        inc="cc")
                    R.wait("sc", "cc", vs)
                    vs = R.op("sc", lambda h, rs=rs: h.activation(
                        RS[:, rs + 3:rs + 4], RS[:, rs + 2:rs + 3], AF.Sigmoid,
                        scale=CV[3]), inc="cc")
                    R.wait("v", "cc", vs)
                    vv = R.op("v", lambda h, tc=tc, rs=rs, mo=mo: h.tensor_scalar(
                        COMBT[:, tc * 16:(tc + 1) * 16], MS1[:, mo:mo + 16],
                        RS[:, rs + 3:rs + 4], None, ALU.mult), inc="cv")
                    R.wait("v", "cv", vv)
                    R.op("v", lambda h, tc=tc, rs=rs, mo=mo: h.scalar_tensor_tensor(
                        COMBT[:, tc * 16:(tc + 1) * 16], MS2[:, mo:mo + 16],
                        RS[:, rs + 4:rs + 5], COMBT[:, tc * 16:(tc + 1) * 16],
                        ALU.mult, ALU.add), inc="cv")
                v_combt = R.vals["cv"]
                if "comb" in taps:
                    R.wait("sy", "cv", v_combt)
                    R.op("sy", lambda h: h.dma_start(tap_h["comb"], COMBT[:]),
                         inc="cs", n=16)
                R.wait("t", "cv", v_combt)
                v_combf = 0
                for r2 in range(2):
                    R.wait("t", "cc", v_combf)
                    for j in range(8):
                        tc = r2 * 8 + j
                        R.op("t", lambda h, tc=tc, j=j: h.matmul(
                            psL[:, j * 128:(j + 1) * 128],
                            COMBT[:, tc * 16:(tc + 1) * 16], IDENTF[:],
                            start=True, stop=True))
                    vt = R.op("t", None, inc="ct")
                    R.wait("sc", "ct", vt)
                    v_combf = R.op("sc", lambda h, r2=r2: h.activation(
                        COMBF[:, r2 * 1024:(r2 + 1) * 1024], psL[:], AF.Copy),
                        inc="cc")
                R.wait("g", "cc", v_combf)
                v_combd = R.op("g", lambda h: h.dma_start(combd_h, COMBF[:]),
                               inc="cg", n=16)
            return v_combd
        if stage >= 1:
            _v_ev_saved = phase_p0_ln()
        if stage >= 2:
            phase_w()
        pw.close()
        _v_combd = phase_router() if (do_ffn and stage >= 4) else 0
        if stage < 1:
            vz = R.op("v", lambda h: h.memset(XNT[:], 0.0), inc="cv")
            R.wait("t", "cv", vz)
        p0.close()
        R.barrier()

        if do_attn and stage >= 3:
            COST = sbA("COST", [128, S_LEN], BF16)
            SINT = sbA("SINT", [128, S_LEN], BF16)
            R.op("sy", lambda h: h.dma_start(COST[:], cost_h[:]), inc="cs", n=16)
            v_tab = R.op("sy", lambda h: h.dma_start(SINT[:], sint_h[:]), inc="cs", n=16)

            QHB = sbA("QHB", [128, S_LEN], BF16)
            KHB = sbA("KHB", [128, S_LEN], BF16)
            QR = sbA("QR", [128, S_LEN], BF16)
            KR = sbA("KR", [128, S_LEN], BF16)
            VT = sbA("VT", [128, S_LEN], BF16)
            T1 = sbA("T1", [128, 512], BF16)
            T2 = sbA("T2", [128, 512], BF16)
            PT = [sbA(f"PT{i}", [128, 1024], BF16) for i in range(3)]
            OSB = KHB
            OT = QHB
            ASB = OSB
            RSROW = sbA("RSROW", [1, S_LEN], BF16)
            RECT = sbA("RECT", [128, 16], F32)

            v_rope_prev = 0
            v_rope_hist = [0, 0]
            v_accev_prev = 0
            v_exp_carry = [0, 0]
            v_pv_carry3 = [0, 0, 0]
            v_oe = 0
            for head in range(n_heads):
                with contextlib.ExitStack() as p1a:
                    psba = lambda n_, s_, d_: p1a.enter_context(nc.psum_tensor(n_ + f"_{head}", s_, d_))
                    psQK = psba("psQK", [128, S_LEN], F32)
                    psV = psba("psV", [128, 1024], F32)
                    psSH = psba("psSH", [128, 1024], F32)

                    def proj_qk(wt, dst, head=head):
                        for pc in range(8):
                            for ec in range(8):
                                R.op("t", lambda h, pc=pc, ec=ec, wt=wt, head=head: h.matmul(
                                    psQK[:, pc * TH:(pc + 1) * TH],
                                    W123T[wt][:, ec * E + pc * 128: ec * E + pc * 128 + 128],
                                    XNT[:, ec * S_LEN + head * TH: ec * S_LEN + head * TH + TH],
                                    start=(ec == 0), stop=(ec == 7)))
                        vt = R.op("t", None, inc="ct")
                        R.wait("sc", "ct", vt)
                        return R.op("sc", lambda h, dst=dst: h.activation(
                            dst[:], psQK[:], AF.Copy), inc="cc")

                    R.wait("sc", "cv", v_rope_prev)
                    v_q = proj_qk("w1", QHB)
                    R.wait("t", "cc", v_q)
                    v_k = proj_qk("w2", KHB)

                    v_vev = 0
                    for blk in range(2):
                        R.wait("t", "cc", v_vev)
                        for ec in range(8):
                            for hf in range(2):
                                R.op("t", lambda h, ec=ec, hf=hf, blk=blk, head=head: h.matmul(
                                    psV[:, hf * 512:(hf + 1) * 512],
                                    XNT[:, ec * S_LEN + head * TH + blk * 128:
                                        ec * S_LEN + head * TH + blk * 128 + 128],
                                    W123T["w3"][:, ec * E + hf * 512: ec * E + hf * 512 + 512],
                                    start=(ec == 0), stop=(ec == 7)))
                        vt = R.op("t", None, inc="ct")
                        R.wait("sc", "ct", vt)
                        v_vev = R.op("sc", lambda h, blk=blk: h.activation(
                            bass.AP(VT, blk * 128, [[S_LEN, 128], [TH, 8], [1, 128]]),
                            bass.AP(psV, 0, [[1024, 128], [128, 8], [1, 128]]),
                            AF.Copy), inc="cc")

                    def rope(src, dst, v_src):
                        vq = 0
                        for j in range(4):
                            par = j % 2
                            R.wait("t", "cc", v_src)
                            R.wait("t", "cv", v_rope_hist[par])
                            vt = R.op("t", lambda h, j=j, par=par, src=src: h.matmul(
                                psSH[:, par * 512:(par + 1) * 512], RMAT[:],
                                src[:, j * 512:(j + 1) * 512], start=True, stop=True),
                                inc="ct")
                            R.wait("v", "cs", v_tab)
                            R.wait("v", "cc", v_src)
                            R.op("v", lambda h, j=j, src=src: h.tensor_tensor(
                                T1[:], src[:, j * 512:(j + 1) * 512],
                                COST[:, j * 512:(j + 1) * 512], ALU.mult), inc="cv")
                            R.wait("v", "ct", vt)
                            vv = R.op("v", lambda h, j=j, par=par: h.tensor_tensor(
                                T2[:], psSH[:, par * 512:(par + 1) * 512],
                                SINT[:, j * 512:(j + 1) * 512], ALU.mult), inc="cv")
                            R.wait("v", "cv", vv)
                            vq = R.op("v", lambda h, j=j, dst=dst: h.tensor_tensor(
                                dst[:, j * 512:(j + 1) * 512], T1[:], T2[:], ALU.add),
                                inc="cv")
                            v_rope_hist[par] = vq
                        return vq

                    v_qr = rope(QHB, QR, v_q)
                    v_kr = rope(KHB, KR, v_k)
                    v_rope_prev = v_kr

                if head == 0 and "qr0" in taps:
                    R.wait("sy", "cv", v_qr)
                    R.op("sy", lambda h: h.dma_start(tap_h["qr0"], QR[:]), inc="cs", n=16)
                    R.wait("sy", "cv", v_kr)
                    R.op("sy", lambda h: h.dma_start(tap_h["kr0"], KR[:]), inc="cs", n=16)
                if head == 0 and "vt0" in taps:
                    R.wait("sy", "cc", R.vals["cc"])
                    R.op("sy", lambda h: h.dma_start(tap_h["vt0"], VT[:]), inc="cs", n=16)

                with contextlib.ExitStack() as p1b:
                    psbb = lambda n_, s_, d_: p1b.enter_context(nc.psum_tensor(n_ + f"_{head}", s_, d_))
                    psS = [psbb(f"psS{i}", [128, 1024], F32) for i in range(2)]
                    psO = psbb("psO", [128, 1024], F32)
                    psR = psbb("psR", [1, 1024], F32)
                    R.wait("t", "cv", v_qr)
                    R.wait("t", "cv", v_kr)
                    v_exp = [0] * 32
                    v_pv = [0] * 32
                    v_s = [0] * 32
                    for kk in range(34):
                        if kk < 32:
                            qh, kt = kk // 16, kk % 16
                            par = kk % 2
                            R.wait("t", "cc",
                                   v_exp[kk - 2] if kk >= 2 else v_exp_carry[kk])
                            for hf in range(2):
                                R.op("t", lambda h, kt=kt, par=par, qh=qh, hf=hf: h.matmul(
                                    psS[par][:, hf * 512:(hf + 1) * 512],
                                    KR[:, kt * 128:(kt + 1) * 128],
                                    QR[:, qh * 1024 + hf * 512: qh * 1024 + hf * 512 + 512],
                                    start=True, stop=True))
                            v_s[kk] = R.op("t", None, inc="ct")
                            R.wait("sc", "ct", v_s[kk])
                            R.wait("sc", "ct",
                                   v_pv[kk - 3] if kk >= 3 else v_pv_carry3[kk])
                            v_exp[kk] = R.op("sc", lambda h, par3=kk % 3, par=par: h.activation(
                                PT[par3][:], psS[par][:], AF.Exp,
                                bias=SOFTMAX_SHIFT, scale=INV_SQRT8), inc="cc")
                        if kk >= 2:
                            pk = kk - 2
                            qh, kt = pk // 16, pk % 16
                            par = pk % 2
                            R.wait("t", "cc", v_exp[pk])
                            if kt == 0:
                                R.wait("t", "cc", v_oe)
                            par3 = pk % 3
                            for hf in range(2):
                                R.op("t", lambda h, kt=kt, par3=par3, hf=hf: h.matmul(
                                    psO[:, hf * 512:(hf + 1) * 512],
                                    VT[:, kt * 128:(kt + 1) * 128],
                                    PT[par3][:, hf * 512:(hf + 1) * 512],
                                    start=(kt == 0), stop=(kt == 15),
                                    skip_group_check=True))
                            for hf in range(2):
                                R.op("t", lambda h, kt=kt, par3=par3, hf=hf: h.matmul(
                                    psR[:, hf * 512:(hf + 1) * 512],
                                    ONES[:],
                                    PT[par3][:, hf * 512:(hf + 1) * 512],
                                    start=(kt == 0), stop=(kt == 15),
                                    skip_group_check=True))
                            v_pv[pk] = R.op("t", None, inc="ct")
                            if kt == 15:
                                R.wait("sc", "ct", v_pv[pk])
                                R.op("sc", lambda h, qh=qh: h.activation(
                                    RSROW[0:1, qh * 1024:(qh + 1) * 1024], psR[0:1, :],
                                    AF.Copy), inc="cc")
                                v_oe = R.op("sc", lambda h, qh=qh: h.activation(
                                    OSB[:, qh * 1024:(qh + 1) * 1024], psO[:],
                                    AF.Copy), inc="cc")
                    v_exp_carry = [v_exp[30], v_exp[31]]
                    v_pv_carry3 = [v_pv[31]] * 3

                with contextlib.ExitStack() as p1c:
                    psbc = lambda n_, s_, d_: p1c.enter_context(nc.psum_tensor(n_ + f"_{head}", s_, d_))
                    psAm = psbc("psAm", [128, 1024], F32)
                    psTT = psbc("psTT", [128, 1024], F32)
                    R.wait("t", "cc", v_oe)
                    cc_ev = v_accev_prev
                    for j in range(16):
                        R.op("t", lambda h, j=j: h.matmul(
                            psAm[:, j:j + 1],
                            RSROW[0:1, j * 128:(j + 1) * 128], IDENT[0:1, 0:1], start=True, stop=True))
                    vt_rs = R.op("t", None, inc="ct")
                    R.wait("v", "ct", vt_rs)
                    v_rec = R.op("v", lambda h: h.reciprocal(
                        RECT[:], psAm[:, 0:16]), inc="cv")
                    for r in range(2):
                        R.wait("t", "cc", cc_ev)
                        for j in range(8):
                            R.op("t", lambda h, r=r, j=j: h.matmul(
                                psTT[:, j * 128:(j + 1) * 128],
                                OSB[:, (r * 8 + j) * 128:(r * 8 + j) * 128 + 128],
                                IDENT[:], start=True, stop=True))
                        vt = R.op("t", None, inc="ct")
                        R.wait("sc", "ct", vt)
                        R.wait("sc", "cv", v_rec)
                        for j in range(8):
                            kt2 = r * 8 + j
                            cc_ev = R.op("sc", lambda h, kt2=kt2, j=j: h.activation(
                                OT[:, kt2 * 128:(kt2 + 1) * 128],
                                psTT[:, j * 128:(j + 1) * 128], AF.Copy,
                                scale=RECT[:, kt2:kt2 + 1]), inc="cc")
                    if head == 0 and "ot0" in taps:
                        R.wait("sy", "cc", cc_ev)
                        R.op("sy", lambda h: h.dma_start(tap_h["ot0"], OT[:]), inc="cs",
                             n=16)
                    for u in range(2):
                        R.wait("t", "cc", cc_ev)
                        R.wait("t", "cv", v_rec)
                        for i, kt in enumerate(range(u, 16, 2)):
                            for hf in range(2):
                                R.op("t", lambda h, kt=kt, i=i, hf=hf: h.matmul(
                                    psAm[:, hf * 512:(hf + 1) * 512],
                                    OT[:, kt * 128:(kt + 1) * 128],
                                    WOTP[:, (kt // 2) * E + hf * 512:
                                         (kt // 2) * E + hf * 512 + 512],
                                    start=(i == 0), stop=(i == 7)))
                        vt = R.op("t", None, inc="ct")
                        R.wait("sc", "ct", vt)
                        v_asb = R.op("sc", lambda h: h.activation(
                            ASB[:, 0:1024], psAm[:], AF.Copy), inc="cc")
                        R.wait("t", "cc", v_asb)
                        for cc_i in range(8):
                            R.op("t", lambda h, cc_i=cc_i: h.matmul(
                                psTT[:, cc_i * 128:(cc_i + 1) * 128],
                                ASB[:, cc_i * 128:(cc_i + 1) * 128], IDENT[:], start=True, stop=True))
                        vt = R.op("t", None, inc="ct")
                        R.wait("sc", "ct", vt)
                        cc_ev = R.op("sc", lambda h, u=u, head=head: h.activation(
                            bass.AP(ACC, (u * 8) * E + head * 128,
                                    [[16 * E, 128], [E, 8], [1, 128]]),
                            bass.AP(psTT, 0, [[1024, 128], [128, 8], [1, 128]]),
                            AF.Copy), inc="cc")
                    v_accev_prev = cc_ev
            if "acc1" in taps:
                R.wait("sy", "cc", R.vals["cc"])
                R.op("sy", lambda h: h.dma_start(tap_h["acc1"], ACC[:]), inc="cs", n=16)
        if not (do_attn and stage >= 3):
            vz = R.op("v", lambda h: h.memset(ACC[:], 0.0), inc="cv")
            R.wait("t", "cv", vz)
        R.barrier()

        pA.close()

        if do_ffn and stage >= 4:
            pass
            R.barrier()

            with contextlib.ExitStack() as p2e:
                psbe = lambda n_, s_, d_: p2e.enter_context(nc.psum_tensor(n_, s_, d_))
                sbe = lambda n_, s_, d_: p2e.enter_context(nc.sbuf_tensor(n_, s_, d_))
                psG1 = psbe("psG1", [128, 1024], F32)
                psG2 = psbe("psG2", [128, 1024], F32)
                psD = [psbe(f"psD{i}", [128, 1024], F32) for i in range(2)]
                HARE = sbe("HARE", [128, 8 * S_LEN], BF16)
                SG = sbe("SG", [128, S_LEN], BF16)
                TFFN = sbe("TFFN", [128, S_LEN], BF16)
                WGA = sbe("WGA", [128, 8 * 512], BF16)
                WUA = sbe("WUA", [128, 8 * 512], BF16)
                WDA = sbe("WDA", [128, 8 * 1024], BF16)
                CB = [sbe(f"CB{i}", [128, S_LEN], BF16) for i in range(2)]

                groups = [("r", [4 * g + i for i in range(4)]) for g in range(4)]
                groups.append(("s", [0, 1]))
                XI = [XNT[:, 0:2048].bitcast(F32), XNT[:, 2048:4096].bitcast(F32),
                      XNT[:, 8192:10240].bitcast(F32)]
                OU = [XNT[:, 4096:6144].bitcast(F32), XNT[:, 6144:8192].bitcast(F32)]
                v_ad3 = [0] * 16
                v_st3 = [0] * 16
                v_wg_used = {}
                v_wd_used = {}
                v_cb_used = {}
                v_prev_add = [0, 0]
                v_silu_prev = 0
                v_h_prev = 0
                for gi, (gkind, gl) in enumerate(groups):
                    shared = gkind == "s"
                    nmc = 4 if shared else 2
                    mid = 512 if shared else 256
                    ndslots = 2 if shared else 4
                    v_h = []
                    v_ld_all = 0
                    for li, e in enumerate(gl):
                        sl_g = 0 if shared else li % 2
                        sl_d = li % ndslots
                        wg_off = sl_g * 8 * 256 if not shared else 0
                        wd_off = sl_d * (4 * E if shared else 2 * E)
                        R.wait("g", "cv", v_wg_used.get(sl_g, 0))
                        src_g = (wg_s_h if shared else wg_r_h)[e].rearrange(
                            "(ec e2) m -> e2 ec m", e2=128)
                        v_lg_ = R.op("g", lambda h, src=src_g, off=wg_off, mid=mid:
                                     h.dma_start(WGA[:, off:off + 8 * mid].rearrange(
                                         "p (ec m) -> p ec m", ec=8), src),
                                     inc="cg", n=16)
                        src_u = (wu_s_h if shared else wu_r_h)[e].rearrange(
                            "(ec e2) m -> e2 ec m", e2=128)
                        v_lu_ = R.op("g", lambda h, src=src_u, off=wg_off, mid=mid:
                                     h.dma_start(WUA[:, off:off + 8 * mid].rearrange(
                                         "p (ec m) -> p ec m", ec=8), src),
                                     inc="cg", n=16)
                        R.wait("g", "cv", v_wd_used.get(sl_d, 0))
                        src_d = (wd_s_h if shared else wd_r_h)[e].rearrange(
                            "(mc m) x -> m mc x", m=128)
                        v_ld_all = R.op("g", lambda h, src=src_d, off=wd_off, nmc=nmc:
                                        h.dma_start(WDA[:, off:off + nmc * E].rearrange(
                                            "p (mc x) -> p mc x", mc=nmc), src),
                                        inc="cg", n=16)
                        v_cb_ = 0
                        if not shared:
                            R.wait("g", "cv", v_cb_used.get(sl_g, 0))
                            v_cb_ = R.op("g", lambda h, e=e, sl=sl_g: h.dma_start(
                                CB[sl][:], combd_h[e:e + 1, :].partition_broadcast(128)),
                                inc="cg", n=16)
                        for mc in range(nmc):
                            for th in range(2):
                                R.wait("t", "cg", v_lg_)
                                R.wait("t", "cc", v_silu_prev)
                                for ec in range(8):
                                    for hf in range(2):
                                        R.op("t", lambda h, ec=ec, hf=hf, mc=mc, th=th,
                                             off=wg_off, mid=mid: h.matmul(
                                            psG1[:, hf * 512:(hf + 1) * 512],
                                            WGA[:, off + ec * mid + mc * 128:
                                                off + ec * mid + mc * 128 + 128],
                                            XNT[:, ec * S_LEN + th * 1024 + hf * 512:
                                                ec * S_LEN + th * 1024 + hf * 512 + 512],
                                            start=(ec == 0), stop=(ec == 7)))
                                vt_g = R.op("t", None, inc="ct")
                                R.wait("sc", "ct", vt_g)
                                R.wait("sc", "cv", v_h_prev)
                                v_silu_prev = R.op("sc", lambda h, th=th: h.activation(
                                    SG[:, th * 1024:(th + 1) * 1024], psG1[:],
                                    AF.Silu), inc="cc")
                                R.wait("t", "cg", v_lu_)
                                R.wait("t", "cv", v_h_prev)
                                for ec in range(8):
                                    for hf in range(2):
                                        R.op("t", lambda h, ec=ec, hf=hf, mc=mc, th=th,
                                             off=wg_off, mid=mid: h.matmul(
                                            psG2[:, hf * 512:(hf + 1) * 512],
                                            WUA[:, off + ec * mid + mc * 128:
                                                off + ec * mid + mc * 128 + 128],
                                            XNT[:, ec * S_LEN + th * 1024 + hf * 512:
                                                ec * S_LEN + th * 1024 + hf * 512 + 512],
                                            start=(ec == 0), stop=(ec == 7)))
                                vt_u = R.op("t", None, inc="ct")
                                R.wait("v", "ct", vt_u)
                                R.wait("v", "cc", v_silu_prev)
                                hslot = li * nmc + mc
                                base = hslot * S_LEN + th * 1024
                                if not shared:
                                    R.wait("v", "cg", v_cb_)
                                    vv = R.op("v", lambda h, th=th: h.tensor_tensor(
                                        TFFN[:, 0:1024], SG[:, th * 1024:(th + 1) * 1024],
                                        psG2[:], ALU.mult), inc="cv")
                                    R.wait("v", "cv", vv)
                                    vv = R.op("v", lambda h, base=base, th=th, sl=sl_g:
                                              h.tensor_tensor(
                                        HARE[:, base:base + 1024], TFFN[:, 0:1024],
                                        CB[sl][:, th * 1024:(th + 1) * 1024],
                                        ALU.mult), inc="cv")
                                else:
                                    vv = R.op("v", lambda h, base=base, th=th:
                                              h.tensor_tensor(
                                        HARE[:, base:base + 1024],
                                        SG[:, th * 1024:(th + 1) * 1024],
                                        psG2[:], ALU.mult), inc="cv")
                                v_h_prev = vv
                                v_h.append(vv)
                        v_wg_used[sl_g] = v_h_prev
                        if not shared:
                            v_cb_used[sl_g] = v_h_prev
                    v_hall = max(v_h)
                    R.wait("t", "cv", v_hall)
                    R.wait("t", "cg", v_ld_all)
                    v_add = [0] * 16
                    for tc in range(16):
                        par = tc % 2
                        prev = v_add[tc - 2] if tc >= 2 else v_prev_add[par]
                        R.wait("t", "cv", prev)
                        first = True
                        nexp = len(gl)
                        for li in range(nexp):
                            sl_d = li % ndslots
                            wd_off2 = sl_d * (4 * E if shared else 2 * E)
                            for mc in range(nmc):
                                hslot = li * nmc + mc
                                last = (li == nexp - 1 and mc == nmc - 1)
                                for hf in range(2):
                                    R.op("t", lambda h, tc=tc, par=par, hslot=hslot,
                                         off=wd_off2, mc=mc, first=first, hf=hf, last=last:
                                         h.matmul(
                                        psD[par][:, hf * 512:(hf + 1) * 512],
                                        HARE[:, hslot * S_LEN + tc * 128:
                                             hslot * S_LEN + tc * 128 + 128],
                                        WDA[:, off + mc * E + hf * 512:
                                            off + mc * E + hf * 512 + 512],
                                        start=first, stop=last,
                                        skip_group_check=True))
                                first = False
                        vt = R.op("t", None, inc="ct")
                        R.wait("v", "ct", vt)
                        v_add[tc] = R.op("v", lambda h, tc=tc, par=par: h.tensor_tensor(
                            ACC[:, tc * E:(tc + 1) * E], psD[par][:],
                            ACC[:, tc * E:(tc + 1) * E], ALU.add), inc="cv")
                        if shared:
                            pe = tc % 3
                            if tc < 3:
                                R.wait("sy", "cv", v_hall)
                            else:
                                R.wait("sy", "cv", v_ad3[tc - 3])
                            v_xl = R.op("sy", lambda h, tc=tc, pe=pe: h.dma_start(
                                XI[pe], x_h[tc * 128:(tc + 1) * 128, :]),
                                inc="cs", n=16)
                            R.wait("v", "cs", v_xl)
                            if tc >= 2:
                                R.wait("v", "cst", v_st3[tc - 2])
                            v_ad3[tc] = R.op("v", lambda h, tc=tc, par=par, pe=pe:
                                             h.tensor_tensor(
                                OU[par], ACC[:, tc * E:(tc + 1) * E], XI[pe],
                                ALU.add), inc="cv")
                            R.wait("sc", "cv", v_ad3[tc])
                            v_st3[tc] = R.op("sc", lambda h, tc=tc, par=par:
                                             h.dma_start(
                                out_h[tc * 128:(tc + 1) * 128, :], OU[par]),
                                inc="cst", n=16)
                    v_prev_add = [v_add[14], v_add[15]]
                    for sl in range(ndslots):
                        v_wd_used[sl] = v_add[15]
                    v_add_final = list(v_add)
                    v_hall_last = v_hall
        if (not do_ffn or stage < 4) or "accf" in taps:
            R.barrier()
            v_add_final = [R.vals["cv"]] * 16
            v_hall_last = R.vals["cv"]
        if "accf" in taps:
            R.op("sy", lambda h: h.dma_start(tap_h["accf"], ACC[:]), inc="cs", n=16)

        if not (do_ffn and stage >= 4):
            XI = [XNT[:, 0:2048].bitcast(F32), XNT[:, 2048:4096].bitcast(F32)]
            OU = [XNT[:, 4096:6144].bitcast(F32), XNT[:, 6144:8192].bitcast(F32)]
            v_st = [0] * 16
            v_ad = [0] * 16
            for tc in range(16):
                par = tc % 2
                if tc < 2:
                    R.wait("sy", "cv", v_hall_last)
                if tc >= 2:
                    R.wait("sy", "cv", v_ad[tc - 2])
                    R.wait("sy", "cs", v_st[tc - 2])
                v_xl = R.op("sy", lambda h, tc=tc, par=par: h.dma_start(
                    XI[par], x_h[tc * 128:(tc + 1) * 128, :]), inc="cs", n=16)
                R.wait("v", "cs", v_xl)
                R.wait("v", "cv", v_add_final[tc])
                v_ad[tc] = R.op("v", lambda h, tc=tc, par=par: h.tensor_tensor(
                    OU[par], ACC[:, tc * E:(tc + 1) * E], XI[par], ALU.add),
                    inc="cv")
                R.wait("sc", "cv", v_ad[tc])
                v_st[tc] = R.op("sc", lambda h, tc=tc, par=par: h.dma_start(
                    out_h[tc * 128:(tc + 1) * 128, :], OU[par]), inc="cs", n=16)
            R.wait("sy", "cs", R.vals["cs"])
            R.wait("g", "cs", R.vals["cs"])
        R.wait("sy", "cs", R.vals["cs"])
        R.wait("g", "cs", R.vals["cs"])
        R.wait("sy", "cst", R.vals["cst"])
        R.wait("g", "cst", R.vals["cst"])

        with nc.Block() as block:
            R.replay(block)
    return nc


def make_in_map(inputs, core_id):
    hc = host_constants()
    m = {
        "x": np.ascontiguousarray(np.asarray(inputs["L0_attn_emb"])[core_id]),
        "w1": np.asarray(inputs["L0_attn_W1"]),
        "w2": np.asarray(inputs["L0_attn_W2"]),
        "w3": np.asarray(inputs["L0_attn_W3"]),
        "wo": np.asarray(inputs["L0_attn_Wo"]),
        "rw": np.asarray(inputs["L0_ffn_router_w"]),
        "wg_r": np.asarray(inputs["L0_ffn_wg_r"]),
        "wu_r": np.asarray(inputs["L0_ffn_wu_r"]),
        "wd_r": np.asarray(inputs["L0_ffn_wd_r"]),
        "wg_s": np.asarray(inputs["L0_ffn_wg_s"]),
        "wu_s": np.asarray(inputs["L0_ffn_wu_s"]),
        "wd_s": np.asarray(inputs["L0_ffn_wd_s"]),
        "cost": hc["cost"], "sint": hc["sint"], "ident": hc["ident"],
        "identf": hc["identf"], "rmat": hc["rmat"], "consts": hc["consts"],
    }
    return m


def kernel(**inputs):
    nc = build()
    in_maps = [make_in_map(inputs, c) for c in range(8)]
    res = run_bass_kernel_spmd(nc, in_maps, core_ids=list(range(8)))
    import numpy as _np
    return _np.stack([_np.asarray(r["out"]) for r in res.results], axis=0)


# version 16
# speedup vs baseline: 1.4881x; 1.0088x over previous
import contextlib
import numpy as np

import concourse.bass as bass
import concourse.mybir as mybir
from concourse.bass_utils import run_bass_kernel_spmd

F32 = mybir.dt.float32
BF16 = mybir.dt.bfloat16
AF = mybir.ActivationFunctionType
ALU = mybir.AluOpType
AX = mybir.AxisListType

S_LEN = 2048
E = 1024
NH = 8
TH = 256

INV_SQRT8 = 0.35355339059327373
SOFTMAX_SHIFT = -30.0
CV = [1.0 / 1024.0, 1e-5, -1e30, -1.0, INV_SQRT8, SOFTMAX_SHIFT]

ENG_NAMES = {"t": "tensor", "sc": "scalar", "v": "vector", "g": "gpsimd", "sy": "sync"}


class Rec:
    def __init__(self, nc, stack):
        self.nc = nc
        self.q = {e: [] for e in ENG_NAMES}
        self.hwm = {}
        self.sems = {}
        self.vals = {}
        for name in ("cg", "cs", "cst", "ct", "cc", "cv", "cgp",
                     "et", "ec", "ev2", "eg"):
            self.sems[name] = stack.enter_context(nc.semaphore(name))
            self.vals[name] = 0

    EVT = {"sc": "ec", "v": "ev2", "g": "eg"}

    def op(self, eng, fn, inc=None, n=1):
        if inc is None:
            if eng == "t":
                self.q[eng].append({"fn": fn, "inc": None})
            else:
                self.q[eng].append(fn)
            return None
        sem = self.sems[inc]
        self.vals[inc] += n
        v = self.vals[inc]
        if eng == "t":
            if fn is None:
                last = self.q[eng][-1]
                assert isinstance(last, dict) and last["inc"] is None, \
                    "no tensor instruction to attach inc to"
                last["inc"] = (sem, n)
            else:
                self.q[eng].append({"fn": fn, "inc": (sem, n)})
            return v
        if inc in ("cs", "cg", "cst"):
            def wrapped(h, fn=fn, sem=sem, n=n):
                fn(h).then_inc(sem, n)

            self.q[eng].append(wrapped)
        else:
            en = self.EVT[eng]
            esem = self.sems[en]
            self.vals[en] += 1
            ev = self.vals[en]

            def wrapped(h, fn=fn, sem=sem, n=n, esem=esem, ev=ev):
                fn(h).then_inc(esem, 1)
                h.wait_ge(esem, ev)
                h.nop().then_inc(sem, n)

            self.q[eng].append(wrapped)
        return v

    def wait(self, eng, cname, v):
        if not v or v <= 0:
            return
        hw = self.hwm.setdefault((eng, cname), 0)
        if v <= hw:
            return
        self.hwm[(eng, cname)] = v
        sem = self.sems[cname]
        self.q[eng].append(lambda h, sem=sem, v=v: h.wait_ge(sem, v))

    def barrier(self):
        engs = ("t", "sc", "v", "g", "sy")
        for e in ("sc", "v", "g"):
            self.op(e, lambda h: h.nop(),
                    inc={"sc": "cc", "v": "cv", "g": "cgp"}[e])
        if self.q["t"] and isinstance(self.q["t"][-1], dict) \
                and self.q["t"][-1]["inc"] is None:
            self.op("t", None, inc="ct")
        snap = dict(self.vals)
        for e in engs:
            for cn in ("ct", "cc", "cv", "cgp", "cg", "cs"):
                if e == "t" and cn == "ct":
                    continue
                self.wait(e, cn, snap[cn])

    def replay(self, block):
        for eng, name in ENG_NAMES.items():
            ops = self.q[eng]

            def body(h, ops=ops):
                for f in ops:
                    if isinstance(f, dict):
                        instr = f["fn"](h)
                        if f["inc"] is not None:
                            instr.then_inc(*f["inc"])
                    else:
                        f(h)

            getattr(block, name)(body)


def host_constants():
    import ml_dtypes
    DH = 128
    d = {}
    inv_freq = 1.0 / 10000.0 ** (np.arange(0, DH, 2, dtype=np.float64) / DH)
    t = np.arange(S_LEN, dtype=np.float64)
    freqs = np.repeat(t[:, None] * inv_freq[None, :], 2, axis=1)
    COS = np.cos(freqs)
    SIN2 = np.sin(freqs)
    SIN2[:, 0::2] *= -1.0
    tau_to_sp = np.empty(S_LEN, np.int64)
    for c in range(8):
        for sl in range(TH):
            tau_to_sp[c * TH + sl] = 8 * sl + c
    d["cost"] = np.ascontiguousarray(COS[tau_to_sp, :].T).astype(ml_dtypes.bfloat16)
    d["sint"] = np.ascontiguousarray(SIN2[tau_to_sp, :].T).astype(ml_dtypes.bfloat16)
    d["ident"] = np.eye(128, dtype=ml_dtypes.bfloat16)
    d["identf"] = np.eye(128, dtype=np.float32)
    r = np.zeros((128, 128), np.float32)
    for i in range(128):
        r[i ^ 1, i] = 1.0
    d["rmat"] = r.astype(ml_dtypes.bfloat16)
    consts = np.zeros((128, 8), np.float32)
    for i, v in enumerate(CV):
        consts[:, i] = v
    d["consts"] = consts
    return d


def build(n_heads=NH, do_attn=True, do_ffn=True, taps=(), stage=9, p0_tr=True):
    nc = bass.Bass("TRN2", target_bir_lowering=False, detect_race_conditions=False)

    x_h = nc.dram_tensor("x", [S_LEN, E], F32, kind="ExternalInput").ap()
    w_h = {nm: nc.dram_tensor(nm, [E, E], F32, kind="ExternalInput").ap()
           for nm in ("w1", "w2", "w3", "wo")}
    rw_h = nc.dram_tensor("rw", [E, 16], F32, kind="ExternalInput").ap()
    wg_r_h = nc.dram_tensor("wg_r", [16, E, 256], F32, kind="ExternalInput").ap()
    wu_r_h = nc.dram_tensor("wu_r", [16, E, 256], F32, kind="ExternalInput").ap()
    wd_r_h = nc.dram_tensor("wd_r", [16, 256, E], F32, kind="ExternalInput").ap()
    wg_s_h = nc.dram_tensor("wg_s", [2, E, 512], F32, kind="ExternalInput").ap()
    wu_s_h = nc.dram_tensor("wu_s", [2, E, 512], F32, kind="ExternalInput").ap()
    wd_s_h = nc.dram_tensor("wd_s", [2, 512, E], F32, kind="ExternalInput").ap()
    cost_h = nc.dram_tensor("cost", [128, S_LEN], BF16, kind="ExternalInput").ap()
    sint_h = nc.dram_tensor("sint", [128, S_LEN], BF16, kind="ExternalInput").ap()
    ident_h = nc.dram_tensor("ident", [128, 128], BF16, kind="ExternalInput").ap()
    identf_h = nc.dram_tensor("identf", [128, 128], F32, kind="ExternalInput").ap()
    rmat_h = nc.dram_tensor("rmat", [128, 128], BF16, kind="ExternalInput").ap()
    consts_h = nc.dram_tensor("consts", [128, 8], F32, kind="ExternalInput").ap()
    out_h = nc.dram_tensor("out", [S_LEN, E], F32, kind="ExternalOutput").ap()
    combd_h = nc.dram_tensor("combd", [16, S_LEN], BF16, kind="Internal").ap()

    tap_h = {}

    def mktap(name, shape, dtype):
        if name in taps:
            tap_h[name] = nc.dram_tensor("tap_" + name, shape, dtype,
                                         kind="ExternalOutput").ap()

    mktap("xnt", [128, 8 * S_LEN], BF16)
    mktap("qr0", [128, S_LEN], BF16)
    mktap("kr0", [128, S_LEN], BF16)
    mktap("vt0", [128, S_LEN], BF16)
    mktap("ot0", [128, S_LEN], BF16)
    mktap("acc1", [128, 16 * E], F32)
    mktap("comb", [128, 256], F32)
    mktap("accf", [128, 16 * E], F32)

    top = contextlib.ExitStack()
    with top:
        R = Rec(nc, top)
        sb = lambda name, shape, dtype: top.enter_context(nc.sbuf_tensor(name, shape, dtype))

        ACC = sb("ACC", [128, 16 * E], F32)
        XNT = sb("XNT", [128, 8 * S_LEN], BF16)
        CONSTS = sb("CONSTS", [128, 8], F32)
        IDENT = sb("IDENT", [128, 128], BF16)
        IDENTF = sb("IDENTF", [128, 128], F32)
        RMAT = sb("RMAT", [128, 128], BF16)
        ONES = sb("ONESB", [128, 1], BF16)
        COMBF = sb("COMBF", [16, S_LEN], BF16)
        for i, v in enumerate(CV):
            nc.const_aps.register(F32, v, CONSTS[:, i:i + 1])

        v_c = R.op("sy", lambda h: h.dma_start(CONSTS[:], consts_h[:]), inc="cs", n=16)
        v_c = R.op("sy", lambda h: h.dma_start(IDENT[:], ident_h[:]), inc="cs", n=16)
        v_c = R.op("sy", lambda h: h.dma_start(IDENTF[:], identf_h[:]), inc="cs", n=16)
        v_c = R.op("sy", lambda h: h.dma_start(RMAT[:], rmat_h[:]), inc="cs", n=16)
        v_rw0 = R.op("g", lambda h: h.dma_start(
            RWB[:].rearrange("e (ec x) -> e ec x", ec=8),
            rw_h.rearrange("(ec e) x -> e ec x", e=128)), inc="cg", n=16)
        v_ones = R.op("g", lambda h: h.memset(ONES[:], 1.0), inc="cgp")
        for e in ("t", "sc", "v", "g", "sy"):
            R.wait(e, "cs", v_c)
            R.wait(e, "cgp", v_ones)

        p0 = contextlib.ExitStack()

        def phase_p0_ln():
            psb0 = lambda n_, s_, d_: p0.enter_context(nc.psum_tensor(n_, s_, d_))
            sb0 = lambda n_, s_, d_: p0.enter_context(nc.sbuf_tensor(n_, s_, d_))
            XIN = [sb0(f"XIN{i}", [128, E], F32) for i in range(3)]
            XNM = [sb0(f"XNM{i}", [128, E], BF16) for i in range(2)]
            JUNK = [sb0(f"JUNK{i}", [128, E], BF16) for i in range(2)]
            MB = sb0("MB", [128, 16], F32)
            psT = [psb0(f"psT0{i}", [128, 1024], F32) for i in range(2)]

            v_ln = [0] * 16
            v_ev = [0] * 16
            v_tr = [0] * 16
            if w_units:
                w_alloc()
                w_units[0]()
            for tc in range(16):
                par = tc % 2
                pin = tc % 3
                if tc >= 3:
                    R.wait("sy", "cv", v_ln[tc - 3])
                v_x = R.op("sy", lambda h, tc=tc, pin=pin: h.dma_start(
                    XIN[pin][:], x_h[tc * 128:(tc + 1) * 128, :]), inc="cs", n=16)
                mb = par * 8
                R.wait("v", "cs", v_x)
                vv = R.op("v", lambda h, pin=pin, mb=mb: h.reduce_sum(
                    MB[:, mb + 6:mb + 7], XIN[pin][:], AX.X), inc="cv")
                R.wait("v", "cv", vv)
                vv = R.op("v", lambda h, mb=mb: h.tensor_scalar(
                    MB[:, mb + 0:mb + 1], MB[:, mb + 6:mb + 7], CV[0], None,
                    ALU.mult), inc="cv")
                R.wait("sc", "cs", v_x)
                if tc >= 2:
                    R.wait("sc", "cv", v_ln[tc - 2])
                vs = R.op("sc", lambda h, par=par, pin=pin, mb=mb: h.activation(
                    JUNK[par][:], XIN[pin][:], AF.Square,
                    accum_out=MB[:, mb + 1:mb + 2]), inc="cc")
                R.wait("v", "cv", vv)
                vv = R.op("v", lambda h, mb=mb: h.tensor_tensor(
                    MB[:, mb + 2:mb + 3], MB[:, mb + 0:mb + 1],
                    MB[:, mb + 0:mb + 1], ALU.mult), inc="cv")
                R.wait("v", "cc", vs)
                R.wait("v", "cv", vv)
                vv = R.op("v", lambda h, mb=mb: h.scalar_tensor_tensor(
                    MB[:, mb + 3:mb + 4], MB[:, mb + 1:mb + 2], CV[0],
                    MB[:, mb + 2:mb + 3], ALU.mult, ALU.subtract), inc="cv")
                R.wait("sc", "cv", vv)
                vs = R.op("sc", lambda h, mb=mb: h.activation(
                    MB[:, mb + 5:mb + 6], MB[:, mb + 3:mb + 4], AF.Sqrt,
                    bias=CV[1]), inc="cc")
                R.wait("v", "cc", vs)
                vv = R.op("v", lambda h, mb=mb: h.reciprocal(
                    MB[:, mb + 4:mb + 5], MB[:, mb + 5:mb + 6]), inc="cv")
                R.wait("v", "cv", vv)
                if tc >= 2:
                    R.wait("v", "ct", v_tr[tc - 2])
                v_ln[tc] = R.op("v", lambda h, par=par, pin=pin, mb=mb: h.tensor_scalar(
                    XNM[par][:], XIN[pin][:], MB[:, mb + 0:mb + 1],
                    MB[:, mb + 4:mb + 5], ALU.subtract, ALU.mult), inc="cv")
                R.wait("t", "cv", v_ln[tc])
                if tc >= 2:
                    R.wait("t", "cc", v_ev[tc - 2])
                for ec in range(8):
                    R.op("t", lambda h, ec=ec, tc=tc, par=par: h.matmul(
                        psT[par][:, ec * 128:(ec + 1) * 128],
                        XNM[par][:, ec * 128:(ec + 1) * 128],
                        IDENT[:], start=True, stop=True))
                v_tr[tc] = R.op("t", None, inc="ct")
                R.wait("sc", "ct", v_tr[tc])
                v_ev[tc] = R.op("sc", lambda h, tc=tc, par=par: h.activation(
                    bass.AP(XNT, tc * 128, [[8 * S_LEN, 128], [S_LEN, 8], [1, 128]]),
                    bass.AP(psT[par], 0, [[1024, 128], [128, 8], [1, 128]]),
                    AF.Copy), inc="cc")
                for wu in w_units_for_chunk(tc):
                    wu()
            if "xnt" in taps:
                R.wait("sy", "cc", v_ev[15])
                R.op("sy", lambda h: h.dma_start(tap_h["xnt"], XNT[:]), inc="cs", n=16)
            return v_ev

        pA = contextlib.ExitStack()
        sbA = lambda name, shape, dtype: pA.enter_context(nc.sbuf_tensor(name, shape, dtype))
        WOTP = sbA("WOTP", [128, 8 * E], BF16)
        W123T = {nm: sbA(nm + "T", [128, 8 * E], BF16) for nm in ("w1", "w2", "w3")}
        RWB = sbA("RWB", [128, 8 * 16], BF16)
        pw = contextlib.ExitStack()
        pw2 = contextlib.ExitStack()
        wst = {"cc_ev": 0, "v_tdone": [0] * 4, "v_wl": 0}
        wbuf = {}

        def w_alloc():
            wbuf["WNAT"] = pw.enter_context(nc.sbuf_tensor("WNAT0", [128, 8 * E], BF16))
            wbuf["psTW"] = pw2.enter_context(nc.psum_tensor("psTW", [128, 1024], F32))

        def w_dma_unit(i, nm):
            def go(i=i, nm=nm):
                if i >= 1:
                    R.wait("g", "ct", wst["v_tdone"][i - 1])
                wst["v_wl"] = R.op("g", lambda h, nm=nm: h.dma_start(
                    wbuf["WNAT"][:].rearrange("p (pc e) -> p pc e", pc=8),
                    w_h[nm].rearrange("(pc p) e -> p pc e", p=128)),
                    inc="cg", n=16)
            return go

        def w_blk_unit(i, nm, blk):
            def go(i=i, nm=nm, blk=blk):
                R.wait("t", "cg", wst["v_wl"])
                R.wait("t", "cc", wst["cc_ev"])
                for j in range(8):
                    if nm != "wo":
                        R.op("t", lambda h, blk=blk, j=j: h.matmul(
                            wbuf["psTW"][:, j * 128:(j + 1) * 128],
                            wbuf["WNAT"][:, j * E + blk * 128: j * E + blk * 128 + 128],
                            IDENT[:], start=True, stop=True))
                    else:
                        R.op("t", lambda h, blk=blk, j=j: h.matmul(
                            wbuf["psTW"][:, j * 128:(j + 1) * 128],
                            wbuf["WNAT"][:, j * E + blk: j * E + blk + 1017:8],
                            IDENT[:], start=True, stop=True))
                vt = R.op("t", None, inc="ct")
                R.wait("sc", "ct", vt)
                if nm != "wo":
                    wst["cc_ev"] = R.op("sc", lambda h, nm=nm, blk=blk: h.activation(
                        W123T[nm][:, blk * E:(blk + 1) * E], wbuf["psTW"][:], AF.Copy),
                        inc="cc")
                else:
                    wst["cc_ev"] = R.op("sc", lambda h, blk=blk: h.activation(
                        WOTP[:, blk * E:(blk + 1) * E], wbuf["psTW"][:], AF.Copy),
                        inc="cc")
                if blk == 7:
                    wst["v_tdone"][i] = R.vals["ct"]
            return go

        w_units = []
        if stage >= 2:
            for i, nm in enumerate(["w1", "w2", "w3", "wo"]):
                w_units.append(w_dma_unit(i, nm))
                for blk in range(8):
                    w_units.append(w_blk_unit(i, nm, blk))
        w_sched = {}
        for ui in range(1, len(w_units)):
            chunk = 2 + ((ui - 1) * 14) // max(len(w_units) - 1, 1)
            w_sched.setdefault(min(chunk, 15), []).append(w_units[ui])

        def w_units_for_chunk(tc):
            return w_sched.get(tc, [])

        def phase_w():
            R.wait("t", "cc", wst["cc_ev"])
        def phase_router():
            with contextlib.ExitStack() as p2r:
                psbr = lambda n_, s_, d_: p2r.enter_context(nc.psum_tensor(n_, s_, d_))
                sbr = lambda n_, s_, d_: p2r.enter_context(nc.sbuf_tensor(n_, s_, d_))
                LGF = sbr("LGF", [16, S_LEN], F32)
                LGT = sbr("LGT", [128, 256], F32)
                COMBT = sbr("COMBT", [128, 256], F32)
                RS = sbr("RS", [128, 16], F32)
                L2 = sbr("L2", [128, 32], F32)
                MS1 = sbr("MS1", [128, 32], F32)
                MS2 = sbr("MS2", [128, 32], F32)
                psL = psbr("psL", [16, 1024], F32)
                psLT = psbr("psLT", [128, 256], F32)
                R.wait("t", "cg", v_rw0)
                v_lg = 0
                for hf2 in range(2):
                    R.wait("t", "cc", v_lg)
                    for ec in range(8):
                        for hf in range(2):
                            R.op("t", lambda h, ec=ec, hf=hf, hf2=hf2: h.matmul(
                                psL[:, hf * 512:(hf + 1) * 512],
                                RWB[:, ec * 16:(ec + 1) * 16],
                                XNT[:, ec * S_LEN + hf2 * 1024 + hf * 512:
                                    ec * S_LEN + hf2 * 1024 + hf * 512 + 512],
                                start=(ec == 0), stop=(ec == 7)))
                    vt = R.op("t", None, inc="ct")
                    R.wait("sc", "ct", vt)
                    v_lg = R.op("sc", lambda h, hf2=hf2: h.activation(
                        LGF[:, hf2 * 1024:(hf2 + 1) * 1024], psL[:], AF.Copy),
                        inc="cc")
                R.wait("t", "cc", v_lg)
                for tc in range(16):
                    R.op("t", lambda h, tc=tc: h.matmul(
                        psLT[:, tc * 16:(tc + 1) * 16],
                        LGF[:, tc * 128:(tc + 1) * 128], IDENTF[0:16, 0:16], start=True, stop=True))
                vt = R.op("t", None, inc="ct")
                R.wait("sc", "ct", vt)
                v_lgt = R.op("sc", lambda h: h.activation(LGT[:], psLT[:], AF.Copy),
                             inc="cc")
                R.wait("v", "cc", v_lgt)
                for tc in range(16):
                    rs = (tc % 2) * 8
                    mo = (tc % 2) * 16
                    vv = R.op("v", lambda h, tc=tc, rs=rs: h.reduce_max(
                        RS[:, rs + 0:rs + 1], LGT[:, tc * 16:(tc + 1) * 16], AX.X),
                        inc="cv")
                    R.wait("v", "cv", vv)
                    vv = R.op("v", lambda h, tc=tc, rs=rs, mo=mo: h.tensor_scalar(
                        MS1[:, mo:mo + 16], LGT[:, tc * 16:(tc + 1) * 16],
                        RS[:, rs + 0:rs + 1], None, ALU.is_equal), inc="cv")
                    R.wait("v", "cv", vv)
                    vv = R.op("v", lambda h, tc=tc, mo=mo: h.scalar_tensor_tensor(
                        L2[:, mo:mo + 16], MS1[:, mo:mo + 16], CV[2],
                        LGT[:, tc * 16:(tc + 1) * 16], ALU.mult, ALU.add), inc="cv")
                    R.wait("v", "cv", vv)
                    vv = R.op("v", lambda h, rs=rs, mo=mo: h.reduce_max(
                        RS[:, rs + 1:rs + 2], L2[:, mo:mo + 16], AX.X), inc="cv")
                    R.wait("v", "cv", vv)
                    vv = R.op("v", lambda h, rs=rs, mo=mo: h.tensor_scalar(
                        MS2[:, mo:mo + 16], L2[:, mo:mo + 16], RS[:, rs + 1:rs + 2],
                        None, ALU.is_equal), inc="cv")
                    R.wait("v", "cv", vv)
                    vv = R.op("v", lambda h, rs=rs: h.tensor_tensor(
                        RS[:, rs + 2:rs + 3], RS[:, rs + 1:rs + 2],
                        RS[:, rs + 0:rs + 1], ALU.subtract), inc="cv")
                    R.wait("sc", "cv", vv)
                    vs = R.op("sc", lambda h, rs=rs: h.activation(
                        RS[:, rs + 4:rs + 5], RS[:, rs + 2:rs + 3], AF.Sigmoid),
                        inc="cc")
                    R.wait("sc", "cc", vs)
                    vs = R.op("sc", lambda h, rs=rs: h.activation(
                        RS[:, rs + 3:rs + 4], RS[:, rs + 2:rs + 3], AF.Sigmoid,
                        scale=CV[3]), inc="cc")
                    R.wait("v", "cc", vs)
                    vv = R.op("v", lambda h, tc=tc, rs=rs, mo=mo: h.tensor_scalar(
                        COMBT[:, tc * 16:(tc + 1) * 16], MS1[:, mo:mo + 16],
                        RS[:, rs + 3:rs + 4], None, ALU.mult), inc="cv")
                    R.wait("v", "cv", vv)
                    R.op("v", lambda h, tc=tc, rs=rs, mo=mo: h.scalar_tensor_tensor(
                        COMBT[:, tc * 16:(tc + 1) * 16], MS2[:, mo:mo + 16],
                        RS[:, rs + 4:rs + 5], COMBT[:, tc * 16:(tc + 1) * 16],
                        ALU.mult, ALU.add), inc="cv")
                v_combt = R.vals["cv"]
                if "comb" in taps:
                    R.wait("sy", "cv", v_combt)
                    R.op("sy", lambda h: h.dma_start(tap_h["comb"], COMBT[:]),
                         inc="cs", n=16)
                R.wait("t", "cv", v_combt)
                v_combf = 0
                for r2 in range(2):
                    R.wait("t", "cc", v_combf)
                    for j in range(8):
                        tc = r2 * 8 + j
                        R.op("t", lambda h, tc=tc, j=j: h.matmul(
                            psL[:, j * 128:(j + 1) * 128],
                            COMBT[:, tc * 16:(tc + 1) * 16], IDENTF[:],
                            start=True, stop=True))
                    vt = R.op("t", None, inc="ct")
                    R.wait("sc", "ct", vt)
                    v_combf = R.op("sc", lambda h, r2=r2: h.activation(
                        COMBF[:, r2 * 1024:(r2 + 1) * 1024], psL[:], AF.Copy),
                        inc="cc")
                R.wait("g", "cc", v_combf)
                v_combd = R.op("g", lambda h: h.dma_start(combd_h, COMBF[:]),
                               inc="cg", n=16)
            return v_combd
        if stage >= 1:
            _v_ev_saved = phase_p0_ln()
        if stage >= 2:
            phase_w()
        pw2.close()
        pw.close()
        _v_combd = phase_router() if (do_ffn and stage >= 4) else 0
        if stage < 1:
            vz = R.op("v", lambda h: h.memset(XNT[:], 0.0), inc="cv")
            R.wait("t", "cv", vz)
        p0.close()
        R.barrier()

        if do_attn and stage >= 3:
            COST = sbA("COST", [128, S_LEN], BF16)
            SINT = sbA("SINT", [128, S_LEN], BF16)
            R.op("sy", lambda h: h.dma_start(COST[:], cost_h[:]), inc="cs", n=16)
            v_tab = R.op("sy", lambda h: h.dma_start(SINT[:], sint_h[:]), inc="cs", n=16)

            QHB = sbA("QHB", [128, S_LEN], BF16)
            KHB = sbA("KHB", [128, S_LEN], BF16)
            QR = sbA("QR", [128, S_LEN], BF16)
            KR = sbA("KR", [128, S_LEN], BF16)
            VT = sbA("VT", [128, S_LEN], BF16)
            T1 = sbA("T1", [128, 512], BF16)
            T2 = sbA("T2", [128, 512], BF16)
            PT = [sbA(f"PT{i}", [128, 1024], BF16) for i in range(3)]
            OSB = KHB
            OT = QHB
            ASB = OSB
            RSROW = sbA("RSROW", [1, S_LEN], BF16)
            RECT = sbA("RECT", [128, 16], F32)

            v_rope_prev = 0
            v_rope_hist = [0, 0]
            v_accev_prev = 0
            v_exp_carry = [0, 0]
            v_pv_carry3 = [0, 0, 0]
            v_oe = 0
            for head in range(n_heads):
                with contextlib.ExitStack() as p1a:
                    psba = lambda n_, s_, d_: p1a.enter_context(nc.psum_tensor(n_ + f"_{head}", s_, d_))
                    psQK = psba("psQK", [128, S_LEN], F32)
                    psV = psba("psV", [128, 1024], F32)
                    psSH = psba("psSH", [128, 1024], F32)

                    def proj_qk(wt, dst, head=head):
                        for pc in range(8):
                            for ec in range(8):
                                R.op("t", lambda h, pc=pc, ec=ec, wt=wt, head=head: h.matmul(
                                    psQK[:, pc * TH:(pc + 1) * TH],
                                    W123T[wt][:, ec * E + pc * 128: ec * E + pc * 128 + 128],
                                    XNT[:, ec * S_LEN + head * TH: ec * S_LEN + head * TH + TH],
                                    start=(ec == 0), stop=(ec == 7)))
                        vt = R.op("t", None, inc="ct")
                        R.wait("sc", "ct", vt)
                        return R.op("sc", lambda h, dst=dst: h.activation(
                            dst[:], psQK[:], AF.Copy), inc="cc")

                    R.wait("sc", "cv", v_rope_prev)
                    v_q = proj_qk("w1", QHB)
                    R.wait("t", "cc", v_q)
                    v_k = proj_qk("w2", KHB)

                    v_vev = 0
                    for blk in range(2):
                        R.wait("t", "cc", v_vev)
                        for ec in range(8):
                            for hf in range(2):
                                R.op("t", lambda h, ec=ec, hf=hf, blk=blk, head=head: h.matmul(
                                    psV[:, hf * 512:(hf + 1) * 512],
                                    XNT[:, ec * S_LEN + head * TH + blk * 128:
                                        ec * S_LEN + head * TH + blk * 128 + 128],
                                    W123T["w3"][:, ec * E + hf * 512: ec * E + hf * 512 + 512],
                                    start=(ec == 0), stop=(ec == 7)))
                        vt = R.op("t", None, inc="ct")
                        R.wait("sc", "ct", vt)
                        v_vev = R.op("sc", lambda h, blk=blk: h.activation(
                            bass.AP(VT, blk * 128, [[S_LEN, 128], [TH, 8], [1, 128]]),
                            bass.AP(psV, 0, [[1024, 128], [128, 8], [1, 128]]),
                            AF.Copy), inc="cc")

                    def rope(src, dst, v_src):
                        vq = 0
                        for j in range(4):
                            par = j % 2
                            R.wait("t", "cc", v_src)
                            R.wait("t", "cv", v_rope_hist[par])
                            vt = R.op("t", lambda h, j=j, par=par, src=src: h.matmul(
                                psSH[:, par * 512:(par + 1) * 512], RMAT[:],
                                src[:, j * 512:(j + 1) * 512], start=True, stop=True),
                                inc="ct")
                            R.wait("v", "cs", v_tab)
                            R.wait("v", "cc", v_src)
                            R.op("v", lambda h, j=j, src=src: h.tensor_tensor(
                                T1[:], src[:, j * 512:(j + 1) * 512],
                                COST[:, j * 512:(j + 1) * 512], ALU.mult), inc="cv")
                            R.wait("v", "ct", vt)
                            vv = R.op("v", lambda h, j=j, par=par: h.tensor_tensor(
                                T2[:], psSH[:, par * 512:(par + 1) * 512],
                                SINT[:, j * 512:(j + 1) * 512], ALU.mult), inc="cv")
                            R.wait("v", "cv", vv)
                            vq = R.op("v", lambda h, j=j, dst=dst: h.tensor_tensor(
                                dst[:, j * 512:(j + 1) * 512], T1[:], T2[:], ALU.add),
                                inc="cv")
                            v_rope_hist[par] = vq
                        return vq

                    v_qr = rope(QHB, QR, v_q)
                    v_kr = rope(KHB, KR, v_k)
                    v_rope_prev = v_kr

                if head == 0 and "qr0" in taps:
                    R.wait("sy", "cv", v_qr)
                    R.op("sy", lambda h: h.dma_start(tap_h["qr0"], QR[:]), inc="cs", n=16)
                    R.wait("sy", "cv", v_kr)
                    R.op("sy", lambda h: h.dma_start(tap_h["kr0"], KR[:]), inc="cs", n=16)
                if head == 0 and "vt0" in taps:
                    R.wait("sy", "cc", R.vals["cc"])
                    R.op("sy", lambda h: h.dma_start(tap_h["vt0"], VT[:]), inc="cs", n=16)

                with contextlib.ExitStack() as p1b:
                    psbb = lambda n_, s_, d_: p1b.enter_context(nc.psum_tensor(n_ + f"_{head}", s_, d_))
                    psS = [psbb(f"psS{i}", [128, 1024], F32) for i in range(2)]
                    psO = psbb("psO", [128, 1024], F32)
                    psR = psbb("psR", [1, 1024], F32)
                    R.wait("t", "cv", v_qr)
                    R.wait("t", "cv", v_kr)
                    v_exp = [0] * 32
                    v_pv = [0] * 32
                    v_s = [0] * 32
                    for kk in range(34):
                        if kk < 32:
                            qh, kt = kk // 16, kk % 16
                            par = kk % 2
                            R.wait("t", "cc",
                                   v_exp[kk - 2] if kk >= 2 else v_exp_carry[kk])
                            for hf in range(2):
                                R.op("t", lambda h, kt=kt, par=par, qh=qh, hf=hf: h.matmul(
                                    psS[par][:, hf * 512:(hf + 1) * 512],
                                    KR[:, kt * 128:(kt + 1) * 128],
                                    QR[:, qh * 1024 + hf * 512: qh * 1024 + hf * 512 + 512],
                                    start=True, stop=True))
                            v_s[kk] = R.op("t", None, inc="ct")
                            R.wait("sc", "ct", v_s[kk])
                            R.wait("sc", "ct",
                                   v_pv[kk - 3] if kk >= 3 else v_pv_carry3[kk])
                            v_exp[kk] = R.op("sc", lambda h, par3=kk % 3, par=par: h.activation(
                                PT[par3][:], psS[par][:], AF.Exp,
                                bias=SOFTMAX_SHIFT, scale=INV_SQRT8), inc="cc")
                        if kk >= 2:
                            pk = kk - 2
                            qh, kt = pk // 16, pk % 16
                            par = pk % 2
                            R.wait("t", "cc", v_exp[pk])
                            if kt == 0:
                                R.wait("t", "cc", v_oe)
                            par3 = pk % 3
                            for hf in range(2):
                                R.op("t", lambda h, kt=kt, par3=par3, hf=hf: h.matmul(
                                    psO[:, hf * 512:(hf + 1) * 512],
                                    VT[:, kt * 128:(kt + 1) * 128],
                                    PT[par3][:, hf * 512:(hf + 1) * 512],
                                    start=(kt == 0), stop=(kt == 15),
                                    skip_group_check=True))
                            for hf in range(2):
                                R.op("t", lambda h, kt=kt, par3=par3, hf=hf: h.matmul(
                                    psR[:, hf * 512:(hf + 1) * 512],
                                    ONES[:],
                                    PT[par3][:, hf * 512:(hf + 1) * 512],
                                    start=(kt == 0), stop=(kt == 15),
                                    skip_group_check=True))
                            v_pv[pk] = R.op("t", None, inc="ct")
                            if kt == 15:
                                R.wait("sc", "ct", v_pv[pk])
                                R.op("sc", lambda h, qh=qh: h.activation(
                                    RSROW[0:1, qh * 1024:(qh + 1) * 1024], psR[0:1, :],
                                    AF.Copy), inc="cc")
                                v_oe = R.op("sc", lambda h, qh=qh: h.activation(
                                    OSB[:, qh * 1024:(qh + 1) * 1024], psO[:],
                                    AF.Copy), inc="cc")
                    v_exp_carry = [v_exp[30], v_exp[31]]
                    v_pv_carry3 = [v_pv[31]] * 3

                with contextlib.ExitStack() as p1c:
                    psbc = lambda n_, s_, d_: p1c.enter_context(nc.psum_tensor(n_ + f"_{head}", s_, d_))
                    psAm = psbc("psAm", [128, 1024], F32)
                    psTT = psbc("psTT", [128, 1024], F32)
                    R.wait("t", "cc", v_oe)
                    cc_ev = v_accev_prev
                    for j in range(16):
                        R.op("t", lambda h, j=j: h.matmul(
                            psAm[:, j:j + 1],
                            RSROW[0:1, j * 128:(j + 1) * 128], IDENT[0:1, 0:1], start=True, stop=True))
                    vt_rs = R.op("t", None, inc="ct")
                    R.wait("v", "ct", vt_rs)
                    v_rec = R.op("v", lambda h: h.reciprocal(
                        RECT[:], psAm[:, 0:16]), inc="cv")
                    for r in range(2):
                        R.wait("t", "cc", cc_ev)
                        for j in range(8):
                            R.op("t", lambda h, r=r, j=j: h.matmul(
                                psTT[:, j * 128:(j + 1) * 128],
                                OSB[:, (r * 8 + j) * 128:(r * 8 + j) * 128 + 128],
                                IDENT[:], start=True, stop=True))
                        vt = R.op("t", None, inc="ct")
                        R.wait("sc", "ct", vt)
                        R.wait("sc", "cv", v_rec)
                        for j in range(8):
                            kt2 = r * 8 + j
                            cc_ev = R.op("sc", lambda h, kt2=kt2, j=j: h.activation(
                                OT[:, kt2 * 128:(kt2 + 1) * 128],
                                psTT[:, j * 128:(j + 1) * 128], AF.Copy,
                                scale=RECT[:, kt2:kt2 + 1]), inc="cc")
                    if head == 0 and "ot0" in taps:
                        R.wait("sy", "cc", cc_ev)
                        R.op("sy", lambda h: h.dma_start(tap_h["ot0"], OT[:]), inc="cs",
                             n=16)
                    for u in range(2):
                        R.wait("t", "cc", cc_ev)
                        R.wait("t", "cv", v_rec)
                        for i, kt in enumerate(range(u, 16, 2)):
                            for hf in range(2):
                                R.op("t", lambda h, kt=kt, i=i, hf=hf: h.matmul(
                                    psAm[:, hf * 512:(hf + 1) * 512],
                                    OT[:, kt * 128:(kt + 1) * 128],
                                    WOTP[:, (kt // 2) * E + hf * 512:
                                         (kt // 2) * E + hf * 512 + 512],
                                    start=(i == 0), stop=(i == 7)))
                        vt = R.op("t", None, inc="ct")
                        R.wait("sc", "ct", vt)
                        v_asb = R.op("sc", lambda h: h.activation(
                            ASB[:, 0:1024], psAm[:], AF.Copy), inc="cc")
                        R.wait("t", "cc", v_asb)
                        for cc_i in range(8):
                            R.op("t", lambda h, cc_i=cc_i: h.matmul(
                                psTT[:, cc_i * 128:(cc_i + 1) * 128],
                                ASB[:, cc_i * 128:(cc_i + 1) * 128], IDENT[:], start=True, stop=True))
                        vt = R.op("t", None, inc="ct")
                        R.wait("sc", "ct", vt)
                        cc_ev = R.op("sc", lambda h, u=u, head=head: h.activation(
                            bass.AP(ACC, (u * 8) * E + head * 128,
                                    [[16 * E, 128], [E, 8], [1, 128]]),
                            bass.AP(psTT, 0, [[1024, 128], [128, 8], [1, 128]]),
                            AF.Copy), inc="cc")
                    v_accev_prev = cc_ev
            if "acc1" in taps:
                R.wait("sy", "cc", R.vals["cc"])
                R.op("sy", lambda h: h.dma_start(tap_h["acc1"], ACC[:]), inc="cs", n=16)
        if not (do_attn and stage >= 3):
            vz = R.op("v", lambda h: h.memset(ACC[:], 0.0), inc="cv")
            R.wait("t", "cv", vz)
        R.barrier()

        pA.close()

        if do_ffn and stage >= 4:
            pass
            R.barrier()

            with contextlib.ExitStack() as p2e:
                psbe = lambda n_, s_, d_: p2e.enter_context(nc.psum_tensor(n_, s_, d_))
                sbe = lambda n_, s_, d_: p2e.enter_context(nc.sbuf_tensor(n_, s_, d_))
                psG1 = psbe("psG1", [128, 1024], F32)
                psG2 = psbe("psG2", [128, 1024], F32)
                psD = [psbe(f"psD{i}", [128, 1024], F32) for i in range(2)]
                HARE = sbe("HARE", [128, 8 * S_LEN], BF16)
                SG = sbe("SG", [128, S_LEN], BF16)
                TFFN = sbe("TFFN", [128, S_LEN], BF16)
                WGA = sbe("WGA", [128, 8 * 512], BF16)
                WUA = sbe("WUA", [128, 8 * 512], BF16)
                WDA = sbe("WDA", [128, 8 * 1024], BF16)
                CB = [sbe(f"CB{i}", [128, S_LEN], BF16) for i in range(2)]

                groups = [("r", [4 * g + i for i in range(4)]) for g in range(4)]
                groups.append(("s", [0, 1]))
                XI = [XNT[:, 0:2048].bitcast(F32), XNT[:, 2048:4096].bitcast(F32),
                      XNT[:, 8192:10240].bitcast(F32)]
                OU = [XNT[:, 4096:6144].bitcast(F32), XNT[:, 6144:8192].bitcast(F32)]
                v_ad3 = [0] * 16
                v_st3 = [0] * 16
                v_wg_used = {}
                v_wd_used = {}
                v_cb_used = {}
                v_prev_add = [0, 0]
                v_silu_prev = 0
                v_h_prev = 0
                for gi, (gkind, gl) in enumerate(groups):
                    shared = gkind == "s"
                    nmc = 4 if shared else 2
                    mid = 512 if shared else 256
                    ndslots = 2 if shared else 4
                    v_h = []
                    v_ld_all = 0
                    for li, e in enumerate(gl):
                        sl_g = 0 if shared else li % 2
                        sl_d = li % ndslots
                        wg_off = sl_g * 8 * 256 if not shared else 0
                        wd_off = sl_d * (4 * E if shared else 2 * E)
                        R.wait("g", "cv", v_wg_used.get(sl_g, 0))
                        src_g = (wg_s_h if shared else wg_r_h)[e].rearrange(
                            "(ec e2) m -> e2 ec m", e2=128)
                        v_lg_ = R.op("g", lambda h, src=src_g, off=wg_off, mid=mid:
                                     h.dma_start(WGA[:, off:off + 8 * mid].rearrange(
                                         "p (ec m) -> p ec m", ec=8), src),
                                     inc="cg", n=16)
                        src_u = (wu_s_h if shared else wu_r_h)[e].rearrange(
                            "(ec e2) m -> e2 ec m", e2=128)
                        v_lu_ = R.op("g", lambda h, src=src_u, off=wg_off, mid=mid:
                                     h.dma_start(WUA[:, off:off + 8 * mid].rearrange(
                                         "p (ec m) -> p ec m", ec=8), src),
                                     inc="cg", n=16)
                        R.wait("g", "cv", v_wd_used.get(sl_d, 0))
                        src_d = (wd_s_h if shared else wd_r_h)[e].rearrange(
                            "(mc m) x -> m mc x", m=128)
                        v_ld_all = R.op("g", lambda h, src=src_d, off=wd_off, nmc=nmc:
                                        h.dma_start(WDA[:, off:off + nmc * E].rearrange(
                                            "p (mc x) -> p mc x", mc=nmc), src),
                                        inc="cg", n=16)
                        v_cb_ = 0
                        if not shared:
                            R.wait("g", "cv", v_cb_used.get(sl_g, 0))
                            v_cb_ = R.op("g", lambda h, e=e, sl=sl_g: h.dma_start(
                                CB[sl][:], combd_h[e:e + 1, :].partition_broadcast(128)),
                                inc="cg", n=16)
                        for mc in range(nmc):
                            for th in range(2):
                                R.wait("t", "cg", v_lg_)
                                R.wait("t", "cc", v_silu_prev)
                                for ec in range(8):
                                    for hf in range(2):
                                        R.op("t", lambda h, ec=ec, hf=hf, mc=mc, th=th,
                                             off=wg_off, mid=mid: h.matmul(
                                            psG1[:, hf * 512:(hf + 1) * 512],
                                            WGA[:, off + ec * mid + mc * 128:
                                                off + ec * mid + mc * 128 + 128],
                                            XNT[:, ec * S_LEN + th * 1024 + hf * 512:
                                                ec * S_LEN + th * 1024 + hf * 512 + 512],
                                            start=(ec == 0), stop=(ec == 7)))
                                vt_g = R.op("t", None, inc="ct")
                                R.wait("sc", "ct", vt_g)
                                R.wait("sc", "cv", v_h_prev)
                                v_silu_prev = R.op("sc", lambda h, th=th: h.activation(
                                    SG[:, th * 1024:(th + 1) * 1024], psG1[:],
                                    AF.Silu), inc="cc")
                                R.wait("t", "cg", v_lu_)
                                R.wait("t", "cv", v_h_prev)
                                for ec in range(8):
                                    for hf in range(2):
                                        R.op("t", lambda h, ec=ec, hf=hf, mc=mc, th=th,
                                             off=wg_off, mid=mid: h.matmul(
                                            psG2[:, hf * 512:(hf + 1) * 512],
                                            WUA[:, off + ec * mid + mc * 128:
                                                off + ec * mid + mc * 128 + 128],
                                            XNT[:, ec * S_LEN + th * 1024 + hf * 512:
                                                ec * S_LEN + th * 1024 + hf * 512 + 512],
                                            start=(ec == 0), stop=(ec == 7)))
                                vt_u = R.op("t", None, inc="ct")
                                R.wait("v", "ct", vt_u)
                                R.wait("v", "cc", v_silu_prev)
                                hslot = li * nmc + mc
                                base = hslot * S_LEN + th * 1024
                                if not shared:
                                    R.wait("v", "cg", v_cb_)
                                    vv = R.op("v", lambda h, th=th: h.tensor_tensor(
                                        TFFN[:, 0:1024], SG[:, th * 1024:(th + 1) * 1024],
                                        psG2[:], ALU.mult), inc="cv")
                                    R.wait("v", "cv", vv)
                                    vv = R.op("v", lambda h, base=base, th=th, sl=sl_g:
                                              h.tensor_tensor(
                                        HARE[:, base:base + 1024], TFFN[:, 0:1024],
                                        CB[sl][:, th * 1024:(th + 1) * 1024],
                                        ALU.mult), inc="cv")
                                else:
                                    vv = R.op("v", lambda h, base=base, th=th:
                                              h.tensor_tensor(
                                        HARE[:, base:base + 1024],
                                        SG[:, th * 1024:(th + 1) * 1024],
                                        psG2[:], ALU.mult), inc="cv")
                                v_h_prev = vv
                                v_h.append(vv)
                        v_wg_used[sl_g] = v_h_prev
                        if not shared:
                            v_cb_used[sl_g] = v_h_prev
                    v_hall = max(v_h)
                    R.wait("t", "cv", v_hall)
                    R.wait("t", "cg", v_ld_all)
                    v_add = [0] * 16
                    for tc in range(16):
                        par = tc % 2
                        prev = v_add[tc - 2] if tc >= 2 else v_prev_add[par]
                        R.wait("t", "cv", prev)
                        first = True
                        nexp = len(gl)
                        for li in range(nexp):
                            sl_d = li % ndslots
                            wd_off2 = sl_d * (4 * E if shared else 2 * E)
                            for mc in range(nmc):
                                hslot = li * nmc + mc
                                last = (li == nexp - 1 and mc == nmc - 1)
                                for hf in range(2):
                                    R.op("t", lambda h, tc=tc, par=par, hslot=hslot,
                                         off=wd_off2, mc=mc, first=first, hf=hf, last=last:
                                         h.matmul(
                                        psD[par][:, hf * 512:(hf + 1) * 512],
                                        HARE[:, hslot * S_LEN + tc * 128:
                                             hslot * S_LEN + tc * 128 + 128],
                                        WDA[:, off + mc * E + hf * 512:
                                            off + mc * E + hf * 512 + 512],
                                        start=first, stop=last,
                                        skip_group_check=True))
                                first = False
                        vt = R.op("t", None, inc="ct")
                        R.wait("v", "ct", vt)
                        v_add[tc] = R.op("v", lambda h, tc=tc, par=par: h.tensor_tensor(
                            ACC[:, tc * E:(tc + 1) * E], psD[par][:],
                            ACC[:, tc * E:(tc + 1) * E], ALU.add), inc="cv")
                        if shared:
                            pe = tc % 3
                            if tc < 3:
                                R.wait("sy", "cv", v_hall)
                            else:
                                R.wait("sy", "cv", v_ad3[tc - 3])
                            v_xl = R.op("sy", lambda h, tc=tc, pe=pe: h.dma_start(
                                XI[pe], x_h[tc * 128:(tc + 1) * 128, :]),
                                inc="cs", n=16)
                            R.wait("v", "cs", v_xl)
                            if tc >= 2:
                                R.wait("v", "cst", v_st3[tc - 2])
                            v_ad3[tc] = R.op("v", lambda h, tc=tc, par=par, pe=pe:
                                             h.tensor_tensor(
                                OU[par], ACC[:, tc * E:(tc + 1) * E], XI[pe],
                                ALU.add), inc="cv")
                            R.wait("sc", "cv", v_ad3[tc])
                            v_st3[tc] = R.op("sc", lambda h, tc=tc, par=par:
                                             h.dma_start(
                                out_h[tc * 128:(tc + 1) * 128, :], OU[par]),
                                inc="cst", n=16)
                    v_prev_add = [v_add[14], v_add[15]]
                    for sl in range(ndslots):
                        v_wd_used[sl] = v_add[15]
                    v_add_final = list(v_add)
                    v_hall_last = v_hall
        if (not do_ffn or stage < 4) or "accf" in taps:
            R.barrier()
            v_add_final = [R.vals["cv"]] * 16
            v_hall_last = R.vals["cv"]
        if "accf" in taps:
            R.op("sy", lambda h: h.dma_start(tap_h["accf"], ACC[:]), inc="cs", n=16)

        if not (do_ffn and stage >= 4):
            XI = [XNT[:, 0:2048].bitcast(F32), XNT[:, 2048:4096].bitcast(F32)]
            OU = [XNT[:, 4096:6144].bitcast(F32), XNT[:, 6144:8192].bitcast(F32)]
            v_st = [0] * 16
            v_ad = [0] * 16
            for tc in range(16):
                par = tc % 2
                if tc < 2:
                    R.wait("sy", "cv", v_hall_last)
                if tc >= 2:
                    R.wait("sy", "cv", v_ad[tc - 2])
                    R.wait("sy", "cs", v_st[tc - 2])
                v_xl = R.op("sy", lambda h, tc=tc, par=par: h.dma_start(
                    XI[par], x_h[tc * 128:(tc + 1) * 128, :]), inc="cs", n=16)
                R.wait("v", "cs", v_xl)
                R.wait("v", "cv", v_add_final[tc])
                v_ad[tc] = R.op("v", lambda h, tc=tc, par=par: h.tensor_tensor(
                    OU[par], ACC[:, tc * E:(tc + 1) * E], XI[par], ALU.add),
                    inc="cv")
                R.wait("sc", "cv", v_ad[tc])
                v_st[tc] = R.op("sc", lambda h, tc=tc, par=par: h.dma_start(
                    out_h[tc * 128:(tc + 1) * 128, :], OU[par]), inc="cs", n=16)
            R.wait("sy", "cs", R.vals["cs"])
            R.wait("g", "cs", R.vals["cs"])
        R.wait("sy", "cs", R.vals["cs"])
        R.wait("g", "cs", R.vals["cs"])
        R.wait("sy", "cst", R.vals["cst"])
        R.wait("g", "cst", R.vals["cst"])

        with nc.Block() as block:
            R.replay(block)
    return nc


def make_in_map(inputs, core_id):
    hc = host_constants()
    m = {
        "x": np.ascontiguousarray(np.asarray(inputs["L0_attn_emb"])[core_id]),
        "w1": np.asarray(inputs["L0_attn_W1"]),
        "w2": np.asarray(inputs["L0_attn_W2"]),
        "w3": np.asarray(inputs["L0_attn_W3"]),
        "wo": np.asarray(inputs["L0_attn_Wo"]),
        "rw": np.asarray(inputs["L0_ffn_router_w"]),
        "wg_r": np.asarray(inputs["L0_ffn_wg_r"]),
        "wu_r": np.asarray(inputs["L0_ffn_wu_r"]),
        "wd_r": np.asarray(inputs["L0_ffn_wd_r"]),
        "wg_s": np.asarray(inputs["L0_ffn_wg_s"]),
        "wu_s": np.asarray(inputs["L0_ffn_wu_s"]),
        "wd_s": np.asarray(inputs["L0_ffn_wd_s"]),
        "cost": hc["cost"], "sint": hc["sint"], "ident": hc["ident"],
        "identf": hc["identf"], "rmat": hc["rmat"], "consts": hc["consts"],
    }
    return m


def kernel(**inputs):
    nc = build()
    in_maps = [make_in_map(inputs, c) for c in range(8)]
    res = run_bass_kernel_spmd(nc, in_maps, core_ids=list(range(8)))
    import numpy as _np
    return _np.stack([_np.asarray(r["out"]) for r in res.results], axis=0)
